# version 06
# baseline (speedup 1.0000x reference)
import numpy as np
import ml_dtypes
import concourse.bass as bass
import concourse.mybir as mybir
from concourse.bass_utils import run_bass_kernel_spmd
from contextlib import ExitStack

F32 = mybir.dt.float32
BF16 = mybir.dt.bfloat16
I8 = mybir.dt.int8
ALU = mybir.AluOpType
ACT = mybir.ActivationFunctionType
AX = mybir.AxisListType.X

NCORES = 8
B_, T_, D = 4, 8192, 1024
CH = 512
NSLOT = 8
DCH = D // 128
RING = 9
C2 = 1.5 * 2.0**23


class Sem:
    def __init__(self, stack, nc, name):
        self.h = stack.enter_context(nc.semaphore(name))
        self.c = 0


class Prog:
    ENGS = ("sync", "tensor", "vector", "scalar", "gpsimd")

    def __init__(self):
        self.items = {e: [] for e in self.ENGS}

    def op(self, eng, fn, waits=(), inc=None, inc_n=1):
        incs = []
        if inc is not None:
            inc.c += inc_n
            incs.append((inc, inc_n))
        self.items[eng].append((tuple(waits), fn, incs))

    def dma(self, eng, out, in_, sem, waits=(), slow=False):
        self.op(eng, lambda e, o=out, i=in_: e.dma_start(
            out=o, in_=i, allow_slow_non_contiguous=slow),
                waits=waits, inc=sem, inc_n=16)

    def realize(self, block):
        def mk(eng_name):
            def body(e):
                waited = {}
                for waits, fn, incs in self.items[eng_name]:
                    for s, v in waits:
                        if v > 0 and waited.get(id(s), -1) < v:
                            e.wait_ge(s.h, v)
                            waited[id(s)] = v
                    if fn is None:
                        continue
                    ins = fn(e)
                    if incs and eng_name in ("vector", "scalar"):
                        ins = e.drain(fusable=True)
                    for s, n in incs:
                        ins.then_inc(s.h, n)
            return body
        block.sync(mk("sync"))
        block.tensor(mk("tensor"))
        block.vector(mk("vector"))
        block.scalar(mk("scalar"))
        block.gpsimd(mk("gpsimd"))


DBG_P0 = True
DBG_SLOTS = NSLOT
DBG_KEYS = True
DBG_P2 = True
DBG_P3 = NSLOT
DBG_STAGE = 9
DBG_PROJ = 3
DBG_TAP = None


def build_nc():
    nc = bass.Bass("TRN2", target_bir_lowering=False)

    k_in = {j: nc.dram_tensor(f"k{j}", [1024 * (j + 1), D], F32,
                              kind="ExternalInput").ap() for j in range(4, 8)}
    q_in = {j: nc.dram_tensor(f"q{j}", [CH, D], F32, kind="ExternalInput").ap()
            for j in range(8)}
    m_in = {j: nc.dram_tensor(f"m{j}", [1024, CH], BF16, kind="ExternalInput").ap()
            for j in range(8)}
    wq_in = nc.dram_tensor("wq", [D, D], F32, kind="ExternalInput").ap()
    wk_in = nc.dram_tensor("wk", [D, D], F32, kind="ExternalInput").ap()
    wv_in = nc.dram_tensor("wv", [D, D], F32, kind="ExternalInput").ap()
    wi8_in = nc.dram_tensor("wi8", [D, D], I8, kind="ExternalInput").ap()
    wsc_in = nc.dram_tensor("wsc", [128, DCH], F32, kind="ExternalInput").ap()
    wbs_in = nc.dram_tensor("wbs", [128, DCH], F32, kind="ExternalInput").ap()
    iden_in = nc.dram_tensor("iden", [128, 128], F32, kind="ExternalInput").ap()
    idb_in = nc.dram_tensor("idb", [128, 128], BF16, kind="ExternalInput").ap()
    out_d = nc.dram_tensor("out", [NSLOT * CH, D], F32, kind="ExternalOutput").ap()

    stash = nc.dram_tensor("stash", [NSLOT * CH, D], F32).ap()
    wvt_d = nc.dram_tensor("wvt", [D, D], F32).ap()
    cc1_in = nc.dram_tensor("cc1_in", [1, 1], F32).ap()
    cc1_out = nc.dram_tensor("cc1_out", [1, 1], F32).ap()
    cc2_in = nc.dram_tensor("cc2_in", [1, 1], F32).ap()
    cc2_out = nc.dram_tensor("cc2_out", [1, 1], F32).ap()

    stack = ExitStack()
    sb = lambda name, shape, dt: stack.enter_context(
        nc.sbuf_tensor(name, shape, dt))

    msb = [sb(f"msb{i}", [128, D], F32) for i in range(DCH)]
    wt = [sb(f"wt{i}", [128, D], BF16) for i in range(DCH)]
    mask = [sb(f"mask{i}", [128, CH], BF16) for i in range(8)]
    qown = [sb(f"qown{i}", [128, D], F32) for i in range(4)]
    xfq = [sb(f"xfq{i}", [128, CH], F32) for i in range(DCH)]
    qp = [sb(f"qp{i}", [128, CH], F32) for i in range(DCH)]
    xk = [sb(f"xk{i}", [128, D], F32) for i in range(RING)]
    xf = [sb(f"xf{i}", [128, D], F32) for i in range(2)]
    pt = [sb(f"pt{i}", [128, CH], F32) for i in range(8)]
    zacc = [sb(f"zacc{i}", [128, D], F32) for i in range(4)]
    vout = zacc
    xqt = sb("xqt", [128, 4 * D], BF16)

    iden = sb("iden_s", [128, 128], F32)
    idb = sb("idb_s", [128, 128], BF16)
    ones_c = sb("ones_c", [128, 1], F32)
    ones_b = sb("ones_b", [1, 128], BF16)
    eps_c = sb("eps_c", [128, 1], F32)
    wsc_c = sb("wsc_c", [128, DCH], F32)
    wbs_c = sb("wbs_c", [128, DCH], F32)
    sc2_c = sb("sc2_c", [128, DCH], F32)
    kn2 = sb("kn2", [1, 1], F32)
    knl = sb("knl", [128, 1], F32)
    knr = sb("knr", [1, 128], F32)
    kns = sb("kns", [1, 1], F32)
    gmax = sb("gmax", [128, 1], F32)
    gmr = sb("gmr", [1, 128], F32)
    gms = sb("gms", [1, 1], F32)
    xsr = sb("xsr", [128, 1], F32)
    xs_c = sb("xs_c", [128, 1], F32)
    rxs_c = sb("rxs_c", [128, 1], F32)
    l_sb = sb("l_sb", [128, 4], F32)
    rl_sb = sb("rl_sb", [128, 4], F32)
    srow = sb("srow", [128, 4], F32)
    ssq = sb("ssq", [128, 4], F32)
    negm = sb("negm", [128, 4], F32)
    vvar = sb("vvar", [128, 4], F32)
    m2t = sb("m2t", [128, 4], F32)
    sdt = sb("sdt", [128, 4], F32)
    rstd = sb("rstd", [128, 4], F32)
    tmax = sb("tmax", [128, 1], F32)
    dbg1 = sb("dbg1", [128, 4], F32)
    dbg2 = sb("dbg2", [128, 4], F32)
    nbrow = sb("nbrow", [1, CH], F32)
    nbb = sb("nbb", [1, CH], BF16)

    psAll = stack.enter_context(nc.psum_tensor("psAll", [128, 4096], F32))

    def bank(b, w=512):
        return psAll[:, 512 * b:512 * b + w]
    psZ = [psAll[:, 0:1024], psAll[:, 1024:2048]]
    psS = [bank(4), bank(5)]
    psQ = [bank(4), bank(5)]
    psL = bank(6)[:, 0:4]
    psB = bank(6)[0:1, 0:512]
    psT = bank(7)
    psTball = bank(7).bitcast(BF16)
    psF = [bank(0), bank(1)]
    psV = [bank(2), bank(3)]

    sK = Sem(stack, nc, "sK")
    sG = Sem(stack, nc, "sG")
    sT = Sem(stack, nc, "sT")
    sV = Sem(stack, nc, "sV")
    sS = Sem(stack, nc, "sS")
    sC = Sem(stack, nc, "sC")

    P = Prog()

    wload = kn_g = wt_s = wvt_g = mprime_t = mprime_v = wi8_k = 0
    P.dma("sync", iden[:], iden_in, sK)
    P.dma("sync", idb[:], idb_in, sK)
    P.dma("sync", wsc_c[:], wsc_in, sK)
    P.dma("sync", wbs_c[:], wbs_in, sK)
    P.op("vector", lambda e: e.memset(ones_c[:], 1.0))
    P.op("vector", lambda e: e.memset(ones_b[:], 1.0))
    P.op("vector", lambda e: e.memset(eps_c[:], 1e-5))
    P.op("vector", lambda e: e.memset(gmax[:], 0.0))
    P.op("vector", lambda e: e.memset(knl[:], 0.0), inc=sV)

    wq_t = xk[:8]
    wk_t = qown + zacc
    for o in range(DCH) if DBG_P0 else []:
        P.dma("sync", wq_t[o][:], wq_in[128 * o:128 * o + 128, :], sK)
        P.dma("sync", wk_t[o][:], wk_in[128 * o:128 * o + 128, :], sK)
    wload = sK.c

    if DBG_P0:
        evac_v = {}
        for n in range(16):
            i, h = n // 2, n % 2
            pq = psQ[n % 2]
            w = [(sK, wload)]
            if n >= 2:
                w.append((sV, evac_v[n - 2]))
            for o in range(DCH):
                P.op("tensor",
                     lambda e, pq=pq, o=o, i=i, h=h: e.matmul(
                         pq, wq_t[o][:, 128 * i:128 * i + 128],
                         wk_t[o][:, 512 * h:512 * h + 512],
                         start=(o == 0), stop=(o == DCH - 1)),
                     waits=w if o == 0 else (),
                     inc=sT if o == DCH - 1 else None)
            P.op("vector",
                 lambda e, pq=pq, i=i, h=h: e.tensor_scalar(
                     msb[i][:, 512 * h:512 * h + 512], pq, 0.03125, None, ALU.mult),
                 waits=[(sT, sT.c)], inc=sV)
            evac_v[n] = sV.c
        mprime_v = sV.c
        mprime_t = sT.c

        for j in range(NSLOT):
            for qt in range(4):
                P.dma("sync", xf[1][:], q_in[j][128 * qt:128 * qt + 128, :], sK,
                      waits=[(sS, sS.c)])
                P.op("scalar",
                     lambda e: e.activation(xf[0][:], xf[1][:], ACT.Square,
                                            accum_out=tmax[:]),
                     waits=[(sK, sK.c), (sV, sV.c)], inc=sS)
                P.op("vector", lambda e: e.tensor_max(knl[:], knl[:], tmax[:]),
                     waits=[(sS, sS.c)], inc=sV)
        P.dma("gpsimd", knr[:], knl[:], sG, waits=[(sV, sV.c)])
        P.op("vector", lambda e: e.tensor_reduce(kns[:], knr[:], AX, ALU.max),
             waits=[(sG, sG.c)], inc=sV)
        P.dma("gpsimd", cc1_in, kns[:], sG, waits=[(sV, sV.c)])
        P.op("gpsimd",
             lambda e: e.collective_compute("AllReduce", ALU.max,
                                            replica_groups=[list(range(NCORES))],
                                            ins=[cc1_in], outs=[cc1_out]),
             waits=[(sG, sG.c)], inc=sC)
        P.dma("gpsimd", kn2[:], cc1_out, sG, waits=[(sC, sC.c)])
        kn_g = sG.c

        xf0b = xf[0].bitcast(BF16)[:, 0:D]
        for nt in range(DCH):
            P.dma("sync", mask[nt][:].bitcast(I8)[:, 0:D],
                  wi8_in[128 * nt:128 * nt + 128, :], sK)
        wi8_k = sK.c
        for nt in range(DCH):
            P.op("vector",
                 lambda e, nt=nt: e.tensor_copy(xf0b, mask[nt][:].bitcast(I8)[:, 0:D]),
                 waits=[(sK, wi8_k), (sS, sS.c), (sT, sT.c)], inc=sV)
            for dc in range(DCH):
                P.op("tensor",
                     lambda e, dc=dc: e.transpose(
                         psTball[:, 128 * dc:128 * dc + 128],
                         xf0b[:, 128 * dc:128 * dc + 128], idb[:]),
                     waits=[(sV, sV.c), (sS, sS.c)] if dc == 0 else (),
                     inc=sT if dc == DCH - 1 else None)
            for dc in range(DCH):
                P.op("scalar",
                     lambda e, nt=nt, dc=dc: e.copy(
                         wt[dc][:, 128 * nt:128 * nt + 128],
                         psTball[:, 128 * dc:128 * dc + 128]),
                     waits=[(sT, sT.c)], inc=sS if dc == DCH - 1 else None)
        wt_s = sS.c

        wv_asm = qown + zacc
        for nt in range(DCH):
            P.dma("sync", xf[1][:], wv_in[128 * nt:128 * nt + 128, :], sK,
                  waits=[(sS, sS.c)])
            ldk = sK.c
            for g in range(2):
                for b in range(4):
                    dc = 4 * g + b
                    P.op("tensor",
                         lambda e, dc=dc, b=b: e.transpose(
                             psT[:, 128 * b:128 * b + 128],
                             xf[1][:, 128 * dc:128 * dc + 128], iden[:]),
                         waits=[(sK, ldk), (sS, sS.c)] if b == 0 else (),
                         inc=sT if b == 3 else None)
                for b in range(4):
                    dc = 4 * g + b
                    P.op("scalar",
                         lambda e, nt=nt, dc=dc, b=b: e.copy(
                             wv_asm[dc][:, 128 * nt:128 * nt + 128],
                             psT[:, 128 * b:128 * b + 128]),
                         waits=[(sT, sT.c)], inc=sS if b == 3 else None)
        for dc in range(DCH):
            P.dma("gpsimd", wvt_d[128 * dc:128 * dc + 128, :], wv_asm[dc][:], sG,
                  waits=[(sS, sS.c)])
        wvt_g = sG.c

    zwv_t_prev = 0
    prev_vo_v = 0
    for j in range(DBG_SLOTS):
        ktiles = 8 * (j + 1)
        nblk = ktiles // 8
        kj = k_in[j + 4] if j < 4 else k_in[j]

        for t8 in range(8):
            P.dma("sync", mask[t8][:], m_in[j][128 * t8:128 * t8 + 128, :], sK,
                  waits=[(sS, wt_s), (sV, sV.c)])
        mload = sK.c
        for qt in range(4):
            P.dma("sync", qown[qt][:], q_in[j][128 * qt:128 * qt + 128, :], sK,
                  waits=[(sV, prev_vo_v)] + ([(sG, wvt_g)] if j == 0 else []))
        qload = sK.c
        if DBG_STAGE < 1:
            continue

        for qt in range(4):
            for g in range(2):
                for b in range(4):
                    dc = 4 * g + b
                    P.op("tensor",
                         lambda e, qt=qt, dc=dc, b=b: e.transpose(
                             psT[:, 128 * b:128 * b + 128],
                             qown[qt][:, 128 * dc:128 * dc + 128], iden[:]),
                         waits=[(sK, qload), (sS, sS.c)]
                         if b == 0 else (),
                         inc=sT if b == 3 else None)
                for b in range(4):
                    dc = 4 * g + b
                    P.op("scalar",
                         lambda e, qt=qt, dc=dc, b=b: e.copy(
                             xfq[dc][:, 128 * qt:128 * qt + 128],
                             psT[:, 128 * b:128 * b + 128]),
                         waits=[(sT, sT.c)], inc=sS if b == 3 else None)
        xfq_s = sS.c
        if DBG_STAGE < 2:
            continue

        qp_evac = {}
        for i in range(DCH):
            pq = psQ[i % 2]
            w = [(sS, xfq_s)]
            if i >= 2:
                w = [(sV, qp_evac[i - 2][0]), (sS, qp_evac[i - 2][1])]
            for dc in range(DCH):
                P.op("tensor",
                     lambda e, pq=pq, i=i, dc=dc: e.matmul(
                         pq, msb[dc][:, 128 * i:128 * i + 128], xfq[dc][:],
                         start=(dc == 0), stop=(dc == DCH - 1)),
                     waits=w if dc == 0 else (),
                     inc=sT if dc == DCH - 1 else None)
            mmq = sT.c
            P.op("vector",
                 lambda e, pq=pq, i=i: e.tensor_copy(qp[i][:], pq),
                 waits=[(sT, mmq), (sG, sG.c)], inc=sV)
            cp_v = sV.c
            if DBG_PROJ >= 2:
                P.op("scalar",
                     lambda e, i=i: e.activation(xf[0][:, 0:512], qp[i][:],
                                                 ACT.Square),
                     waits=[(sV, cp_v)], inc=sS)
            sq_s = sS.c
            qp_evac[i] = (cp_v, sq_s)
            if DBG_PROJ >= 3:
                P.op("tensor",
                     lambda e, i=i: e.matmul(psB, ones_c[:], xf[0][:, 0:512],
                                             start=(i == 0),
                                             stop=(i == DCH - 1)),
                     waits=[(sS, sq_s)] + ([(sV, sV.c)] if i == 0 else []),
                     inc=sT if i == DCH - 1 else None)
        if DBG_STAGE < 3:
            continue
        P.op("scalar",
             lambda e: e.activation(nbrow[:], psB, ACT.Sqrt,
                                    scale=kn2[0:1, 0:1]),
             waits=[(sT, sT.c), (sG, kn_g)], inc=sS)
        P.op("vector",
             lambda e: e.tensor_scalar(nbb[:], nbrow[:], -1.0, None, ALU.mult),
             waits=[(sS, sS.c)], inc=sV)
        negb_v = sV.c

        if not DBG_KEYS:
            continue
        zinit = [False] * 4
        av_done_t = {}
        av_h0_t = {}
        ttadd_v = {}
        exp_s = {}
        first_l = True
        kdma_t = {}
        evac_g1_v = {}

        def key_dma(t):
            if t >= ktiles:
                return
            r = t % RING
            w = []
            if t >= RING:
                told = t - RING
                w.append((sT, av_h0_t[told // 8] if told % 8 < 4
                          else av_done_t[told // 8]))
            else:
                if j == 0:
                    w.append((sT, mprime_t))
                else:
                    w.append((sT, zwv_t_prev))
            P.dma("sync", xk[r][:], kj[128 * t:128 * t + 128, :], sK, waits=w)
            kdma_t[t] = sK.c

        def trans_group(t, g):
            r = t % RING
            xft = xf[t % 2]
            if g == 0:
                w = [(sK, kdma_t[t]), (sV, evac_g1_v.get(t - 1, 0)),
                     (sS, sS.c)]
            else:
                w = [(sV, sV.c)]
            for b in range(4):
                dc = 4 * g + b
                P.op("tensor",
                     lambda e, r=r, dc=dc, b=b: e.transpose(
                         psT[:, 128 * b:128 * b + 128],
                         xk[r][:, 128 * dc:128 * dc + 128], iden[:]),
                     waits=w if b == 0 else (),
                     inc=sT if b == 3 else None)
            P.op("vector",
                 lambda e, xft=xft, g=g: e.tensor_copy(
                     xft[:, 512 * g:512 * g + 512], psT[:, 0:512]),
                 waits=[(sT, sT.c)], inc=sV)
            if g == 1:
                evac_g1_v[t] = sV.c

        def scores_half(t, half):
            pss = psS[t % 2]
            xft = xf[t % 2]
            if half == 0:
                w = [(sV, max(evac_g1_v[t], negb_v))]
                if t >= 2:
                    w.append((sS, exp_s[t - 2]))
            else:
                w = []
            for dc in range(4 * half, 4 * half + 4):
                P.op("tensor",
                     lambda e, pss=pss, xft=xft, dc=dc: e.matmul(
                         pss, xft[:, 128 * dc:128 * dc + 128], qp[dc][:],
                         start=(dc == 0), stop=False),
                     waits=w if dc == 4 * half else ())
            if half == 1:
                P.op("tensor",
                     lambda e, pss=pss: e.matmul(pss, ones_b[:], nbb[:],
                                                 start=False, stop=True),
                     inc=sT)

        key_dma(0)
        key_dma(1)
        trans_group(0, 0)
        trans_group(0, 1)
        for t in range(ktiles):
            blk, tb = t // 8, t % 8
            if t + 1 < ktiles:
                trans_group(t + 1, 0)
            scores_half(t, 0)
            if t + 1 < ktiles:
                trans_group(t + 1, 1)
            scores_half(t, 1)
            scores_done = sT.c

            ex_w = [(sT, scores_done)]
            if blk > 0:
                ex_w.append((sT, av_h0_t[blk - 1] if tb < 4
                             else av_done_t[blk - 1]))
            P.op("scalar",
                 lambda e, pss=psS[t % 2], tb=tb: e.activation(
                     pt[tb][:], pss, ACT.Exp),
                 waits=ex_w, inc=sS)
            exp_s[t] = sS.c
            if t >= ktiles - 8:
                P.op("vector",
                     lambda e, tb=tb, mi=t - (ktiles - 8): e.tensor_tensor(
                         pt[tb][:], pt[tb][:], mask[mi][:], ALU.mult),
                     waits=[(sS, exp_s[t]), (sK, mload)], inc=sV)

            def av_half(ablk, half):
                need_s = exp_s[8 * ablk + 4 * half + 3]
                need_v = sV.c
                for qt in range(4):
                    pz = psZ[qt % 2]
                    av_w = [(sS, need_s), (sV, need_v)]
                    prev = (ablk, 0, qt) if half == 1 else \
                        (ablk - 1, 1, qt)
                    if qt >= 2:
                        prev = (ablk, half, qt - 2)
                    if prev in ttadd_v:
                        av_w.append((sV, ttadd_v[prev]))
                    for tb2 in range(4 * half, 4 * half + 4):
                        r2 = (8 * ablk + tb2) % RING
                        for h in range(2):
                            P.op("tensor",
                                 lambda e, pz=pz, tb2=tb2, qt=qt, r2=r2, h=h,
                                 half=half:
                                 e.matmul(
                                     pz[:, 512 * h:512 * h + 512],
                                     pt[tb2][:, 128 * qt:128 * qt + 128],
                                     xk[r2][:, 512 * h:512 * h + 512],
                                     start=(tb2 == 4 * half),
                                     stop=(tb2 == 4 * half + 3)),
                                 waits=av_w if (tb2 == 4 * half and h == 0)
                                 else ())
                        is_last_l = (ablk == nblk - 1 and qt == 3
                                     and tb2 == 7)
                        P.op("tensor",
                             lambda e, tb2=tb2, qt=qt,
                             fl=(ablk == 0 and qt == 0 and tb2 == 0),
                             ll=is_last_l:
                             e.matmul(psL[:, qt:qt + 1],
                                      pt[tb2][:, 128 * qt:128 * qt + 128],
                                      ones_c[:], start=fl, stop=ll,
                                      skip_group_check=True),
                             inc=sT if tb2 == 4 * half + 3 else None)
                    av_mm = sT.c
                    if not zinit[qt]:
                        zw = [(sT, av_mm)]
                        if j == 0:
                            zw.append((sG, wvt_g))
                        P.op("vector",
                             lambda e, qt=qt, pz=pz: e.tensor_copy(
                                 zacc[qt][:], pz),
                             waits=zw, inc=sV)
                        zinit[qt] = True
                    else:
                        P.op("vector",
                             lambda e, qt=qt, pz=pz: e.tensor_tensor(
                                 zacc[qt][:], zacc[qt][:], pz, ALU.add),
                             waits=[(sT, av_mm)], inc=sV)
                    ttadd_v[(ablk, half, qt)] = sV.c
                if half == 0:
                    av_h0_t[ablk] = sT.c
                else:
                    av_done_t[ablk] = sT.c

            if tb == 5:
                av_half(blk, 0)
            elif tb == 1 and blk > 0:
                av_half(blk - 1, 1)
            if t == ktiles - 1:
                av_half(blk, 1)
            key_dma(t + 2)

        av_last_t = av_done_t[nblk - 1]
        av_last_v = sV.c

        P.op("vector", lambda e: e.tensor_copy(l_sb[:], psL),
             waits=[(sT, av_last_t)], inc=sV)
        P.op("vector", lambda e: e.reciprocal(rl_sb[:], l_sb[:]), inc=sV)
        rl_v = sV.c

        for qt in range(4):
            for g in range(2):
                for b in range(4):
                    dc = 4 * g + b
                    P.op("tensor",
                         lambda e, qt=qt, dc=dc, b=b: e.transpose(
                             psT[:, 128 * b:128 * b + 128],
                             zacc[qt][:, 128 * dc:128 * dc + 128], iden[:]),
                         waits=[(sV, sV.c), (sS, sS.c)] if b == 0 else (),
                         inc=sT if b == 3 else None)
                for b in range(4):
                    dc = 4 * g + b
                    P.op("scalar",
                         lambda e, qt=qt, dc=dc, b=b: e.copy(
                             xfq[dc][:, 128 * qt:128 * qt + 128],
                             psT[:, 128 * b:128 * b + 128]),
                         waits=[(sT, sT.c)], inc=sS if b == 3 else None)
        zf_s = sS.c

        for dc in range(DCH):
            P.dma("sync", xk[dc][:], wvt_d[128 * dc:128 * dc + 128, :], sK,
                  waits=[(sT, av_last_t), (sG, wvt_g)])
        wvt_k = sK.c

        vo_prev = {}
        for n in range(8):
            qt, h = n // 2, n % 2
            pv = psV[n % 2]
            w = [(sK, wvt_k), (sS, zf_s)]
            if n >= 2:
                w.append((sV, vo_prev[n - 2]))
            for dc in range(DCH):
                P.op("tensor",
                     lambda e, pv=pv, qt=qt, h=h, dc=dc: e.matmul(
                         pv, xfq[dc][:, 128 * qt:128 * qt + 128],
                         xk[dc][:, 512 * h:512 * h + 512],
                         start=(dc == 0), stop=(dc == DCH - 1)),
                     waits=w if dc == 0 else (),
                     inc=sT if dc == DCH - 1 else None)
            P.op("vector",
                 lambda e, pv=pv, qt=qt, h=h: e.scalar_tensor_tensor(
                     vout[qt][:, 512 * h:512 * h + 512], pv,
                     rl_sb[:, qt:qt + 1], qown[qt][:, 512 * h:512 * h + 512],
                     ALU.mult, ALU.add),
                 waits=[(sT, sT.c), (sV, rl_v), (sS, sS.c)], inc=sV)
            vo_prev[n] = sV.c
        prev_vo_v = sV.c
        zwv_t_prev = sT.c

        for qt in range(4):
            P.op("vector",
                 lambda e, qt=qt: e.reduce_sum(srow[:, qt:qt + 1],
                                               vout[qt][:], AX),
                 waits=[(sV, vo_prev[2 * qt + 1])], inc=sV)
            P.op("scalar",
                 lambda e, qt=qt: e.activation(xf[1][:], vout[qt][:],
                                               ACT.Square,
                                               accum_out=ssq[:, qt:qt + 1]),
                 waits=[(sV, vo_prev[2 * qt + 1])], inc=sS)
        P.op("vector",
             lambda e: e.tensor_scalar(negm[:], srow[:], -1.0 / D, None,
                                       ALU.mult),
             waits=[(sV, sV.c)], inc=sV)
        P.op("scalar", lambda e: e.activation(m2t[:], negm[:], ACT.Square),
             waits=[(sV, sV.c)], inc=sS)
        P.op("vector",
             lambda e: e.tensor_scalar(vvar[:], ssq[:], 1.0 / D, None,
                                       ALU.mult),
             waits=[(sS, sS.c)], inc=sV)
        P.op("vector", lambda e: e.tensor_copy(dbg1[:], vvar[:]), inc=sV)
        P.op("vector",
             lambda e: e.tensor_tensor(vvar[:], vvar[:], m2t[:], ALU.subtract),
             inc=sV)
        P.op("vector", lambda e: e.tensor_copy(dbg2[:], vvar[:]), inc=sV)
        P.op("scalar",
             lambda e: e.activation(sdt[:], vvar[:], ACT.Sqrt,
                                    bias=eps_c[:, 0:1]),
             waits=[(sV, sV.c)], inc=sS)
        P.op("vector", lambda e: e.reciprocal(rstd[:], sdt[:]),
             waits=[(sS, sS.c)], inc=sV)
        ln1_v = sV.c
        for qt in range(4):
            for hh in range(2):
                P.op("vector",
                     lambda e, qt=qt, hh=hh: e.tensor_scalar(
                         qp[2 * qt + hh][:],
                         vout[qt][:, 512 * hh:512 * hh + 512],
                         negm[:, qt:qt + 1],
                         rstd[:, qt:qt + 1], ALU.add, ALU.mult),
                     waits=[(sV, ln1_v)], inc=sV)
                P.op("vector",
                     lambda e, qt=qt, hh=hh: e.tensor_reduce(
                         tmax[:], qp[2 * qt + hh][:], AX, ALU.max,
                         apply_absolute_value=True),
                     inc=sV)
                P.op("vector",
                     lambda e: e.tensor_max(gmax[:], gmax[:], tmax[:]),
                     inc=sV)
                P.dma("gpsimd",
                      stash[CH * j + 128 * qt:CH * j + 128 * qt + 128,
                            512 * hh:512 * hh + 512],
                      qp[2 * qt + hh][:], sG, waits=[(sV, sV.c)])

    p2_v = sV.c
    stash_g = sG.c
    if DBG_P2:
        P.dma("gpsimd", gmr[:], gmax[:], sG, waits=[(sV, sV.c)])
        P.op("vector", lambda e: e.tensor_reduce(gms[:], gmr[:], AX, ALU.max),
             waits=[(sG, sG.c)], inc=sV)
        P.dma("gpsimd", cc2_in, gms[:], sG, waits=[(sV, sV.c)])
        P.op("gpsimd",
             lambda e: e.collective_compute("AllReduce", ALU.max,
                                            replica_groups=[list(range(NCORES))],
                                            ins=[cc2_in], outs=[cc2_out]),
             waits=[(sG, sG.c)], inc=sC)
        P.dma("gpsimd", xsr[:], cc2_out.partition_broadcast(128), sG,
              waits=[(sC, sC.c)])
        P.op("vector",
             lambda e: e.tensor_scalar(xs_c[:], xsr[:], 1.0 / 127.0, 1e-5,
                                       ALU.mult, ALU.max),
             waits=[(sG, sG.c)], inc=sV)
        P.op("vector", lambda e: e.reciprocal(rxs_c[:], xs_c[:]), inc=sV)
        P.op("vector",
             lambda e: e.tensor_scalar(sc2_c[:], wsc_c[:], xs_c[:, 0:1], None,
                                       ALU.mult),
             inc=sV)
        p2_v = sV.c
        stash_g = sG.c

    for j in range(DBG_P3):
        for qt in range(4):
            P.dma("sync", qown[qt][:],
                  stash[CH * j + 128 * qt:CH * j + 128 * qt + 128, :], sK,
                  waits=[(sG, stash_g), (sV, sV.c), (sT, sT.c)])
        uload = sK.c

        for qt in range(4):
            P.op("vector",
                 lambda e, qt=qt: e.tensor_scalar(
                     xf[0][:], qown[qt][:], rxs_c[:, 0:1], C2,
                     ALU.mult, ALU.add),
                 waits=[(sK, uload), (sV, p2_v), (sS, sS.c), (sT, sT.c)],
                 inc=sV)
            P.op("vector",
                 lambda e: e.tensor_scalar(xf[1][:], xf[0][:], -C2, None,
                                           ALU.add),
                 inc=sV)
            for hh in range(2):
                P.op("vector",
                     lambda e, qt=qt, hh=hh: e.tensor_scalar(
                         mask[2 * qt + hh][:],
                         xf[1][:, 512 * hh:512 * hh + 512],
                         127.0, -128.0, ALU.min, ALU.max),
                     inc=sV)
        xq_v = sV.c

        for qt in range(4):
            for hh in range(2):
                src = mask[2 * qt + hh]
                for b in range(4):
                    dc = 4 * hh + b
                    P.op("tensor",
                         lambda e, src=src, b=b, dc=dc: e.transpose(
                             psTball[:, 128 * dc:128 * dc + 128],
                             src[:, 128 * b:128 * b + 128], idb[:]),
                         waits=[(sV, sV.c), (sS, sS.c)]
                         if (hh == 0 and b == 0) else (),
                         inc=sT if b == 3 else None)
            P.op("vector",
                 lambda e, qt=qt: e.tensor_copy(
                     xqt.rearrange("p (dc q) -> p dc q", dc=8)[
                         :, :, 128 * qt:128 * qt + 128],
                     psTball[:, 0:1024].rearrange("p (dc q) -> p dc q", dc=8)),
                 waits=[(sT, sT.c)], inc=sV)
        xqt_v = sV.c

        ffn_evac = {}
        for nt in range(DCH):
            pf = psF[nt % 2]
            w = [(sV, xqt_v)]
            if nt >= 2:
                w = [(sV, ffn_evac[nt - 2])]
            for dc in range(DCH):
                P.op("tensor",
                     lambda e, pf=pf, nt=nt, dc=dc: e.matmul(
                         pf, wt[dc][:, 128 * nt:128 * nt + 128],
                         xqt[:, 512 * dc:512 * dc + 512],
                         start=(dc == 0), stop=(dc == DCH - 1)),
                     waits=w if dc == 0 else (),
                     inc=sT if dc == DCH - 1 else None)
            P.op("vector",
                 lambda e, pf=pf, nt=nt: e.tensor_scalar(
                     qp[nt][:], pf, sc2_c[:, nt:nt + 1], wbs_c[:, nt:nt + 1],
                     ALU.mult, ALU.add),
                 waits=[(sT, sT.c)], inc=sV)
            ffn_evac[nt] = sV.c

        for qt in range(4):
            for g in range(2):
                for b in range(4):
                    nt = 4 * g + b
                    P.op("tensor",
                         lambda e, qt=qt, nt=nt, b=b: e.transpose(
                             psT[:, 128 * b:128 * b + 128],
                             qp[nt][:, 128 * qt:128 * qt + 128], iden[:]),
                         waits=[(sV, sV.c), (sS, sS.c)] if b == 0 else (),
                         inc=sT if b == 3 else None)
                P.op("vector",
                     lambda e, qt=qt, g=g: e.tensor_tensor(
                         vout[qt][:, 512 * g:512 * g + 512], psT[:, 0:512],
                         qown[qt][:, 512 * g:512 * g + 512], ALU.add),
                     waits=[(sT, sT.c), (sG, sG.c)], inc=sV)
        s2_v = sV.c

        for qt in range(4):
            P.op("vector",
                 lambda e, qt=qt: e.reduce_sum(srow[:, qt:qt + 1],
                                               vout[qt][:], AX),
                 waits=[(sV, s2_v)], inc=sV)
            P.op("scalar",
                 lambda e, qt=qt: e.activation(xf[1][:], vout[qt][:],
                                               ACT.Square,
                                               accum_out=ssq[:, qt:qt + 1]),
                 waits=[(sV, s2_v)], inc=sS)
        P.op("vector",
             lambda e: e.tensor_scalar(negm[:], srow[:], -1.0 / D, None,
                                       ALU.mult),
             waits=[(sV, sV.c)], inc=sV)
        P.op("scalar", lambda e: e.activation(m2t[:], negm[:], ACT.Square),
             waits=[(sV, sV.c)], inc=sS)
        P.op("vector",
             lambda e: e.tensor_scalar(vvar[:], ssq[:], 1.0 / D, None,
                                       ALU.mult),
             waits=[(sS, sS.c)], inc=sV)
        P.op("vector",
             lambda e: e.tensor_tensor(vvar[:], vvar[:], m2t[:], ALU.subtract),
             inc=sV)
        P.op("scalar",
             lambda e: e.activation(sdt[:], vvar[:], ACT.Sqrt,
                                    bias=eps_c[:, 0:1]),
             waits=[(sV, sV.c)], inc=sS)
        P.op("vector", lambda e: e.reciprocal(rstd[:], sdt[:]),
             waits=[(sS, sS.c)], inc=sV)
        ln2_v = sV.c
        for qt in range(4):
            P.op("vector",
                 lambda e, qt=qt: e.tensor_scalar(
                     zacc[qt][:], vout[qt][:], negm[:, qt:qt + 1],
                     rstd[:, qt:qt + 1], ALU.add, ALU.mult),
                 waits=[(sV, ln2_v), (sG, sG.c)], inc=sV)
            P.dma("gpsimd", out_d[CH * j + 128 * qt:CH * j + 128 * qt + 128, :],
                  zacc[qt][:], sG, waits=[(sV, sV.c)])

    if DBG_TAP:
        fw = [(sT, sT.c), (sV, sV.c), (sS, sS.c), (sK, sK.c)]
        def tap(row, ap, cols=D):
            P.dma("gpsimd", out_d[row:row + ap.shape[0], 0:cols], ap, sG,
                  waits=fw, slow=True)
        if DBG_TAP == "msb":
            for i in range(DCH):
                tap(128 * i, msb[i][:])
        elif DBG_TAP == "qp":
            for i in range(DCH):
                tap(128 * i, qp[i][:], 512)
        elif DBG_TAP == "xfq":
            for i in range(DCH):
                tap(128 * i, xfq[i][:], 512)
        elif DBG_TAP == "zacc":
            for i in range(4):
                tap(128 * i, zacc[i][:])
        elif DBG_TAP == "qown":
            for i in range(4):
                tap(128 * i, qown[i][:])
        elif DBG_TAP == "pt":
            for i in range(8):
                tap(128 * i, pt[i][:], 512)
        elif DBG_TAP == "wt":
            for i in range(DCH):
                P.dma("gpsimd", out_d[128 * i:128 * i + 128, :], wt[i][:],
                      sG, waits=fw)
        elif DBG_TAP == "ln1":
            for i in range(4):
                tap(128 * i, zacc[i][:])
            tap(3000, srow[:], 4)
            tap(3200, ssq[:], 4)
            tap(3400, negm[:], 4)
            tap(3600, rstd[:], 4)
            tap(3800, l_sb[:], 4)
            tap(2000, vvar[:], 4)
            tap(2200, sdt[:], 4)
            tap(2400, m2t[:], 4)
            tap(2600, dbg1[:], 4)
            tap(2800, dbg2[:], 4)
        elif DBG_TAP == "scalars":
            tap(0, kn2[:], 1)
            tap(1, nbrow[:], 512)
            tap(128, l_sb[:], 4)
            tap(256, rl_sb[:], 4)
            tap(384, rstd[:], 4)
            tap(512, negm[:], 4)
            tap(640, xs_c[:], 1)
            tap(768, srow[:], 4)
            tap(896, ssq[:], 4)

    P.op("gpsimd", None, waits=[(sG, sG.c)])

    with nc.Block() as block:
        P.realize(block)
    stack.close()
    return nc


_NC_CACHE = None


def _get_nc():
    global _NC_CACHE
    if _NC_CACHE is None:
        _NC_CACHE = build_nc()
    return _NC_CACHE


def unit(c, j):
    p = (c + j) % 2
    return ((c // 2 + j) % 4, 2 * j + p)


def make_in_maps(x, wq, wk, wv, wi8, wsc, wbs):
    wsc_c = np.ascontiguousarray(wsc.reshape(DCH, 128).T)
    wbs_c = np.ascontiguousarray(wbs.reshape(DCH, 128).T)
    iden = np.eye(128, dtype=np.float32)
    idb = np.eye(128, dtype=ml_dtypes.bfloat16)
    in_maps = []
    for c in range(NCORES):
        im = {"wq": wq, "wk": wk, "wv": wv, "wi8": wi8,
              "wsc": wsc_c, "wbs": wbs_c, "iden": iden, "idb": idb}
        for j in range(NSLOT):
            b, ch = unit(c, j)
            im[f"q{j}"] = x[b, CH * ch:CH * ch + CH, :]
            keys = np.arange(1024 * j, 1024 * (j + 1))
            qs = CH * ch + np.arange(CH)
            im[f"m{j}"] = (keys[:, None] <= qs[None, :]).astype(
                ml_dtypes.bfloat16)
            if j >= 4:
                im[f"k{j}"] = x[b, :1024 * (j + 1), :]
        in_maps.append(im)
    return in_maps


def kernel(L0_attn_input_x, L0_attn_Wq, L0_attn_Wk, L0_attn_Wv,
           L0_ffn_weight_int8, L0_ffn_weight_scale, L0_ffn_bias):
    x = np.ascontiguousarray(np.asarray(L0_attn_input_x, dtype=np.float32))
    wq = np.ascontiguousarray(np.asarray(L0_attn_Wq, dtype=np.float32))
    wk = np.ascontiguousarray(np.asarray(L0_attn_Wk, dtype=np.float32))
    wv = np.ascontiguousarray(np.asarray(L0_attn_Wv, dtype=np.float32))
    wi8 = np.ascontiguousarray(np.asarray(L0_ffn_weight_int8, dtype=np.int8))
    wsc = np.asarray(L0_ffn_weight_scale, dtype=np.float32).reshape(D)
    wbs = np.asarray(L0_ffn_bias, dtype=np.float32).reshape(D)

    in_maps = make_in_maps(x, wq, wk, wv, wi8, wsc, wbs)
    nc = _get_nc()
    res = run_bass_kernel_spmd(nc, in_maps, core_ids=list(range(NCORES)))

    full = np.empty((B_, T_, D), dtype=np.float32)
    for c in range(NCORES):
        o = np.asarray(res.results[c]["out"])
        for j in range(NSLOT):
            b, ch = unit(c, j)
            full[b, CH * ch:CH * ch + CH, :] = o[CH * j:CH * j + CH, :]
    return full


if __name__ == "__main__":
    nc = build_nc()
    print("built ok; instructions:", nc.n_instructions())


# version 07
# speedup vs baseline: 1.0279x; 1.0279x over previous
import numpy as np
import ml_dtypes
import concourse.bass as bass
import concourse.mybir as mybir
from concourse.bass_utils import run_bass_kernel_spmd
from contextlib import ExitStack

F32 = mybir.dt.float32
BF16 = mybir.dt.bfloat16
I8 = mybir.dt.int8
ALU = mybir.AluOpType
ACT = mybir.ActivationFunctionType
AX = mybir.AxisListType.X

NCORES = 8
B_, T_, D = 4, 8192, 1024
CH = 512
NSLOT = 8
DCH = D // 128
RING = 9
C2 = 1.5 * 2.0**23


class Sem:
    def __init__(self, stack, nc, name):
        self.h = stack.enter_context(nc.semaphore(name))
        self.c = 0


class Prog:
    ENGS = ("sync", "tensor", "vector", "scalar", "gpsimd")

    def __init__(self):
        self.items = {e: [] for e in self.ENGS}

    def op(self, eng, fn, waits=(), inc=None, inc_n=1):
        incs = []
        if inc is not None:
            inc.c += inc_n
            incs.append((inc, inc_n))
        self.items[eng].append((tuple(waits), fn, incs))

    def dma(self, eng, out, in_, sem, waits=(), slow=False):
        self.op(eng, lambda e, o=out, i=in_: e.dma_start(
            out=o, in_=i, allow_slow_non_contiguous=slow),
                waits=waits, inc=sem, inc_n=16)

    def realize(self, block):
        def mk(eng_name):
            def body(e):
                waited = {}
                for waits, fn, incs in self.items[eng_name]:
                    for s, v in waits:
                        if v > 0 and waited.get(id(s), -1) < v:
                            e.wait_ge(s.h, v)
                            waited[id(s)] = v
                    if fn is None:
                        continue
                    ins = fn(e)
                    if incs and eng_name in ("vector", "scalar"):
                        ins = e.drain(fusable=True)
                    for s, n in incs:
                        ins.then_inc(s.h, n)
            return body
        block.sync(mk("sync"))
        block.tensor(mk("tensor"))
        block.vector(mk("vector"))
        block.scalar(mk("scalar"))
        block.gpsimd(mk("gpsimd"))


DBG_P0 = True
DBG_SLOTS = NSLOT
DBG_KEYS = True
DBG_P2 = True
DBG_P3 = NSLOT
DBG_STAGE = 9
DBG_PROJ = 3
DBG_TAP = None


def build_nc():
    nc = bass.Bass("TRN2", target_bir_lowering=False)

    k_in = {j: nc.dram_tensor(f"k{j}", [1024 * (j + 1), D], F32,
                              kind="ExternalInput").ap() for j in range(4, 8)}
    q_in = {j: nc.dram_tensor(f"q{j}", [CH, D], F32, kind="ExternalInput").ap()
            for j in range(8)}
    m_in = {j: nc.dram_tensor(f"m{j}", [1024, CH], BF16, kind="ExternalInput").ap()
            for j in range(8)}
    wq_in = nc.dram_tensor("wq", [D, D], F32, kind="ExternalInput").ap()
    wk_in = nc.dram_tensor("wk", [D, D], F32, kind="ExternalInput").ap()
    wv_in = nc.dram_tensor("wv", [D, D], F32, kind="ExternalInput").ap()
    wi8_in = nc.dram_tensor("wi8", [D, D], I8, kind="ExternalInput").ap()
    wsc_in = nc.dram_tensor("wsc", [128, DCH], F32, kind="ExternalInput").ap()
    wbs_in = nc.dram_tensor("wbs", [128, DCH], F32, kind="ExternalInput").ap()
    iden_in = nc.dram_tensor("iden", [128, 128], F32, kind="ExternalInput").ap()
    idb_in = nc.dram_tensor("idb", [128, 128], BF16, kind="ExternalInput").ap()
    out_d = nc.dram_tensor("out", [NSLOT * CH, D], F32, kind="ExternalOutput").ap()

    stash = nc.dram_tensor("stash", [NSLOT * CH, D], F32).ap()
    wvt_d = nc.dram_tensor("wvt", [D, D], F32).ap()
    cc1_in = nc.dram_tensor("cc1_in", [1, 1], F32).ap()
    cc1_out = nc.dram_tensor("cc1_out", [1, 1], F32).ap()
    cc2_in = nc.dram_tensor("cc2_in", [1, 1], F32).ap()
    cc2_out = nc.dram_tensor("cc2_out", [1, 1], F32).ap()

    stack = ExitStack()
    sb = lambda name, shape, dt: stack.enter_context(
        nc.sbuf_tensor(name, shape, dt))

    msb = [sb(f"msb{i}", [128, D], F32) for i in range(DCH)]
    wt = [sb(f"wt{i}", [128, D], BF16) for i in range(DCH)]
    mask = [sb(f"mask{i}", [128, CH], BF16) for i in range(8)]
    qown = [sb(f"qown{i}", [128, D], F32) for i in range(4)]
    xfq = [sb(f"xfq{i}", [128, CH], F32) for i in range(DCH)]
    qp = [sb(f"qp{i}", [128, CH], F32) for i in range(DCH)]
    xk = [sb(f"xk{i}", [128, D], F32) for i in range(RING)]
    xf = [sb(f"xf{i}", [128, D], F32) for i in range(2)]
    pt = [sb(f"pt{i}", [128, CH], F32) for i in range(8)]
    zacc = [sb(f"zacc{i}", [128, D], F32) for i in range(4)]
    vout = zacc
    xqt = sb("xqt", [128, 4 * D], BF16)

    iden = sb("iden_s", [128, 128], F32)
    idb = sb("idb_s", [128, 128], BF16)
    ones_c = sb("ones_c", [128, 1], F32)
    ones_b = sb("ones_b", [1, 128], BF16)
    eps_c = sb("eps_c", [128, 1], F32)
    wsc_c = sb("wsc_c", [128, DCH], F32)
    wbs_c = sb("wbs_c", [128, DCH], F32)
    sc2_c = sb("sc2_c", [128, DCH], F32)
    kn2 = sb("kn2", [1, 1], F32)
    knl = sb("knl", [128, 1], F32)
    knr = sb("knr", [1, 128], F32)
    kns = sb("kns", [1, 1], F32)
    gmax = sb("gmax", [128, 1], F32)
    gmr = sb("gmr", [1, 128], F32)
    gms = sb("gms", [1, 1], F32)
    xsr = sb("xsr", [128, 1], F32)
    xs_c = sb("xs_c", [128, 1], F32)
    rxs_c = sb("rxs_c", [128, 1], F32)
    l_sb = sb("l_sb", [128, 4], F32)
    rl_sb = sb("rl_sb", [128, 4], F32)
    srow = sb("srow", [128, 4], F32)
    ssq = sb("ssq", [128, 4], F32)
    negm = sb("negm", [128, 4], F32)
    vvar = sb("vvar", [128, 4], F32)
    m2t = sb("m2t", [128, 4], F32)
    sdt = sb("sdt", [128, 4], F32)
    rstd = sb("rstd", [128, 4], F32)
    tmax = sb("tmax", [128, 1], F32)
    dbg1 = sb("dbg1", [128, 4], F32)
    dbg2 = sb("dbg2", [128, 4], F32)
    nbrow = sb("nbrow", [1, CH], F32)
    nbb = sb("nbb", [1, CH], BF16)

    psAll = stack.enter_context(nc.psum_tensor("psAll", [128, 4096], F32))

    def bank(b, w=512):
        return psAll[:, 512 * b:512 * b + w]
    psZ = [psAll[:, 0:1024], psAll[:, 1024:2048]]
    psS = [bank(4), bank(5)]
    psQ = [bank(4), bank(5)]
    psL = bank(6)[:, 0:4]
    psB = bank(6)[0:1, 0:512]
    psT = bank(7)
    psTball = bank(7).bitcast(BF16)
    psF = [bank(0), bank(1)]
    psV = [bank(2), bank(3)]

    sK = Sem(stack, nc, "sK")
    sG = Sem(stack, nc, "sG")
    sT = Sem(stack, nc, "sT")
    sV = Sem(stack, nc, "sV")
    sS = Sem(stack, nc, "sS")
    sC = Sem(stack, nc, "sC")

    P = Prog()

    wload = kn_g = wt_s = wvt_g = mprime_t = mprime_v = wi8_k = 0
    P.dma("sync", iden[:], iden_in, sK)
    P.dma("sync", idb[:], idb_in, sK)
    P.dma("sync", wsc_c[:], wsc_in, sK)
    P.dma("sync", wbs_c[:], wbs_in, sK)
    P.op("vector", lambda e: e.memset(ones_c[:], 1.0))
    P.op("vector", lambda e: e.memset(ones_b[:], 1.0))
    P.op("vector", lambda e: e.memset(eps_c[:], 1e-5))
    P.op("vector", lambda e: e.memset(gmax[:], 0.0))
    P.op("vector", lambda e: e.memset(knl[:], 0.0), inc=sV)

    wq_t = xk[:8]
    wk_t = qown + zacc
    for o in range(DCH) if DBG_P0 else []:
        P.dma("sync", wq_t[o][:], wq_in[128 * o:128 * o + 128, :], sK)
        P.dma("sync", wk_t[o][:], wk_in[128 * o:128 * o + 128, :], sK)
    wload = sK.c

    if DBG_P0:
        evac_v = {}
        for n in range(16):
            i, h = n // 2, n % 2
            pq = psQ[n % 2]
            w = [(sK, wload)]
            if n >= 2:
                w.append((sV, evac_v[n - 2]))
            for o in range(DCH):
                P.op("tensor",
                     lambda e, pq=pq, o=o, i=i, h=h: e.matmul(
                         pq, wq_t[o][:, 128 * i:128 * i + 128],
                         wk_t[o][:, 512 * h:512 * h + 512],
                         start=(o == 0), stop=(o == DCH - 1)),
                     waits=w if o == 0 else (),
                     inc=sT if o == DCH - 1 else None)
            P.op("vector",
                 lambda e, pq=pq, i=i, h=h: e.tensor_scalar(
                     msb[i][:, 512 * h:512 * h + 512], pq, 0.03125, None, ALU.mult),
                 waits=[(sT, sT.c)], inc=sV)
            evac_v[n] = sV.c
        mprime_v = sV.c
        mprime_t = sT.c

        for j in range(NSLOT):
            for qt in range(4):
                P.dma("sync", xf[1][:], q_in[j][128 * qt:128 * qt + 128, :], sK,
                      waits=[(sS, sS.c)])
                P.op("scalar",
                     lambda e: e.activation(xf[0][:], xf[1][:], ACT.Square,
                                            accum_out=tmax[:]),
                     waits=[(sK, sK.c), (sV, sV.c)], inc=sS)
                P.op("vector", lambda e: e.tensor_max(knl[:], knl[:], tmax[:]),
                     waits=[(sS, sS.c)], inc=sV)
        P.dma("gpsimd", knr[:], knl[:], sG, waits=[(sV, sV.c)])
        P.op("vector", lambda e: e.tensor_reduce(kns[:], knr[:], AX, ALU.max),
             waits=[(sG, sG.c)], inc=sV)
        P.dma("gpsimd", cc1_in, kns[:], sG, waits=[(sV, sV.c)])
        P.op("gpsimd",
             lambda e: e.collective_compute("AllReduce", ALU.max,
                                            replica_groups=[list(range(NCORES))],
                                            ins=[cc1_in], outs=[cc1_out]),
             waits=[(sG, sG.c)], inc=sC)
        P.dma("gpsimd", kn2[:], cc1_out, sG, waits=[(sC, sC.c)])
        kn_g = sG.c

        xf0b = xf[0].bitcast(BF16)[:, 0:D]
        for nt in range(DCH):
            P.dma("sync", mask[nt][:].bitcast(I8)[:, 0:D],
                  wi8_in[128 * nt:128 * nt + 128, :], sK)
        wi8_k = sK.c
        for nt in range(DCH):
            P.op("vector",
                 lambda e, nt=nt: e.tensor_copy(xf0b, mask[nt][:].bitcast(I8)[:, 0:D]),
                 waits=[(sK, wi8_k), (sS, sS.c), (sT, sT.c)], inc=sV)
            for dc in range(DCH):
                P.op("tensor",
                     lambda e, dc=dc: e.transpose(
                         psTball[:, 128 * dc:128 * dc + 128],
                         xf0b[:, 128 * dc:128 * dc + 128], idb[:]),
                     waits=[(sV, sV.c), (sS, sS.c)] if dc == 0 else (),
                     inc=sT if dc == DCH - 1 else None)
            for dc in range(DCH):
                P.op("scalar",
                     lambda e, nt=nt, dc=dc: e.copy(
                         wt[dc][:, 128 * nt:128 * nt + 128],
                         psTball[:, 128 * dc:128 * dc + 128]),
                     waits=[(sT, sT.c)], inc=sS if dc == DCH - 1 else None)
        wt_s = sS.c

        wv_asm = qown + zacc
        for nt in range(DCH):
            P.dma("sync", xf[1][:], wv_in[128 * nt:128 * nt + 128, :], sK,
                  waits=[(sS, sS.c)])
            ldk = sK.c
            for g in range(2):
                for b in range(4):
                    dc = 4 * g + b
                    P.op("tensor",
                         lambda e, dc=dc, b=b: e.transpose(
                             psT[:, 128 * b:128 * b + 128],
                             xf[1][:, 128 * dc:128 * dc + 128], iden[:]),
                         waits=[(sK, ldk), (sS, sS.c)] if b == 0 else (),
                         inc=sT if b == 3 else None)
                for b in range(4):
                    dc = 4 * g + b
                    P.op("scalar",
                         lambda e, nt=nt, dc=dc, b=b: e.copy(
                             wv_asm[dc][:, 128 * nt:128 * nt + 128],
                             psT[:, 128 * b:128 * b + 128]),
                         waits=[(sT, sT.c)], inc=sS if b == 3 else None)
        for dc in range(DCH):
            P.dma("gpsimd", wvt_d[128 * dc:128 * dc + 128, :], wv_asm[dc][:], sG,
                  waits=[(sS, sS.c)])
        wvt_g = sG.c

    zwv_t_prev = 0
    prev_vo_v = 0
    for j in range(DBG_SLOTS):
        ktiles = 8 * (j + 1)
        nblk = ktiles // 8
        kj = k_in[j + 4] if j < 4 else k_in[j]

        for t8 in range(8):
            P.dma("sync", mask[t8][:], m_in[j][128 * t8:128 * t8 + 128, :], sK,
                  waits=[(sS, wt_s), (sV, sV.c)])
        mload = sK.c
        for qt in range(4):
            P.dma("sync", qown[qt][:], q_in[j][128 * qt:128 * qt + 128, :], sK,
                  waits=[(sV, prev_vo_v)] + ([(sG, wvt_g)] if j == 0 else []))
        qload = sK.c
        if DBG_STAGE < 1:
            continue

        for qt in range(4):
            for g in range(2):
                for b in range(4):
                    dc = 4 * g + b
                    P.op("tensor",
                         lambda e, qt=qt, dc=dc, b=b: e.transpose(
                             psT[:, 128 * b:128 * b + 128],
                             qown[qt][:, 128 * dc:128 * dc + 128], iden[:]),
                         waits=[(sK, qload), (sS, sS.c)]
                         if b == 0 else (),
                         inc=sT if b == 3 else None)
                for b in range(4):
                    dc = 4 * g + b
                    P.op("scalar",
                         lambda e, qt=qt, dc=dc, b=b: e.copy(
                             xfq[dc][:, 128 * qt:128 * qt + 128],
                             psT[:, 128 * b:128 * b + 128]),
                         waits=[(sT, sT.c)], inc=sS if b == 3 else None)
        xfq_s = sS.c
        if DBG_STAGE < 2:
            continue

        qp_evac = {}
        for i in range(DCH):
            pq = psQ[i % 2]
            w = [(sS, xfq_s)]
            if i >= 2:
                w = [(sV, qp_evac[i - 2][0]), (sS, qp_evac[i - 2][1])]
            for dc in range(DCH):
                P.op("tensor",
                     lambda e, pq=pq, i=i, dc=dc: e.matmul(
                         pq, msb[dc][:, 128 * i:128 * i + 128], xfq[dc][:],
                         start=(dc == 0), stop=(dc == DCH - 1)),
                     waits=w if dc == 0 else (),
                     inc=sT if dc == DCH - 1 else None)
            mmq = sT.c
            P.op("vector",
                 lambda e, pq=pq, i=i: e.tensor_copy(qp[i][:], pq),
                 waits=[(sT, mmq), (sG, sG.c)], inc=sV)
            cp_v = sV.c
            if DBG_PROJ >= 2:
                P.op("scalar",
                     lambda e, i=i: e.activation(xf[0][:, 0:512], qp[i][:],
                                                 ACT.Square),
                     waits=[(sV, cp_v)], inc=sS)
            sq_s = sS.c
            qp_evac[i] = (cp_v, sq_s)
            if DBG_PROJ >= 3:
                P.op("tensor",
                     lambda e, i=i: e.matmul(psB, ones_c[:], xf[0][:, 0:512],
                                             start=(i == 0),
                                             stop=(i == DCH - 1)),
                     waits=[(sS, sq_s)] + ([(sV, sV.c)] if i == 0 else []),
                     inc=sT if i == DCH - 1 else None)
        if DBG_STAGE < 3:
            continue
        P.op("scalar",
             lambda e: e.activation(nbrow[:], psB, ACT.Sqrt,
                                    scale=kn2[0:1, 0:1]),
             waits=[(sT, sT.c), (sG, kn_g)], inc=sS)
        P.op("vector",
             lambda e: e.tensor_scalar(nbb[:], nbrow[:], -1.0, None, ALU.mult),
             waits=[(sS, sS.c)], inc=sV)
        negb_v = sV.c

        if not DBG_KEYS:
            continue
        zinit = [False] * 4
        av_done_t = {}
        av_h0_t = {}
        ttadd_v = {}
        exp_s = {}
        first_l = True
        kdma_t = {}
        evac_g0_s = {}
        evac_g1_v = {}

        def key_dma(t):
            if t >= ktiles:
                return
            r = t % RING
            w = []
            if t >= RING:
                told = t - RING
                w.append((sT, av_h0_t[told // 8] if told % 8 < 4
                          else av_done_t[told // 8]))
            else:
                if j == 0:
                    w.append((sT, mprime_t))
                else:
                    w.append((sT, zwv_t_prev))
            P.dma("sync", xk[r][:], kj[128 * t:128 * t + 128, :], sK, waits=w)
            kdma_t[t] = sK.c

        def trans_group(t, g):
            r = t % RING
            xft = xf[t % 2]
            if g == 0:
                w = [(sK, kdma_t[t]), (sS, evac_g0_s.get(t - 1, 0)),
                     (sV, evac_g1_v.get(t - 1, 0))]
            else:
                w = [(sS, evac_g0_s[t])]
            for b in range(4):
                dc = 4 * g + b
                P.op("tensor",
                     lambda e, r=r, dc=dc, b=b: e.transpose(
                         psT[:, 128 * b:128 * b + 128],
                         xk[r][:, 128 * dc:128 * dc + 128], iden[:]),
                     waits=w if b == 0 else (),
                     inc=sT if b == 3 else None)
            if g == 0:
                P.op("scalar",
                     lambda e, xft=xft: e.copy(xft[:, 0:512], psT[:, 0:512]),
                     waits=[(sT, sT.c)], inc=sS)
                evac_g0_s[t] = sS.c
            else:
                P.op("vector",
                     lambda e, xft=xft: e.tensor_copy(xft[:, 512:1024],
                                                      psT[:, 0:512]),
                     waits=[(sT, sT.c)], inc=sV)
                evac_g1_v[t] = sV.c

        def scores_half(t, half):
            pss = psS[t % 2]
            xft = xf[t % 2]
            rng = range(0, 4) if half == 0 else range(4, 8)
            if half == 0:
                w = [(sS, evac_g0_s[t]), (sV, max(evac_g1_v[t], negb_v))]
                if t >= 2:
                    w.append((sS, exp_s[t - 2]))
            else:
                w = []
            for dc in rng:
                P.op("tensor",
                     lambda e, pss=pss, xft=xft, dc=dc: e.matmul(
                         pss, xft[:, 128 * dc:128 * dc + 128], qp[dc][:],
                         start=(dc == 0), stop=False),
                     waits=w if dc == rng.start else ())
            if half == 1:
                P.op("tensor",
                     lambda e, pss=pss: e.matmul(pss, ones_b[:], nbb[:],
                                                 start=False, stop=True),
                     inc=sT)

        key_dma(0)
        key_dma(1)
        trans_group(0, 0)
        trans_group(0, 1)
        for t in range(ktiles):
            blk, tb = t // 8, t % 8
            if t + 1 < ktiles:
                trans_group(t + 1, 0)
            scores_half(t, 0)
            if t + 1 < ktiles:
                trans_group(t + 1, 1)
            scores_half(t, 1)
            scores_done = sT.c

            ex_w = [(sT, scores_done)]
            if blk > 0:
                ex_w.append((sT, av_h0_t[blk - 1] if tb < 4
                             else av_done_t[blk - 1]))
            P.op("scalar",
                 lambda e, pss=psS[t % 2], tb=tb: e.activation(
                     pt[tb][:], pss, ACT.Exp),
                 waits=ex_w, inc=sS)
            exp_s[t] = sS.c
            if t >= ktiles - 8:
                P.op("vector",
                     lambda e, tb=tb, mi=t - (ktiles - 8): e.tensor_tensor(
                         pt[tb][:], pt[tb][:], mask[mi][:], ALU.mult),
                     waits=[(sS, exp_s[t]), (sK, mload)], inc=sV)

            def av_half(ablk, half):
                need_s = exp_s[8 * ablk + 4 * half + 3]
                need_v = sV.c
                for qt in range(4):
                    pz = psZ[qt % 2]
                    av_w = [(sS, need_s), (sV, need_v)]
                    prev = (ablk, 0, qt) if half == 1 else \
                        (ablk - 1, 1, qt)
                    if qt >= 2:
                        prev = (ablk, half, qt - 2)
                    if prev in ttadd_v:
                        av_w.append((sV, ttadd_v[prev]))
                    for tb2 in range(4 * half, 4 * half + 4):
                        r2 = (8 * ablk + tb2) % RING
                        for h in range(2):
                            P.op("tensor",
                                 lambda e, pz=pz, tb2=tb2, qt=qt, r2=r2, h=h,
                                 half=half:
                                 e.matmul(
                                     pz[:, 512 * h:512 * h + 512],
                                     pt[tb2][:, 128 * qt:128 * qt + 128],
                                     xk[r2][:, 512 * h:512 * h + 512],
                                     start=(tb2 == 4 * half),
                                     stop=(tb2 == 4 * half + 3)),
                                 waits=av_w if (tb2 == 4 * half and h == 0)
                                 else ())
                        is_last_l = (ablk == nblk - 1 and qt == 3
                                     and tb2 == 7)
                        P.op("tensor",
                             lambda e, tb2=tb2, qt=qt,
                             fl=(ablk == 0 and qt == 0 and tb2 == 0),
                             ll=is_last_l:
                             e.matmul(psL[:, qt:qt + 1],
                                      pt[tb2][:, 128 * qt:128 * qt + 128],
                                      ones_c[:], start=fl, stop=ll,
                                      skip_group_check=True),
                             inc=sT if tb2 == 4 * half + 3 else None)
                    av_mm = sT.c
                    if not zinit[qt]:
                        zw = [(sT, av_mm)]
                        if j == 0:
                            zw.append((sG, wvt_g))
                        P.op("vector",
                             lambda e, qt=qt, pz=pz: e.tensor_copy(
                                 zacc[qt][:], pz),
                             waits=zw, inc=sV)
                        zinit[qt] = True
                    else:
                        P.op("vector",
                             lambda e, qt=qt, pz=pz: e.tensor_tensor(
                                 zacc[qt][:], zacc[qt][:], pz, ALU.add),
                             waits=[(sT, av_mm)], inc=sV)
                    ttadd_v[(ablk, half, qt)] = sV.c
                if half == 0:
                    av_h0_t[ablk] = sT.c
                else:
                    av_done_t[ablk] = sT.c

            if tb == 5:
                av_half(blk, 0)
            elif tb == 1 and blk > 0:
                av_half(blk - 1, 1)
            if t == ktiles - 1:
                av_half(blk, 1)
            key_dma(t + 2)

        av_last_t = av_done_t[nblk - 1]
        av_last_v = sV.c

        P.op("vector", lambda e: e.tensor_copy(l_sb[:], psL),
             waits=[(sT, av_last_t)], inc=sV)
        P.op("vector", lambda e: e.reciprocal(rl_sb[:], l_sb[:]), inc=sV)
        rl_v = sV.c

        for qt in range(4):
            for g in range(2):
                for b in range(4):
                    dc = 4 * g + b
                    P.op("tensor",
                         lambda e, qt=qt, dc=dc, b=b: e.transpose(
                             psT[:, 128 * b:128 * b + 128],
                             zacc[qt][:, 128 * dc:128 * dc + 128], iden[:]),
                         waits=[(sV, sV.c), (sS, sS.c)] if b == 0 else (),
                         inc=sT if b == 3 else None)
                for b in range(4):
                    dc = 4 * g + b
                    P.op("scalar",
                         lambda e, qt=qt, dc=dc, b=b: e.copy(
                             xfq[dc][:, 128 * qt:128 * qt + 128],
                             psT[:, 128 * b:128 * b + 128]),
                         waits=[(sT, sT.c)], inc=sS if b == 3 else None)
        zf_s = sS.c

        for dc in range(DCH):
            P.dma("sync", xk[dc][:], wvt_d[128 * dc:128 * dc + 128, :], sK,
                  waits=[(sT, av_last_t), (sG, wvt_g)])
        wvt_k = sK.c

        vo_prev = {}
        for n in range(8):
            qt, h = n // 2, n % 2
            pv = psV[n % 2]
            w = [(sK, wvt_k), (sS, zf_s)]
            if n >= 2:
                w.append((sV, vo_prev[n - 2]))
            for dc in range(DCH):
                P.op("tensor",
                     lambda e, pv=pv, qt=qt, h=h, dc=dc: e.matmul(
                         pv, xfq[dc][:, 128 * qt:128 * qt + 128],
                         xk[dc][:, 512 * h:512 * h + 512],
                         start=(dc == 0), stop=(dc == DCH - 1)),
                     waits=w if dc == 0 else (),
                     inc=sT if dc == DCH - 1 else None)
            P.op("vector",
                 lambda e, pv=pv, qt=qt, h=h: e.scalar_tensor_tensor(
                     vout[qt][:, 512 * h:512 * h + 512], pv,
                     rl_sb[:, qt:qt + 1], qown[qt][:, 512 * h:512 * h + 512],
                     ALU.mult, ALU.add),
                 waits=[(sT, sT.c), (sV, rl_v), (sS, sS.c)], inc=sV)
            vo_prev[n] = sV.c
        prev_vo_v = sV.c
        zwv_t_prev = sT.c

        for qt in range(4):
            P.op("vector",
                 lambda e, qt=qt: e.reduce_sum(srow[:, qt:qt + 1],
                                               vout[qt][:], AX),
                 waits=[(sV, vo_prev[2 * qt + 1])], inc=sV)
            P.op("scalar",
                 lambda e, qt=qt: e.activation(xf[1][:], vout[qt][:],
                                               ACT.Square,
                                               accum_out=ssq[:, qt:qt + 1]),
                 waits=[(sV, vo_prev[2 * qt + 1])], inc=sS)
        P.op("vector",
             lambda e: e.tensor_scalar(negm[:], srow[:], -1.0 / D, None,
                                       ALU.mult),
             waits=[(sV, sV.c)], inc=sV)
        P.op("scalar", lambda e: e.activation(m2t[:], negm[:], ACT.Square),
             waits=[(sV, sV.c)], inc=sS)
        P.op("vector",
             lambda e: e.tensor_scalar(vvar[:], ssq[:], 1.0 / D, None,
                                       ALU.mult),
             waits=[(sS, sS.c)], inc=sV)
        P.op("vector", lambda e: e.tensor_copy(dbg1[:], vvar[:]), inc=sV)
        P.op("vector",
             lambda e: e.tensor_tensor(vvar[:], vvar[:], m2t[:], ALU.subtract),
             inc=sV)
        P.op("vector", lambda e: e.tensor_copy(dbg2[:], vvar[:]), inc=sV)
        P.op("scalar",
             lambda e: e.activation(sdt[:], vvar[:], ACT.Sqrt,
                                    bias=eps_c[:, 0:1]),
             waits=[(sV, sV.c)], inc=sS)
        P.op("vector", lambda e: e.reciprocal(rstd[:], sdt[:]),
             waits=[(sS, sS.c)], inc=sV)
        ln1_v = sV.c
        for qt in range(4):
            for hh in range(2):
                P.op("vector",
                     lambda e, qt=qt, hh=hh: e.tensor_scalar(
                         qp[2 * qt + hh][:],
                         vout[qt][:, 512 * hh:512 * hh + 512],
                         negm[:, qt:qt + 1],
                         rstd[:, qt:qt + 1], ALU.add, ALU.mult),
                     waits=[(sV, ln1_v)], inc=sV)
                P.op("vector",
                     lambda e, qt=qt, hh=hh: e.tensor_reduce(
                         tmax[:], qp[2 * qt + hh][:], AX, ALU.max,
                         apply_absolute_value=True),
                     inc=sV)
                P.op("vector",
                     lambda e: e.tensor_max(gmax[:], gmax[:], tmax[:]),
                     inc=sV)
                P.dma("gpsimd",
                      stash[CH * j + 128 * qt:CH * j + 128 * qt + 128,
                            512 * hh:512 * hh + 512],
                      qp[2 * qt + hh][:], sG, waits=[(sV, sV.c)])

    p2_v = sV.c
    stash_g = sG.c
    if DBG_P2:
        P.dma("gpsimd", gmr[:], gmax[:], sG, waits=[(sV, sV.c)])
        P.op("vector", lambda e: e.tensor_reduce(gms[:], gmr[:], AX, ALU.max),
             waits=[(sG, sG.c)], inc=sV)
        P.dma("gpsimd", cc2_in, gms[:], sG, waits=[(sV, sV.c)])
        P.op("gpsimd",
             lambda e: e.collective_compute("AllReduce", ALU.max,
                                            replica_groups=[list(range(NCORES))],
                                            ins=[cc2_in], outs=[cc2_out]),
             waits=[(sG, sG.c)], inc=sC)
        P.dma("gpsimd", xsr[:], cc2_out.partition_broadcast(128), sG,
              waits=[(sC, sC.c)])
        P.op("vector",
             lambda e: e.tensor_scalar(xs_c[:], xsr[:], 1.0 / 127.0, 1e-5,
                                       ALU.mult, ALU.max),
             waits=[(sG, sG.c)], inc=sV)
        P.op("vector", lambda e: e.reciprocal(rxs_c[:], xs_c[:]), inc=sV)
        P.op("vector",
             lambda e: e.tensor_scalar(sc2_c[:], wsc_c[:], xs_c[:, 0:1], None,
                                       ALU.mult),
             inc=sV)
        p2_v = sV.c
        stash_g = sG.c

    for j in range(DBG_P3):
        for qt in range(4):
            P.dma("sync", qown[qt][:],
                  stash[CH * j + 128 * qt:CH * j + 128 * qt + 128, :], sK,
                  waits=[(sG, stash_g), (sV, sV.c), (sT, sT.c)])
        uload = sK.c

        for qt in range(4):
            P.op("vector",
                 lambda e, qt=qt: e.tensor_scalar(
                     xf[0][:], qown[qt][:], rxs_c[:, 0:1], C2,
                     ALU.mult, ALU.add),
                 waits=[(sK, uload), (sV, p2_v), (sS, sS.c), (sT, sT.c)],
                 inc=sV)
            P.op("vector",
                 lambda e: e.tensor_scalar(xf[1][:], xf[0][:], -C2, None,
                                           ALU.add),
                 inc=sV)
            for hh in range(2):
                P.op("vector",
                     lambda e, qt=qt, hh=hh: e.tensor_scalar(
                         mask[2 * qt + hh][:],
                         xf[1][:, 512 * hh:512 * hh + 512],
                         127.0, -128.0, ALU.min, ALU.max),
                     inc=sV)
        xq_v = sV.c

        for qt in range(4):
            for hh in range(2):
                src = mask[2 * qt + hh]
                for b in range(4):
                    dc = 4 * hh + b
                    P.op("tensor",
                         lambda e, src=src, b=b, dc=dc: e.transpose(
                             psTball[:, 128 * dc:128 * dc + 128],
                             src[:, 128 * b:128 * b + 128], idb[:]),
                         waits=[(sV, sV.c), (sS, sS.c)]
                         if (hh == 0 and b == 0) else (),
                         inc=sT if b == 3 else None)
            P.op("vector",
                 lambda e, qt=qt: e.tensor_copy(
                     xqt.rearrange("p (dc q) -> p dc q", dc=8)[
                         :, :, 128 * qt:128 * qt + 128],
                     psTball[:, 0:1024].rearrange("p (dc q) -> p dc q", dc=8)),
                 waits=[(sT, sT.c)], inc=sV)
        xqt_v = sV.c

        ffn_evac = {}
        for nt in range(DCH):
            pf = psF[nt % 2]
            w = [(sV, xqt_v)]
            if nt >= 2:
                w = [(sV, ffn_evac[nt - 2])]
            for dc in range(DCH):
                P.op("tensor",
                     lambda e, pf=pf, nt=nt, dc=dc: e.matmul(
                         pf, wt[dc][:, 128 * nt:128 * nt + 128],
                         xqt[:, 512 * dc:512 * dc + 512],
                         start=(dc == 0), stop=(dc == DCH - 1)),
                     waits=w if dc == 0 else (),
                     inc=sT if dc == DCH - 1 else None)
            P.op("vector",
                 lambda e, pf=pf, nt=nt: e.tensor_scalar(
                     qp[nt][:], pf, sc2_c[:, nt:nt + 1], wbs_c[:, nt:nt + 1],
                     ALU.mult, ALU.add),
                 waits=[(sT, sT.c)], inc=sV)
            ffn_evac[nt] = sV.c

        for qt in range(4):
            for g in range(2):
                for b in range(4):
                    nt = 4 * g + b
                    P.op("tensor",
                         lambda e, qt=qt, nt=nt, b=b: e.transpose(
                             psT[:, 128 * b:128 * b + 128],
                             qp[nt][:, 128 * qt:128 * qt + 128], iden[:]),
                         waits=[(sV, sV.c), (sS, sS.c)] if b == 0 else (),
                         inc=sT if b == 3 else None)
                P.op("vector",
                     lambda e, qt=qt, g=g: e.tensor_tensor(
                         vout[qt][:, 512 * g:512 * g + 512], psT[:, 0:512],
                         qown[qt][:, 512 * g:512 * g + 512], ALU.add),
                     waits=[(sT, sT.c), (sG, sG.c)], inc=sV)
        s2_v = sV.c

        for qt in range(4):
            P.op("vector",
                 lambda e, qt=qt: e.reduce_sum(srow[:, qt:qt + 1],
                                               vout[qt][:], AX),
                 waits=[(sV, s2_v)], inc=sV)
            P.op("scalar",
                 lambda e, qt=qt: e.activation(xf[1][:], vout[qt][:],
                                               ACT.Square,
                                               accum_out=ssq[:, qt:qt + 1]),
                 waits=[(sV, s2_v)], inc=sS)
        P.op("vector",
             lambda e: e.tensor_scalar(negm[:], srow[:], -1.0 / D, None,
                                       ALU.mult),
             waits=[(sV, sV.c)], inc=sV)
        P.op("scalar", lambda e: e.activation(m2t[:], negm[:], ACT.Square),
             waits=[(sV, sV.c)], inc=sS)
        P.op("vector",
             lambda e: e.tensor_scalar(vvar[:], ssq[:], 1.0 / D, None,
                                       ALU.mult),
             waits=[(sS, sS.c)], inc=sV)
        P.op("vector",
             lambda e: e.tensor_tensor(vvar[:], vvar[:], m2t[:], ALU.subtract),
             inc=sV)
        P.op("scalar",
             lambda e: e.activation(sdt[:], vvar[:], ACT.Sqrt,
                                    bias=eps_c[:, 0:1]),
             waits=[(sV, sV.c)], inc=sS)
        P.op("vector", lambda e: e.reciprocal(rstd[:], sdt[:]),
             waits=[(sS, sS.c)], inc=sV)
        ln2_v = sV.c
        for qt in range(4):
            P.op("vector",
                 lambda e, qt=qt: e.tensor_scalar(
                     zacc[qt][:], vout[qt][:], negm[:, qt:qt + 1],
                     rstd[:, qt:qt + 1], ALU.add, ALU.mult),
                 waits=[(sV, ln2_v), (sG, sG.c)], inc=sV)
            P.dma("gpsimd", out_d[CH * j + 128 * qt:CH * j + 128 * qt + 128, :],
                  zacc[qt][:], sG, waits=[(sV, sV.c)])

    if DBG_TAP:
        fw = [(sT, sT.c), (sV, sV.c), (sS, sS.c), (sK, sK.c)]
        def tap(row, ap, cols=D):
            P.dma("gpsimd", out_d[row:row + ap.shape[0], 0:cols], ap, sG,
                  waits=fw, slow=True)
        if DBG_TAP == "msb":
            for i in range(DCH):
                tap(128 * i, msb[i][:])
        elif DBG_TAP == "qp":
            for i in range(DCH):
                tap(128 * i, qp[i][:], 512)
        elif DBG_TAP == "xfq":
            for i in range(DCH):
                tap(128 * i, xfq[i][:], 512)
        elif DBG_TAP == "zacc":
            for i in range(4):
                tap(128 * i, zacc[i][:])
        elif DBG_TAP == "qown":
            for i in range(4):
                tap(128 * i, qown[i][:])
        elif DBG_TAP == "pt":
            for i in range(8):
                tap(128 * i, pt[i][:], 512)
        elif DBG_TAP == "wt":
            for i in range(DCH):
                P.dma("gpsimd", out_d[128 * i:128 * i + 128, :], wt[i][:],
                      sG, waits=fw)
        elif DBG_TAP == "ln1":
            for i in range(4):
                tap(128 * i, zacc[i][:])
            tap(3000, srow[:], 4)
            tap(3200, ssq[:], 4)
            tap(3400, negm[:], 4)
            tap(3600, rstd[:], 4)
            tap(3800, l_sb[:], 4)
            tap(2000, vvar[:], 4)
            tap(2200, sdt[:], 4)
            tap(2400, m2t[:], 4)
            tap(2600, dbg1[:], 4)
            tap(2800, dbg2[:], 4)
        elif DBG_TAP == "scalars":
            tap(0, kn2[:], 1)
            tap(1, nbrow[:], 512)
            tap(128, l_sb[:], 4)
            tap(256, rl_sb[:], 4)
            tap(384, rstd[:], 4)
            tap(512, negm[:], 4)
            tap(640, xs_c[:], 1)
            tap(768, srow[:], 4)
            tap(896, ssq[:], 4)

    P.op("gpsimd", None, waits=[(sG, sG.c)])

    with nc.Block() as block:
        P.realize(block)
    stack.close()
    return nc


_NC_CACHE = None


def _get_nc():
    global _NC_CACHE
    if _NC_CACHE is None:
        _NC_CACHE = build_nc()
    return _NC_CACHE


def unit(c, j):
    p = (c + j) % 2
    return ((c // 2 + j) % 4, 2 * j + p)


def make_in_maps(x, wq, wk, wv, wi8, wsc, wbs):
    wsc_c = np.ascontiguousarray(wsc.reshape(DCH, 128).T)
    wbs_c = np.ascontiguousarray(wbs.reshape(DCH, 128).T)
    iden = np.eye(128, dtype=np.float32)
    idb = np.eye(128, dtype=ml_dtypes.bfloat16)
    in_maps = []
    for c in range(NCORES):
        im = {"wq": wq, "wk": wk, "wv": wv, "wi8": wi8,
              "wsc": wsc_c, "wbs": wbs_c, "iden": iden, "idb": idb}
        for j in range(NSLOT):
            b, ch = unit(c, j)
            im[f"q{j}"] = x[b, CH * ch:CH * ch + CH, :]
            keys = np.arange(1024 * j, 1024 * (j + 1))
            qs = CH * ch + np.arange(CH)
            im[f"m{j}"] = (keys[:, None] <= qs[None, :]).astype(
                ml_dtypes.bfloat16)
            if j >= 4:
                im[f"k{j}"] = x[b, :1024 * (j + 1), :]
        in_maps.append(im)
    return in_maps


def kernel(L0_attn_input_x, L0_attn_Wq, L0_attn_Wk, L0_attn_Wv,
           L0_ffn_weight_int8, L0_ffn_weight_scale, L0_ffn_bias):
    x = np.ascontiguousarray(np.asarray(L0_attn_input_x, dtype=np.float32))
    wq = np.ascontiguousarray(np.asarray(L0_attn_Wq, dtype=np.float32))
    wk = np.ascontiguousarray(np.asarray(L0_attn_Wk, dtype=np.float32))
    wv = np.ascontiguousarray(np.asarray(L0_attn_Wv, dtype=np.float32))
    wi8 = np.ascontiguousarray(np.asarray(L0_ffn_weight_int8, dtype=np.int8))
    wsc = np.asarray(L0_ffn_weight_scale, dtype=np.float32).reshape(D)
    wbs = np.asarray(L0_ffn_bias, dtype=np.float32).reshape(D)

    in_maps = make_in_maps(x, wq, wk, wv, wi8, wsc, wbs)
    nc = _get_nc()
    res = run_bass_kernel_spmd(nc, in_maps, core_ids=list(range(NCORES)))

    full = np.empty((B_, T_, D), dtype=np.float32)
    for c in range(NCORES):
        o = np.asarray(res.results[c]["out"])
        for j in range(NSLOT):
            b, ch = unit(c, j)
            full[b, CH * ch:CH * ch + CH, :] = o[CH * j:CH * j + CH, :]
    return full


if __name__ == "__main__":
    nc = build_nc()
    print("built ok; instructions:", nc.n_instructions())


# version 08
# speedup vs baseline: 1.0383x; 1.0101x over previous
import numpy as np
import ml_dtypes
import concourse.bass as bass
import concourse.mybir as mybir
from concourse.bass_utils import run_bass_kernel_spmd
from contextlib import ExitStack

F32 = mybir.dt.float32
BF16 = mybir.dt.bfloat16
I8 = mybir.dt.int8
ALU = mybir.AluOpType
ACT = mybir.ActivationFunctionType
AX = mybir.AxisListType.X

NCORES = 8
B_, T_, D = 4, 8192, 1024
CH = 512
NSLOT = 8
DCH = D // 128
RING = 9
C2 = 1.5 * 2.0**23


class Sem:
    def __init__(self, stack, nc, name):
        self.h = stack.enter_context(nc.semaphore(name))
        self.c = 0


class Prog:
    ENGS = ("sync", "tensor", "vector", "scalar", "gpsimd")

    def __init__(self):
        self.items = {e: [] for e in self.ENGS}

    def op(self, eng, fn, waits=(), inc=None, inc_n=1):
        incs = []
        if inc is not None:
            inc.c += inc_n
            incs.append((inc, inc_n))
        self.items[eng].append((tuple(waits), fn, incs))

    def dma(self, eng, out, in_, sem, waits=(), slow=False):
        self.op(eng, lambda e, o=out, i=in_: e.dma_start(
            out=o, in_=i, allow_slow_non_contiguous=slow),
                waits=waits, inc=sem, inc_n=16)

    def realize(self, block):
        def mk(eng_name):
            def body(e):
                waited = {}
                for waits, fn, incs in self.items[eng_name]:
                    for s, v in waits:
                        if v > 0 and waited.get(id(s), -1) < v:
                            e.wait_ge(s.h, v)
                            waited[id(s)] = v
                    if fn is None:
                        continue
                    ins = fn(e)
                    if incs and eng_name in ("vector", "scalar"):
                        ins = e.drain(fusable=True)
                    for s, n in incs:
                        ins.then_inc(s.h, n)
            return body
        block.sync(mk("sync"))
        block.tensor(mk("tensor"))
        block.vector(mk("vector"))
        block.scalar(mk("scalar"))
        block.gpsimd(mk("gpsimd"))


DBG_P0 = True
DBG_SLOTS = NSLOT
DBG_KEYS = True
DBG_P2 = True
DBG_P3 = NSLOT
DBG_STAGE = 9
DBG_PROJ = 3
DBG_TAP = None


def build_nc():
    nc = bass.Bass("TRN2", target_bir_lowering=False)

    k_in = {j: nc.dram_tensor(f"k{j}", [1024 * (j + 1), D], F32,
                              kind="ExternalInput").ap() for j in range(4, 8)}
    q_in = {j: nc.dram_tensor(f"q{j}", [CH, D], F32, kind="ExternalInput").ap()
            for j in range(8)}
    m_in = {j: nc.dram_tensor(f"m{j}", [1024, CH], BF16, kind="ExternalInput").ap()
            for j in range(8)}
    wq_in = nc.dram_tensor("wq", [D, D], F32, kind="ExternalInput").ap()
    wk_in = nc.dram_tensor("wk", [D, D], F32, kind="ExternalInput").ap()
    wv_in = nc.dram_tensor("wv", [D, D], F32, kind="ExternalInput").ap()
    wi8_in = nc.dram_tensor("wi8", [D, D], I8, kind="ExternalInput").ap()
    wsc_in = nc.dram_tensor("wsc", [128, DCH], F32, kind="ExternalInput").ap()
    wbs_in = nc.dram_tensor("wbs", [128, DCH], F32, kind="ExternalInput").ap()
    iden_in = nc.dram_tensor("iden", [128, 128], F32, kind="ExternalInput").ap()
    idb_in = nc.dram_tensor("idb", [128, 128], BF16, kind="ExternalInput").ap()
    out_d = nc.dram_tensor("out", [NSLOT * CH, D], F32, kind="ExternalOutput").ap()

    stash = nc.dram_tensor("stash", [NSLOT * CH, D], F32).ap()
    wvt_d = nc.dram_tensor("wvt", [D, D], F32).ap()
    cc1_in = nc.dram_tensor("cc1_in", [1, 1], F32).ap()
    cc1_out = nc.dram_tensor("cc1_out", [1, 1], F32).ap()
    cc2_in = nc.dram_tensor("cc2_in", [1, 1], F32).ap()
    cc2_out = nc.dram_tensor("cc2_out", [1, 1], F32).ap()

    stack = ExitStack()
    sb = lambda name, shape, dt: stack.enter_context(
        nc.sbuf_tensor(name, shape, dt))

    msb = [sb(f"msb{i}", [128, D], F32) for i in range(DCH)]
    wt = [sb(f"wt{i}", [128, D], BF16) for i in range(DCH)]
    mask = [sb(f"mask{i}", [128, CH], BF16) for i in range(8)]
    qown = [sb(f"qown{i}", [128, D], F32) for i in range(4)]
    xfq = [sb(f"xfq{i}", [128, CH], F32) for i in range(DCH)]
    qp = [sb(f"qp{i}", [128, CH], F32) for i in range(DCH)]
    xk = [sb(f"xk{i}", [128, D], F32) for i in range(RING)]
    xf = [sb(f"xf{i}", [128, D], F32) for i in range(2)]
    pt = [sb(f"pt{i}", [128, CH], F32) for i in range(8)]
    zacc = [sb(f"zacc{i}", [128, D], F32) for i in range(4)]
    vout = zacc
    xqt = sb("xqt", [128, 4 * D], BF16)

    iden = sb("iden_s", [128, 128], F32)
    idb = sb("idb_s", [128, 128], BF16)
    ones_c = sb("ones_c", [128, 1], F32)
    ones_b = sb("ones_b", [1, 128], BF16)
    eps_c = sb("eps_c", [128, 1], F32)
    wsc_c = sb("wsc_c", [128, DCH], F32)
    wbs_c = sb("wbs_c", [128, DCH], F32)
    sc2_c = sb("sc2_c", [128, DCH], F32)
    kn2 = sb("kn2", [1, 1], F32)
    knl = sb("knl", [128, 1], F32)
    knr = sb("knr", [1, 128], F32)
    kns = sb("kns", [1, 1], F32)
    gmax = sb("gmax", [128, 1], F32)
    gmr = sb("gmr", [1, 128], F32)
    gms = sb("gms", [1, 1], F32)
    xsr = sb("xsr", [128, 1], F32)
    xs_c = sb("xs_c", [128, 1], F32)
    rxs_c = sb("rxs_c", [128, 1], F32)
    l_sb = sb("l_sb", [128, 4], F32)
    rl_sb = sb("rl_sb", [128, 4], F32)
    srow = sb("srow", [128, 4], F32)
    ssq = sb("ssq", [128, 4], F32)
    negm = sb("negm", [128, 4], F32)
    vvar = sb("vvar", [128, 4], F32)
    m2t = sb("m2t", [128, 4], F32)
    sdt = sb("sdt", [128, 4], F32)
    rstd = sb("rstd", [128, 4], F32)
    tmax = sb("tmax", [128, 1], F32)
    dbg1 = sb("dbg1", [128, 4], F32)
    dbg2 = sb("dbg2", [128, 4], F32)
    nbrow = sb("nbrow", [1, CH], F32)
    nbb = sb("nbb", [1, CH], BF16)

    psAll = stack.enter_context(nc.psum_tensor("psAll", [128, 4096], F32))

    def bank(b, w=512):
        return psAll[:, 512 * b:512 * b + w]
    psZ = [psAll[:, 0:1024], psAll[:, 1024:2048]]
    psS = [bank(4), bank(5)]
    psQ = [bank(4), bank(5)]
    psL = bank(6)[:, 0:4]
    psB = bank(6)[0:1, 0:512]
    psT = bank(7)
    psTball = bank(7).bitcast(BF16)
    psF = [bank(0), bank(1)]
    psV = [bank(2), bank(3)]

    sK = Sem(stack, nc, "sK")
    sG = Sem(stack, nc, "sG")
    sT = Sem(stack, nc, "sT")
    sV = Sem(stack, nc, "sV")
    sS = Sem(stack, nc, "sS")
    sC = Sem(stack, nc, "sC")

    P = Prog()

    def tgroup(blocks, bankap, evac_eng, evac_fn, w_extra):
        for i, (src_ap,) in enumerate(blocks):
            P.op("tensor",
                 lambda e, src_ap=src_ap, i=i: e.transpose(
                     bankap[:, 128 * i:128 * i + 128], src_ap, iden[:]),
                 waits=w_extra if i == 0 else (),
                 inc=sT if i == len(blocks) - 1 else None)

    wload = kn_g = wt_s = wvt_g = mprime_t = mprime_v = wi8_k = 0
    P.dma("sync", iden[:], iden_in, sK)
    P.dma("sync", idb[:], idb_in, sK)
    P.dma("sync", wsc_c[:], wsc_in, sK)
    P.dma("sync", wbs_c[:], wbs_in, sK)
    P.op("vector", lambda e: e.memset(ones_c[:], 1.0))
    P.op("vector", lambda e: e.memset(ones_b[:], 1.0))
    P.op("vector", lambda e: e.memset(eps_c[:], 1e-5))
    P.op("vector", lambda e: e.memset(gmax[:], 0.0))
    P.op("vector", lambda e: e.memset(knl[:], 0.0), inc=sV)

    wq_t = xk[:8]
    wk_t = qown + zacc
    for o in range(DCH) if DBG_P0 else []:
        P.dma("sync", wq_t[o][:], wq_in[128 * o:128 * o + 128, :], sK)
        P.dma("sync", wk_t[o][:], wk_in[128 * o:128 * o + 128, :], sK)
    wload = sK.c

    if DBG_P0:
        evac_v = {}
        for n in range(16):
            i, h = n // 2, n % 2
            pq = psQ[n % 2]
            w = [(sK, wload)]
            if n >= 2:
                w.append((sV, evac_v[n - 2]))
            for o in range(DCH):
                P.op("tensor",
                     lambda e, pq=pq, o=o, i=i, h=h: e.matmul(
                         pq, wq_t[o][:, 128 * i:128 * i + 128],
                         wk_t[o][:, 512 * h:512 * h + 512],
                         start=(o == 0), stop=(o == DCH - 1)),
                     waits=w if o == 0 else (),
                     inc=sT if o == DCH - 1 else None)
            P.op("vector",
                 lambda e, pq=pq, i=i, h=h: e.tensor_scalar(
                     msb[i][:, 512 * h:512 * h + 512], pq, 0.03125, None, ALU.mult),
                 waits=[(sT, sT.c)], inc=sV)
            evac_v[n] = sV.c
        mprime_v = sV.c
        mprime_t = sT.c

        for j in range(NSLOT):
            for qt in range(4):
                P.dma("sync", xf[1][:], q_in[j][128 * qt:128 * qt + 128, :], sK,
                      waits=[(sS, sS.c)])
                P.op("scalar",
                     lambda e: e.activation(xf[0][:], xf[1][:], ACT.Square,
                                            accum_out=tmax[:]),
                     waits=[(sK, sK.c), (sV, sV.c)], inc=sS)
                P.op("vector", lambda e: e.tensor_max(knl[:], knl[:], tmax[:]),
                     waits=[(sS, sS.c)], inc=sV)
        P.dma("gpsimd", knr[:], knl[:], sG, waits=[(sV, sV.c)])
        P.op("vector", lambda e: e.tensor_reduce(kns[:], knr[:], AX, ALU.max),
             waits=[(sG, sG.c)], inc=sV)
        P.dma("gpsimd", cc1_in, kns[:], sG, waits=[(sV, sV.c)])
        P.op("gpsimd",
             lambda e: e.collective_compute("AllReduce", ALU.max,
                                            replica_groups=[list(range(NCORES))],
                                            ins=[cc1_in], outs=[cc1_out]),
             waits=[(sG, sG.c)], inc=sC)
        P.dma("gpsimd", kn2[:], cc1_out, sG, waits=[(sC, sC.c)])
        kn_g = sG.c

        xf0b = xf[0].bitcast(BF16)[:, 0:D]
        for nt in range(DCH):
            P.dma("sync", mask[nt][:].bitcast(I8)[:, 0:D],
                  wi8_in[128 * nt:128 * nt + 128, :], sK)
        wi8_k = sK.c
        for nt in range(DCH):
            P.op("vector",
                 lambda e, nt=nt: e.tensor_copy(xf0b, mask[nt][:].bitcast(I8)[:, 0:D]),
                 waits=[(sK, wi8_k), (sS, sS.c), (sT, sT.c)], inc=sV)
            for dc in range(DCH):
                P.op("tensor",
                     lambda e, dc=dc: e.transpose(
                         psTball[:, 128 * dc:128 * dc + 128],
                         xf0b[:, 128 * dc:128 * dc + 128], idb[:]),
                     waits=[(sV, sV.c), (sS, sS.c)] if dc == 0 else (),
                     inc=sT if dc == DCH - 1 else None)
            for dc in range(DCH):
                P.op("scalar",
                     lambda e, nt=nt, dc=dc: e.copy(
                         wt[dc][:, 128 * nt:128 * nt + 128],
                         psTball[:, 128 * dc:128 * dc + 128]),
                     waits=[(sT, sT.c)], inc=sS if dc == DCH - 1 else None)
        wt_s = sS.c

        wv_asm = qown + zacc
        for nt in range(DCH):
            P.dma("sync", xf[1][:], wv_in[128 * nt:128 * nt + 128, :], sK,
                  waits=[(sS, sS.c)])
            ldk = sK.c
            for g in range(2):
                for b in range(4):
                    dc = 4 * g + b
                    P.op("tensor",
                         lambda e, dc=dc, b=b: e.transpose(
                             psT[:, 128 * b:128 * b + 128],
                             xf[1][:, 128 * dc:128 * dc + 128], iden[:]),
                         waits=[(sK, ldk), (sS, sS.c)] if b == 0 else (),
                         inc=sT if b == 3 else None)
                for b in range(4):
                    dc = 4 * g + b
                    P.op("scalar",
                         lambda e, nt=nt, dc=dc, b=b: e.copy(
                             wv_asm[dc][:, 128 * nt:128 * nt + 128],
                             psT[:, 128 * b:128 * b + 128]),
                         waits=[(sT, sT.c)], inc=sS if b == 3 else None)
        for dc in range(DCH):
            P.dma("gpsimd", wvt_d[128 * dc:128 * dc + 128, :], wv_asm[dc][:], sG,
                  waits=[(sS, sS.c)])
        wvt_g = sG.c

    zwv_t_prev = 0
    prev_vo_v = 0
    for j in range(DBG_SLOTS):
        ktiles = 8 * (j + 1)
        nblk = ktiles // 8
        kj = k_in[j + 4] if j < 4 else k_in[j]

        for t8 in range(8):
            P.dma("sync", mask[t8][:], m_in[j][128 * t8:128 * t8 + 128, :], sK,
                  waits=[(sS, wt_s), (sV, sV.c)])
        mload = sK.c
        for qt in range(4):
            P.dma("sync", qown[qt][:], q_in[j][128 * qt:128 * qt + 128, :], sK,
                  waits=[(sV, prev_vo_v)] + ([(sG, wvt_g)] if j == 0 else []))
        qload = sK.c
        if DBG_STAGE < 1:
            continue

        evs = {}
        for gi in range(8):
            qt, g = gi // 2, gi % 2
            pb = psT if gi % 2 == 0 else bank(0)
            w = [(sK, qload), (sS, evs.get(gi - 2, sS.c if gi == 0 else 0))]
            for b in range(4):
                dc = 4 * g + b
                P.op("tensor",
                     lambda e, qt=qt, dc=dc, b=b, pb=pb: e.transpose(
                         pb[:, 128 * b:128 * b + 128],
                         qown[qt][:, 128 * dc:128 * dc + 128], iden[:]),
                     waits=w if b == 0 else (),
                     inc=sT if b == 3 else None)
            for b in range(4):
                dc = 4 * g + b
                P.op("scalar",
                     lambda e, qt=qt, dc=dc, b=b, pb=pb: e.copy(
                         xfq[dc][:, 128 * qt:128 * qt + 128],
                         pb[:, 128 * b:128 * b + 128]),
                     waits=[(sT, sT.c)], inc=sS if b == 3 else None)
            evs[gi] = sS.c
        xfq_s = sS.c
        if DBG_STAGE < 2:
            continue

        qp_evac = {}
        for i in range(DCH):
            pq = psQ[i % 2]
            w = [(sS, xfq_s)]
            if i >= 2:
                w = [(sV, qp_evac[i - 2][0]), (sS, qp_evac[i - 2][1])]
            for dc in range(DCH):
                P.op("tensor",
                     lambda e, pq=pq, i=i, dc=dc: e.matmul(
                         pq, msb[dc][:, 128 * i:128 * i + 128], xfq[dc][:],
                         start=(dc == 0), stop=(dc == DCH - 1)),
                     waits=w if dc == 0 else (),
                     inc=sT if dc == DCH - 1 else None)
            mmq = sT.c
            P.op("vector",
                 lambda e, pq=pq, i=i: e.tensor_copy(qp[i][:], pq),
                 waits=[(sT, mmq), (sG, sG.c)], inc=sV)
            cp_v = sV.c
            if DBG_PROJ >= 2:
                P.op("scalar",
                     lambda e, i=i: e.activation(xf[0][:, 0:512], qp[i][:],
                                                 ACT.Square),
                     waits=[(sV, cp_v)], inc=sS)
            sq_s = sS.c
            qp_evac[i] = (cp_v, sq_s)
            if DBG_PROJ >= 3:
                P.op("tensor",
                     lambda e, i=i: e.matmul(psB, ones_c[:], xf[0][:, 0:512],
                                             start=(i == 0),
                                             stop=(i == DCH - 1)),
                     waits=[(sS, sq_s)] + ([(sV, sV.c)] if i == 0 else []),
                     inc=sT if i == DCH - 1 else None)
        if DBG_STAGE < 3:
            continue
        P.op("scalar",
             lambda e: e.activation(nbrow[:], psB, ACT.Sqrt,
                                    scale=kn2[0:1, 0:1]),
             waits=[(sT, sT.c), (sG, kn_g)], inc=sS)
        P.op("vector",
             lambda e: e.tensor_scalar(nbb[:], nbrow[:], -1.0, None, ALU.mult),
             waits=[(sS, sS.c)], inc=sV)
        negb_v = sV.c

        if not DBG_KEYS:
            continue
        zinit = [False] * 4
        av_done_t = {}
        av_h0_t = {}
        ttadd_v = {}
        exp_s = {}
        first_l = True
        kdma_t = {}
        evac_g0_s = {}
        evac_g1_v = {}

        def key_dma(t):
            if t >= ktiles:
                return
            r = t % RING
            w = []
            if t >= RING:
                told = t - RING
                w.append((sT, av_h0_t[told // 8] if told % 8 < 4
                          else av_done_t[told // 8]))
            else:
                if j == 0:
                    w.append((sT, mprime_t))
                else:
                    w.append((sT, zwv_t_prev))
            P.dma("sync", xk[r][:], kj[128 * t:128 * t + 128, :], sK, waits=w)
            kdma_t[t] = sK.c

        def trans_group(t, g):
            r = t % RING
            xft = xf[t % 2]
            if g == 0:
                w = [(sK, kdma_t[t]), (sS, evac_g0_s.get(t - 1, 0)),
                     (sV, evac_g1_v.get(t - 1, 0))]
            else:
                w = [(sS, evac_g0_s[t])]
            for b in range(4):
                dc = 4 * g + b
                P.op("tensor",
                     lambda e, r=r, dc=dc, b=b: e.transpose(
                         psT[:, 128 * b:128 * b + 128],
                         xk[r][:, 128 * dc:128 * dc + 128], iden[:]),
                     waits=w if b == 0 else (),
                     inc=sT if b == 3 else None)
            if g == 0:
                P.op("scalar",
                     lambda e, xft=xft: e.copy(xft[:, 0:512], psT[:, 0:512]),
                     waits=[(sT, sT.c)], inc=sS)
                evac_g0_s[t] = sS.c
            else:
                P.op("vector",
                     lambda e, xft=xft: e.tensor_copy(xft[:, 512:1024],
                                                      psT[:, 0:512]),
                     waits=[(sT, sT.c)], inc=sV)
                evac_g1_v[t] = sV.c

        def scores_half(t, half):
            pss = psS[t % 2]
            xft = xf[t % 2]
            rng = range(0, 4) if half == 0 else range(4, 8)
            if half == 0:
                w = [(sS, evac_g0_s[t]), (sV, max(evac_g1_v[t], negb_v))]
                if t >= 2:
                    w.append((sS, exp_s[t - 2]))
            else:
                w = []
            for dc in rng:
                P.op("tensor",
                     lambda e, pss=pss, xft=xft, dc=dc: e.matmul(
                         pss, xft[:, 128 * dc:128 * dc + 128], qp[dc][:],
                         start=(dc == 0), stop=False),
                     waits=w if dc == rng.start else ())
            if half == 1:
                P.op("tensor",
                     lambda e, pss=pss: e.matmul(pss, ones_b[:], nbb[:],
                                                 start=False, stop=True),
                     inc=sT)

        key_dma(0)
        key_dma(1)
        trans_group(0, 0)
        trans_group(0, 1)
        for t in range(ktiles):
            blk, tb = t // 8, t % 8
            if t + 1 < ktiles:
                trans_group(t + 1, 0)
            scores_half(t, 0)
            if t + 1 < ktiles:
                trans_group(t + 1, 1)
            scores_half(t, 1)
            scores_done = sT.c

            ex_w = [(sT, scores_done)]
            if blk > 0:
                ex_w.append((sT, av_h0_t[blk - 1] if tb < 4
                             else av_done_t[blk - 1]))
            P.op("scalar",
                 lambda e, pss=psS[t % 2], tb=tb: e.activation(
                     pt[tb][:], pss, ACT.Exp),
                 waits=ex_w, inc=sS)
            exp_s[t] = sS.c
            if t >= ktiles - 8:
                P.op("vector",
                     lambda e, tb=tb, mi=t - (ktiles - 8): e.tensor_tensor(
                         pt[tb][:], pt[tb][:], mask[mi][:], ALU.mult),
                     waits=[(sS, exp_s[t]), (sK, mload)], inc=sV)

            def av_half(ablk, half):
                need_s = exp_s[8 * ablk + 4 * half + 3]
                need_v = sV.c
                for qt in range(4):
                    pz = psZ[qt % 2]
                    av_w = [(sS, need_s), (sV, need_v)]
                    prev = (ablk, 0, qt) if half == 1 else \
                        (ablk - 1, 1, qt)
                    if qt >= 2:
                        prev = (ablk, half, qt - 2)
                    if prev in ttadd_v:
                        av_w.append((sV, ttadd_v[prev]))
                    for tb2 in range(4 * half, 4 * half + 4):
                        r2 = (8 * ablk + tb2) % RING
                        for h in range(2):
                            P.op("tensor",
                                 lambda e, pz=pz, tb2=tb2, qt=qt, r2=r2, h=h,
                                 half=half:
                                 e.matmul(
                                     pz[:, 512 * h:512 * h + 512],
                                     pt[tb2][:, 128 * qt:128 * qt + 128],
                                     xk[r2][:, 512 * h:512 * h + 512],
                                     start=(tb2 == 4 * half),
                                     stop=(tb2 == 4 * half + 3)),
                                 waits=av_w if (tb2 == 4 * half and h == 0)
                                 else ())
                        is_last_l = (ablk == nblk - 1 and qt == 3
                                     and tb2 == 7)
                        P.op("tensor",
                             lambda e, tb2=tb2, qt=qt,
                             fl=(ablk == 0 and qt == 0 and tb2 == 0),
                             ll=is_last_l:
                             e.matmul(psL[:, qt:qt + 1],
                                      pt[tb2][:, 128 * qt:128 * qt + 128],
                                      ones_c[:], start=fl, stop=ll,
                                      skip_group_check=True),
                             inc=sT if tb2 == 4 * half + 3 else None)
                    av_mm = sT.c
                    if not zinit[qt]:
                        zw = [(sT, av_mm)]
                        if j == 0:
                            zw.append((sG, wvt_g))
                        P.op("vector",
                             lambda e, qt=qt, pz=pz: e.tensor_copy(
                                 zacc[qt][:], pz),
                             waits=zw, inc=sV)
                        zinit[qt] = True
                    else:
                        P.op("vector",
                             lambda e, qt=qt, pz=pz: e.tensor_tensor(
                                 zacc[qt][:], zacc[qt][:], pz, ALU.add),
                             waits=[(sT, av_mm)], inc=sV)
                    ttadd_v[(ablk, half, qt)] = sV.c
                if half == 0:
                    av_h0_t[ablk] = sT.c
                else:
                    av_done_t[ablk] = sT.c

            if tb == 5:
                av_half(blk, 0)
            elif tb == 1 and blk > 0:
                av_half(blk - 1, 1)
            if t == ktiles - 1:
                av_half(blk, 1)
            key_dma(t + 2)

        av_last_t = av_done_t[nblk - 1]
        av_last_v = sV.c

        P.op("vector", lambda e: e.tensor_copy(l_sb[:], psL),
             waits=[(sT, av_last_t)], inc=sV)
        P.op("vector", lambda e: e.reciprocal(rl_sb[:], l_sb[:]), inc=sV)
        rl_v = sV.c

        evz = {}
        for gi in range(8):
            qt, g = gi // 2, gi % 2
            pb = psT if gi % 2 == 0 else bank(0)
            w = [(sV, sV.c), (sS, evz.get(gi - 2, sS.c if gi == 0 else 0))]
            for b in range(4):
                dc = 4 * g + b
                P.op("tensor",
                     lambda e, qt=qt, dc=dc, b=b, pb=pb: e.transpose(
                         pb[:, 128 * b:128 * b + 128],
                         zacc[qt][:, 128 * dc:128 * dc + 128], iden[:]),
                     waits=w if b == 0 else (),
                     inc=sT if b == 3 else None)
            for b in range(4):
                dc = 4 * g + b
                P.op("scalar",
                     lambda e, qt=qt, dc=dc, b=b, pb=pb: e.copy(
                         xfq[dc][:, 128 * qt:128 * qt + 128],
                         pb[:, 128 * b:128 * b + 128]),
                     waits=[(sT, sT.c)], inc=sS if b == 3 else None)
            evz[gi] = sS.c
        zf_s = sS.c

        for dc in range(DCH):
            P.dma("sync", xk[dc][:], wvt_d[128 * dc:128 * dc + 128, :], sK,
                  waits=[(sT, av_last_t), (sG, wvt_g)])
        wvt_k = sK.c

        vo_prev = {}
        for n in range(8):
            qt, h = n // 2, n % 2
            pv = psV[n % 2]
            w = [(sK, wvt_k), (sS, zf_s)]
            if n >= 2:
                w.append((sV, vo_prev[n - 2]))
            for dc in range(DCH):
                P.op("tensor",
                     lambda e, pv=pv, qt=qt, h=h, dc=dc: e.matmul(
                         pv, xfq[dc][:, 128 * qt:128 * qt + 128],
                         xk[dc][:, 512 * h:512 * h + 512],
                         start=(dc == 0), stop=(dc == DCH - 1)),
                     waits=w if dc == 0 else (),
                     inc=sT if dc == DCH - 1 else None)
            P.op("vector",
                 lambda e, pv=pv, qt=qt, h=h: e.scalar_tensor_tensor(
                     vout[qt][:, 512 * h:512 * h + 512], pv,
                     rl_sb[:, qt:qt + 1], qown[qt][:, 512 * h:512 * h + 512],
                     ALU.mult, ALU.add),
                 waits=[(sT, sT.c), (sV, rl_v), (sS, sS.c)], inc=sV)
            vo_prev[n] = sV.c
        prev_vo_v = sV.c
        zwv_t_prev = sT.c

        for qt in range(4):
            P.op("vector",
                 lambda e, qt=qt: e.reduce_sum(srow[:, qt:qt + 1],
                                               vout[qt][:], AX),
                 waits=[(sV, vo_prev[2 * qt + 1])], inc=sV)
            P.op("scalar",
                 lambda e, qt=qt: e.activation(xf[1][:], vout[qt][:],
                                               ACT.Square,
                                               accum_out=ssq[:, qt:qt + 1]),
                 waits=[(sV, vo_prev[2 * qt + 1])], inc=sS)
        P.op("vector",
             lambda e: e.tensor_scalar(negm[:], srow[:], -1.0 / D, None,
                                       ALU.mult),
             waits=[(sV, sV.c)], inc=sV)
        P.op("scalar", lambda e: e.activation(m2t[:], negm[:], ACT.Square),
             waits=[(sV, sV.c)], inc=sS)
        P.op("vector",
             lambda e: e.tensor_scalar(vvar[:], ssq[:], 1.0 / D, None,
                                       ALU.mult),
             waits=[(sS, sS.c)], inc=sV)
        P.op("vector", lambda e: e.tensor_copy(dbg1[:], vvar[:]), inc=sV)
        P.op("vector",
             lambda e: e.tensor_tensor(vvar[:], vvar[:], m2t[:], ALU.subtract),
             inc=sV)
        P.op("vector", lambda e: e.tensor_copy(dbg2[:], vvar[:]), inc=sV)
        P.op("scalar",
             lambda e: e.activation(sdt[:], vvar[:], ACT.Sqrt,
                                    bias=eps_c[:, 0:1]),
             waits=[(sV, sV.c)], inc=sS)
        P.op("vector", lambda e: e.reciprocal(rstd[:], sdt[:]),
             waits=[(sS, sS.c)], inc=sV)
        ln1_v = sV.c
        for qt in range(4):
            for hh in range(2):
                P.op("vector",
                     lambda e, qt=qt, hh=hh: e.tensor_scalar(
                         qp[2 * qt + hh][:],
                         vout[qt][:, 512 * hh:512 * hh + 512],
                         negm[:, qt:qt + 1],
                         rstd[:, qt:qt + 1], ALU.add, ALU.mult),
                     waits=[(sV, ln1_v)], inc=sV)
                P.op("vector",
                     lambda e, qt=qt, hh=hh: e.tensor_reduce(
                         tmax[:], qp[2 * qt + hh][:], AX, ALU.max,
                         apply_absolute_value=True),
                     inc=sV)
                P.op("vector",
                     lambda e: e.tensor_max(gmax[:], gmax[:], tmax[:]),
                     inc=sV)
                P.dma("gpsimd",
                      stash[CH * j + 128 * qt:CH * j + 128 * qt + 128,
                            512 * hh:512 * hh + 512],
                      qp[2 * qt + hh][:], sG, waits=[(sV, sV.c)])

    p2_v = sV.c
    stash_g = sG.c
    if DBG_P2:
        P.dma("gpsimd", gmr[:], gmax[:], sG, waits=[(sV, sV.c)])
        P.op("vector", lambda e: e.tensor_reduce(gms[:], gmr[:], AX, ALU.max),
             waits=[(sG, sG.c)], inc=sV)
        P.dma("gpsimd", cc2_in, gms[:], sG, waits=[(sV, sV.c)])
        P.op("gpsimd",
             lambda e: e.collective_compute("AllReduce", ALU.max,
                                            replica_groups=[list(range(NCORES))],
                                            ins=[cc2_in], outs=[cc2_out]),
             waits=[(sG, sG.c)], inc=sC)
        P.dma("gpsimd", xsr[:], cc2_out.partition_broadcast(128), sG,
              waits=[(sC, sC.c)])
        P.op("vector",
             lambda e: e.tensor_scalar(xs_c[:], xsr[:], 1.0 / 127.0, 1e-5,
                                       ALU.mult, ALU.max),
             waits=[(sG, sG.c)], inc=sV)
        P.op("vector", lambda e: e.reciprocal(rxs_c[:], xs_c[:]), inc=sV)
        P.op("vector",
             lambda e: e.tensor_scalar(sc2_c[:], wsc_c[:], xs_c[:, 0:1], None,
                                       ALU.mult),
             inc=sV)
        p2_v = sV.c
        stash_g = sG.c

    for j in range(DBG_P3):
        for qt in range(4):
            P.dma("sync", qown[qt][:],
                  stash[CH * j + 128 * qt:CH * j + 128 * qt + 128, :], sK,
                  waits=[(sG, stash_g), (sV, sV.c), (sT, sT.c)])
        uload = sK.c

        for qt in range(4):
            P.op("vector",
                 lambda e, qt=qt: e.tensor_scalar(
                     xf[0][:], qown[qt][:], rxs_c[:, 0:1], C2,
                     ALU.mult, ALU.add),
                 waits=[(sK, uload), (sV, p2_v), (sS, sS.c), (sT, sT.c)],
                 inc=sV)
            P.op("vector",
                 lambda e: e.tensor_scalar(xf[1][:], xf[0][:], -C2, None,
                                           ALU.add),
                 inc=sV)
            for hh in range(2):
                P.op("vector",
                     lambda e, qt=qt, hh=hh: e.tensor_scalar(
                         mask[2 * qt + hh][:],
                         xf[1][:, 512 * hh:512 * hh + 512],
                         127.0, -128.0, ALU.min, ALU.max),
                     inc=sV)
        xq_v = sV.c

        b6b = bank(6).bitcast(BF16)
        evq = {}
        for gi in range(8):
            qt, hh = gi // 2, gi % 2
            pbb = psTball if gi % 2 == 0 else b6b
            src_t = mask[2 * qt + hh]
            w = [(sV, sV.c), (sS, sS.c),
                 (sV, evq.get(gi - 2, 0))]
            for b in range(4):
                P.op("tensor",
                     lambda e, src_t=src_t, b=b, pbb=pbb: e.transpose(
                         pbb[:, 128 * b:128 * b + 128],
                         src_t[:, 128 * b:128 * b + 128], idb[:]),
                     waits=w if b == 0 else (),
                     inc=sT if b == 3 else None)
            P.op("vector",
                 lambda e, qt=qt, hh=hh, pbb=pbb: e.tensor_copy(
                     xqt.rearrange("p (dc q) -> p dc q", dc=8)[
                         :, 4 * hh:4 * hh + 4, 128 * qt:128 * qt + 128],
                     pbb[:, 0:512].rearrange("p (dc q) -> p dc q", dc=4)),
                 waits=[(sT, sT.c)], inc=sV)
            evq[gi] = sV.c
        xqt_v = sV.c

        ffn_evac = {}
        for nt in range(DCH):
            pf = psF[nt % 2]
            w = [(sV, xqt_v)]
            if nt >= 2:
                w = [(sV, ffn_evac[nt - 2])]
            for dc in range(DCH):
                P.op("tensor",
                     lambda e, pf=pf, nt=nt, dc=dc: e.matmul(
                         pf, wt[dc][:, 128 * nt:128 * nt + 128],
                         xqt[:, 512 * dc:512 * dc + 512],
                         start=(dc == 0), stop=(dc == DCH - 1)),
                     waits=w if dc == 0 else (),
                     inc=sT if dc == DCH - 1 else None)
            P.op("vector",
                 lambda e, pf=pf, nt=nt: e.tensor_scalar(
                     qp[nt][:], pf, sc2_c[:, nt:nt + 1], wbs_c[:, nt:nt + 1],
                     ALU.mult, ALU.add),
                 waits=[(sT, sT.c)], inc=sV)
            ffn_evac[nt] = sV.c

        evf = {}
        for gi in range(8):
            qt, g = gi // 2, gi % 2
            pb = psT if gi % 2 == 0 else bank(6)
            w = [(sV, sV.c), (sS, sS.c), (sV, evf.get(gi - 2, 0))]
            for b in range(4):
                nt = 4 * g + b
                P.op("tensor",
                     lambda e, qt=qt, nt=nt, b=b, pb=pb: e.transpose(
                         pb[:, 128 * b:128 * b + 128],
                         qp[nt][:, 128 * qt:128 * qt + 128], iden[:]),
                     waits=w if b == 0 else (),
                     inc=sT if b == 3 else None)
            P.op("vector",
                 lambda e, qt=qt, g=g, pb=pb: e.tensor_tensor(
                     vout[qt][:, 512 * g:512 * g + 512], pb[:, 0:512],
                     qown[qt][:, 512 * g:512 * g + 512], ALU.add),
                 waits=[(sT, sT.c), (sG, sG.c)], inc=sV)
            evf[gi] = sV.c
        s2_v = sV.c

        for qt in range(4):
            P.op("vector",
                 lambda e, qt=qt: e.reduce_sum(srow[:, qt:qt + 1],
                                               vout[qt][:], AX),
                 waits=[(sV, s2_v)], inc=sV)
            P.op("scalar",
                 lambda e, qt=qt: e.activation(xf[1][:], vout[qt][:],
                                               ACT.Square,
                                               accum_out=ssq[:, qt:qt + 1]),
                 waits=[(sV, s2_v)], inc=sS)
        P.op("vector",
             lambda e: e.tensor_scalar(negm[:], srow[:], -1.0 / D, None,
                                       ALU.mult),
             waits=[(sV, sV.c)], inc=sV)
        P.op("scalar", lambda e: e.activation(m2t[:], negm[:], ACT.Square),
             waits=[(sV, sV.c)], inc=sS)
        P.op("vector",
             lambda e: e.tensor_scalar(vvar[:], ssq[:], 1.0 / D, None,
                                       ALU.mult),
             waits=[(sS, sS.c)], inc=sV)
        P.op("vector",
             lambda e: e.tensor_tensor(vvar[:], vvar[:], m2t[:], ALU.subtract),
             inc=sV)
        P.op("scalar",
             lambda e: e.activation(sdt[:], vvar[:], ACT.Sqrt,
                                    bias=eps_c[:, 0:1]),
             waits=[(sV, sV.c)], inc=sS)
        P.op("vector", lambda e: e.reciprocal(rstd[:], sdt[:]),
             waits=[(sS, sS.c)], inc=sV)
        ln2_v = sV.c
        for qt in range(4):
            P.op("vector",
                 lambda e, qt=qt: e.tensor_scalar(
                     zacc[qt][:], vout[qt][:], negm[:, qt:qt + 1],
                     rstd[:, qt:qt + 1], ALU.add, ALU.mult),
                 waits=[(sV, ln2_v), (sG, sG.c)], inc=sV)
            P.dma("gpsimd", out_d[CH * j + 128 * qt:CH * j + 128 * qt + 128, :],
                  zacc[qt][:], sG, waits=[(sV, sV.c)])

    if DBG_TAP:
        fw = [(sT, sT.c), (sV, sV.c), (sS, sS.c), (sK, sK.c)]
        def tap(row, ap, cols=D):
            P.dma("gpsimd", out_d[row:row + ap.shape[0], 0:cols], ap, sG,
                  waits=fw, slow=True)
        if DBG_TAP == "msb":
            for i in range(DCH):
                tap(128 * i, msb[i][:])
        elif DBG_TAP == "qp":
            for i in range(DCH):
                tap(128 * i, qp[i][:], 512)
        elif DBG_TAP == "xfq":
            for i in range(DCH):
                tap(128 * i, xfq[i][:], 512)
        elif DBG_TAP == "zacc":
            for i in range(4):
                tap(128 * i, zacc[i][:])
        elif DBG_TAP == "qown":
            for i in range(4):
                tap(128 * i, qown[i][:])
        elif DBG_TAP == "pt":
            for i in range(8):
                tap(128 * i, pt[i][:], 512)
        elif DBG_TAP == "wt":
            for i in range(DCH):
                P.dma("gpsimd", out_d[128 * i:128 * i + 128, :], wt[i][:],
                      sG, waits=fw)
        elif DBG_TAP == "ln1":
            for i in range(4):
                tap(128 * i, zacc[i][:])
            tap(3000, srow[:], 4)
            tap(3200, ssq[:], 4)
            tap(3400, negm[:], 4)
            tap(3600, rstd[:], 4)
            tap(3800, l_sb[:], 4)
            tap(2000, vvar[:], 4)
            tap(2200, sdt[:], 4)
            tap(2400, m2t[:], 4)
            tap(2600, dbg1[:], 4)
            tap(2800, dbg2[:], 4)
        elif DBG_TAP == "scalars":
            tap(0, kn2[:], 1)
            tap(1, nbrow[:], 512)
            tap(128, l_sb[:], 4)
            tap(256, rl_sb[:], 4)
            tap(384, rstd[:], 4)
            tap(512, negm[:], 4)
            tap(640, xs_c[:], 1)
            tap(768, srow[:], 4)
            tap(896, ssq[:], 4)

    P.op("gpsimd", None, waits=[(sG, sG.c)])

    with nc.Block() as block:
        P.realize(block)
    stack.close()
    return nc


_NC_CACHE = None


def _get_nc():
    global _NC_CACHE
    if _NC_CACHE is None:
        _NC_CACHE = build_nc()
    return _NC_CACHE


def unit(c, j):
    p = (c + j) % 2
    return ((c // 2 + j) % 4, 2 * j + p)


def make_in_maps(x, wq, wk, wv, wi8, wsc, wbs):
    wsc_c = np.ascontiguousarray(wsc.reshape(DCH, 128).T)
    wbs_c = np.ascontiguousarray(wbs.reshape(DCH, 128).T)
    iden = np.eye(128, dtype=np.float32)
    idb = np.eye(128, dtype=ml_dtypes.bfloat16)
    in_maps = []
    for c in range(NCORES):
        im = {"wq": wq, "wk": wk, "wv": wv, "wi8": wi8,
              "wsc": wsc_c, "wbs": wbs_c, "iden": iden, "idb": idb}
        for j in range(NSLOT):
            b, ch = unit(c, j)
            im[f"q{j}"] = x[b, CH * ch:CH * ch + CH, :]
            keys = np.arange(1024 * j, 1024 * (j + 1))
            qs = CH * ch + np.arange(CH)
            im[f"m{j}"] = (keys[:, None] <= qs[None, :]).astype(
                ml_dtypes.bfloat16)
            if j >= 4:
                im[f"k{j}"] = x[b, :1024 * (j + 1), :]
        in_maps.append(im)
    return in_maps


def kernel(L0_attn_input_x, L0_attn_Wq, L0_attn_Wk, L0_attn_Wv,
           L0_ffn_weight_int8, L0_ffn_weight_scale, L0_ffn_bias):
    x = np.ascontiguousarray(np.asarray(L0_attn_input_x, dtype=np.float32))
    wq = np.ascontiguousarray(np.asarray(L0_attn_Wq, dtype=np.float32))
    wk = np.ascontiguousarray(np.asarray(L0_attn_Wk, dtype=np.float32))
    wv = np.ascontiguousarray(np.asarray(L0_attn_Wv, dtype=np.float32))
    wi8 = np.ascontiguousarray(np.asarray(L0_ffn_weight_int8, dtype=np.int8))
    wsc = np.asarray(L0_ffn_weight_scale, dtype=np.float32).reshape(D)
    wbs = np.asarray(L0_ffn_bias, dtype=np.float32).reshape(D)

    in_maps = make_in_maps(x, wq, wk, wv, wi8, wsc, wbs)
    nc = _get_nc()
    res = run_bass_kernel_spmd(nc, in_maps, core_ids=list(range(NCORES)))

    full = np.empty((B_, T_, D), dtype=np.float32)
    for c in range(NCORES):
        o = np.asarray(res.results[c]["out"])
        for j in range(NSLOT):
            b, ch = unit(c, j)
            full[b, CH * ch:CH * ch + CH, :] = o[CH * j:CH * j + CH, :]
    return full


if __name__ == "__main__":
    nc = build_nc()
    print("built ok; instructions:", nc.n_instructions())


# version 09
# speedup vs baseline: 1.0517x; 1.0129x over previous
import numpy as np
import ml_dtypes
import concourse.bass as bass
import concourse.mybir as mybir
from concourse.bass_utils import run_bass_kernel_spmd
from contextlib import ExitStack

F32 = mybir.dt.float32
BF16 = mybir.dt.bfloat16
I8 = mybir.dt.int8
ALU = mybir.AluOpType
ACT = mybir.ActivationFunctionType
AX = mybir.AxisListType.X

NCORES = 8
B_, T_, D = 4, 8192, 1024
CH = 512
NSLOT = 8
DCH = D // 128
RING = 9
C2 = 1.5 * 2.0**23


class Sem:
    def __init__(self, stack, nc, name):
        self.h = stack.enter_context(nc.semaphore(name))
        self.c = 0


class Prog:
    ENGS = ("sync", "tensor", "vector", "scalar", "gpsimd")

    def __init__(self):
        self.items = {e: [] for e in self.ENGS}

    def op(self, eng, fn, waits=(), inc=None, inc_n=1):
        incs = []
        if inc is not None:
            inc.c += inc_n
            incs.append((inc, inc_n))
        self.items[eng].append((tuple(waits), fn, incs))

    def dma(self, eng, out, in_, sem, waits=(), slow=False):
        self.op(eng, lambda e, o=out, i=in_: e.dma_start(
            out=o, in_=i, allow_slow_non_contiguous=slow),
                waits=waits, inc=sem, inc_n=16)

    def realize(self, block):
        def mk(eng_name):
            def body(e):
                waited = {}
                for waits, fn, incs in self.items[eng_name]:
                    for s, v in waits:
                        if v > 0 and waited.get(id(s), -1) < v:
                            e.wait_ge(s.h, v)
                            waited[id(s)] = v
                    if fn is None:
                        continue
                    ins = fn(e)
                    if incs and eng_name in ("vector", "scalar"):
                        ins = e.drain(fusable=True)
                    for s, n in incs:
                        ins.then_inc(s.h, n)
            return body
        block.sync(mk("sync"))
        block.tensor(mk("tensor"))
        block.vector(mk("vector"))
        block.scalar(mk("scalar"))
        block.gpsimd(mk("gpsimd"))


DBG_P0 = True
DBG_SLOTS = NSLOT
DBG_KEYS = True
DBG_P2 = True
DBG_P3 = NSLOT
DBG_STAGE = 9
DBG_PROJ = 3
DBG_TAP = None


def build_nc():
    nc = bass.Bass("TRN2", target_bir_lowering=False)

    k_in = {j: nc.dram_tensor(f"k{j}", [1024 * (j + 1), D], F32,
                              kind="ExternalInput").ap() for j in range(4, 8)}
    q_in = {j: nc.dram_tensor(f"q{j}", [CH, D], F32, kind="ExternalInput").ap()
            for j in range(8)}
    m_in = {j: nc.dram_tensor(f"m{j}", [1024, CH], BF16, kind="ExternalInput").ap()
            for j in range(8)}
    wq_in = nc.dram_tensor("wq", [D, D], F32, kind="ExternalInput").ap()
    wk_in = nc.dram_tensor("wk", [D, D], F32, kind="ExternalInput").ap()
    wv_in = nc.dram_tensor("wv", [D, D], F32, kind="ExternalInput").ap()
    wi8_in = nc.dram_tensor("wi8", [D, D], I8, kind="ExternalInput").ap()
    wsc_in = nc.dram_tensor("wsc", [128, DCH], F32, kind="ExternalInput").ap()
    wbs_in = nc.dram_tensor("wbs", [128, DCH], F32, kind="ExternalInput").ap()
    iden_in = nc.dram_tensor("iden", [128, 128], F32, kind="ExternalInput").ap()
    idb_in = nc.dram_tensor("idb", [128, 128], BF16, kind="ExternalInput").ap()
    out_d = nc.dram_tensor("out", [NSLOT * CH, D], F32, kind="ExternalOutput").ap()

    stash = nc.dram_tensor("stash", [NSLOT * CH, D], F32).ap()
    wvt_d = nc.dram_tensor("wvt", [D, D], F32).ap()
    cc1_in = nc.dram_tensor("cc1_in", [1, 1], F32).ap()
    cc1_out = nc.dram_tensor("cc1_out", [1, 1], F32).ap()
    cc2_in = nc.dram_tensor("cc2_in", [1, 1], F32).ap()
    cc2_out = nc.dram_tensor("cc2_out", [1, 1], F32).ap()

    stack = ExitStack()
    sb = lambda name, shape, dt: stack.enter_context(
        nc.sbuf_tensor(name, shape, dt))

    msb = [sb(f"msb{i}", [128, D], F32) for i in range(DCH)]
    wt = [sb(f"wt{i}", [128, D], BF16) for i in range(DCH)]
    mask = [sb(f"mask{i}", [128, CH], BF16) for i in range(8)]
    qown = [sb(f"qown{i}", [128, D], F32) for i in range(4)]
    xfq = [sb(f"xfq{i}", [128, CH], F32) for i in range(DCH)]
    qp = [sb(f"qp{i}", [128, CH], F32) for i in range(DCH)]
    xk = [sb(f"xk{i}", [128, D], F32) for i in range(RING)]
    xf = [sb(f"xf{i}", [128, D], F32) for i in range(2)]
    pt = [sb(f"pt{i}", [128, CH], F32) for i in range(8)]
    zacc = [sb(f"zacc{i}", [128, D], F32) for i in range(4)]
    vout = zacc
    xqt = sb("xqt", [128, 4 * D], BF16)

    iden = sb("iden_s", [128, 128], F32)
    idb = sb("idb_s", [128, 128], BF16)
    ones_c = sb("ones_c", [128, 1], F32)
    ones_b = sb("ones_b", [1, 128], BF16)
    eps_c = sb("eps_c", [128, 1], F32)
    wsc_c = sb("wsc_c", [128, DCH], F32)
    wbs_c = sb("wbs_c", [128, DCH], F32)
    sc2_c = sb("sc2_c", [128, DCH], F32)
    kn2 = sb("kn2", [1, 1], F32)
    knl = sb("knl", [128, 1], F32)
    knr = sb("knr", [1, 128], F32)
    kns = sb("kns", [1, 1], F32)
    gmax = sb("gmax", [128, 1], F32)
    gmr = sb("gmr", [1, 128], F32)
    gms = sb("gms", [1, 1], F32)
    xsr = sb("xsr", [128, 1], F32)
    xs_c = sb("xs_c", [128, 1], F32)
    rxs_c = sb("rxs_c", [128, 1], F32)
    l_sb = sb("l_sb", [128, 4], F32)
    rl_sb = sb("rl_sb", [128, 4], F32)
    srow = sb("srow", [128, 4], F32)
    ssq = sb("ssq", [128, 4], F32)
    negm = sb("negm", [128, 4], F32)
    vvar = sb("vvar", [128, 4], F32)
    m2t = sb("m2t", [128, 4], F32)
    sdt = sb("sdt", [128, 4], F32)
    rstd = sb("rstd", [128, 4], F32)
    tmax = sb("tmax", [128, 1], F32)
    dbg1 = sb("dbg1", [128, 4], F32)
    dbg2 = sb("dbg2", [128, 4], F32)
    nbrow = sb("nbrow", [1, CH], F32)
    nbb = sb("nbb", [1, CH], BF16)

    psAll = stack.enter_context(nc.psum_tensor("psAll", [128, 4096], F32))

    def bank(b, w=512):
        return psAll[:, 512 * b:512 * b + w]
    psZ = [psAll[:, 0:1024], psAll[:, 1024:2048]]
    psS = [bank(4), bank(5)]
    psQ = [bank(4), bank(5)]
    psL = bank(6)[:, 0:4]
    psB = bank(6)[0:1, 0:512]
    psT = bank(7)
    psTball = bank(7).bitcast(BF16)
    psF = [bank(0), bank(1)]
    psV = [bank(2), bank(3)]

    sK = Sem(stack, nc, "sK")
    sG = Sem(stack, nc, "sG")
    sT = Sem(stack, nc, "sT")
    sV = Sem(stack, nc, "sV")
    sS = Sem(stack, nc, "sS")
    sC = Sem(stack, nc, "sC")

    P = Prog()

    def tgroup(blocks, bankap, evac_eng, evac_fn, w_extra):
        for i, (src_ap,) in enumerate(blocks):
            P.op("tensor",
                 lambda e, src_ap=src_ap, i=i: e.transpose(
                     bankap[:, 128 * i:128 * i + 128], src_ap, iden[:]),
                 waits=w_extra if i == 0 else (),
                 inc=sT if i == len(blocks) - 1 else None)

    wload = kn_g = wt_s = wvt_g = mprime_t = mprime_v = wi8_k = 0
    P.dma("sync", iden[:], iden_in, sK)
    P.dma("sync", idb[:], idb_in, sK)
    P.dma("sync", wsc_c[:], wsc_in, sK)
    P.dma("sync", wbs_c[:], wbs_in, sK)
    P.op("vector", lambda e: e.memset(ones_c[:], 1.0))
    P.op("vector", lambda e: e.memset(ones_b[:], 1.0))
    P.op("vector", lambda e: e.memset(eps_c[:], 1e-5))
    P.op("vector", lambda e: e.memset(gmax[:], 0.0))
    P.op("vector", lambda e: e.memset(knl[:], 0.0), inc=sV)

    wq_t = xk[:8]
    wk_t = qown + zacc
    for o in range(DCH) if DBG_P0 else []:
        P.dma("sync", wq_t[o][:], wq_in[128 * o:128 * o + 128, :], sK)
        P.dma("sync", wk_t[o][:], wk_in[128 * o:128 * o + 128, :], sK)
    wload = sK.c

    if DBG_P0:
        evac_v = {}
        for n in range(16):
            i, h = n // 2, n % 2
            pq = psQ[n % 2]
            w = [(sK, wload)]
            if n >= 2:
                w.append((sV, evac_v[n - 2]))
            for o in range(DCH):
                P.op("tensor",
                     lambda e, pq=pq, o=o, i=i, h=h: e.matmul(
                         pq, wq_t[o][:, 128 * i:128 * i + 128],
                         wk_t[o][:, 512 * h:512 * h + 512],
                         start=(o == 0), stop=(o == DCH - 1)),
                     waits=w if o == 0 else (),
                     inc=sT if o == DCH - 1 else None)
            P.op("vector",
                 lambda e, pq=pq, i=i, h=h: e.tensor_scalar(
                     msb[i][:, 512 * h:512 * h + 512], pq, 0.03125, None, ALU.mult),
                 waits=[(sT, sT.c)], inc=sV)
            evac_v[n] = sV.c
        mprime_v = sV.c
        mprime_t = sT.c

        qn_sq = {}
        for i in range(32):
            jj, qt = i // 4, i % 4
            buf = xf[i % 2]
            acc = srow[:, i % 2:i % 2 + 1]
            w = [(sS, qn_sq.get(i - 2, 0))]
            P.dma("gpsimd", buf[:], q_in[jj][128 * qt:128 * qt + 128, :],
                  sG, waits=w)
            P.op("scalar",
                 lambda e, buf=buf, acc=acc: e.activation(
                     xk[8][:], buf[:], ACT.Square, accum_out=acc),
                 waits=[(sG, sG.c), (sV, sV.c)], inc=sS)
            qn_sq[i] = sS.c
            P.op("vector",
                 lambda e, acc=acc: e.tensor_max(knl[:], knl[:], acc),
                 waits=[(sS, qn_sq[i])], inc=sV)
        P.dma("gpsimd", knr[:], knl[:], sG, waits=[(sV, sV.c)])
        P.op("vector", lambda e: e.tensor_reduce(kns[:], knr[:], AX, ALU.max),
             waits=[(sG, sG.c)], inc=sV)
        P.dma("gpsimd", cc1_in, kns[:], sG, waits=[(sV, sV.c)])
        P.op("gpsimd",
             lambda e: e.collective_compute("AllReduce", ALU.max,
                                            replica_groups=[list(range(NCORES))],
                                            ins=[cc1_in], outs=[cc1_out]),
             waits=[(sG, sG.c)], inc=sC)
        P.dma("gpsimd", kn2[:], cc1_out, sG, waits=[(sC, sC.c)])
        kn_g = sG.c

        xf0b = xf[0].bitcast(BF16)[:, 0:D]
        for nt in range(DCH):
            P.dma("sync", mask[nt][:].bitcast(I8)[:, 0:D],
                  wi8_in[128 * nt:128 * nt + 128, :], sK)
        wi8_k = sK.c
        for nt in range(DCH):
            P.op("vector",
                 lambda e, nt=nt: e.tensor_copy(xf0b, mask[nt][:].bitcast(I8)[:, 0:D]),
                 waits=[(sK, wi8_k), (sS, sS.c), (sT, sT.c)], inc=sV)
            for dc in range(DCH):
                P.op("tensor",
                     lambda e, dc=dc: e.transpose(
                         psTball[:, 128 * dc:128 * dc + 128],
                         xf0b[:, 128 * dc:128 * dc + 128], idb[:]),
                     waits=[(sV, sV.c), (sS, sS.c)] if dc == 0 else (),
                     inc=sT if dc == DCH - 1 else None)
            for dc in range(DCH):
                P.op("scalar",
                     lambda e, nt=nt, dc=dc: e.copy(
                         wt[dc][:, 128 * nt:128 * nt + 128],
                         psTball[:, 128 * dc:128 * dc + 128]),
                     waits=[(sT, sT.c)], inc=sS if dc == DCH - 1 else None)
        wt_s = sS.c

        wv_asm = qown + zacc
        for nt in range(DCH):
            P.dma("sync", xf[1][:], wv_in[128 * nt:128 * nt + 128, :], sK,
                  waits=[(sS, sS.c)])
            ldk = sK.c
            for g in range(2):
                for b in range(4):
                    dc = 4 * g + b
                    P.op("tensor",
                         lambda e, dc=dc, b=b: e.transpose(
                             psT[:, 128 * b:128 * b + 128],
                             xf[1][:, 128 * dc:128 * dc + 128], iden[:]),
                         waits=[(sK, ldk), (sS, sS.c)] if b == 0 else (),
                         inc=sT if b == 3 else None)
                for b in range(4):
                    dc = 4 * g + b
                    P.op("scalar",
                         lambda e, nt=nt, dc=dc, b=b: e.copy(
                             wv_asm[dc][:, 128 * nt:128 * nt + 128],
                             psT[:, 128 * b:128 * b + 128]),
                         waits=[(sT, sT.c)], inc=sS if b == 3 else None)
        for dc in range(DCH):
            P.dma("gpsimd", wvt_d[128 * dc:128 * dc + 128, :], wv_asm[dc][:], sG,
                  waits=[(sS, sS.c)])
        wvt_g = sG.c

    zwv_t_prev = 0
    prev_av_last_t = 0
    prev_vo_v = 0
    for j in range(DBG_SLOTS):
        ktiles = 8 * (j + 1)
        nblk = ktiles // 8
        kj = k_in[j + 4] if j < 4 else k_in[j]

        for t8 in range(8):
            P.dma("sync", mask[t8][:], m_in[j][128 * t8:128 * t8 + 128, :], sK,
                  waits=[(sS, wt_s), (sV, sV.c)])
        mload = sK.c
        for qt in range(4):
            P.dma("sync", qown[qt][:], q_in[j][128 * qt:128 * qt + 128, :], sK,
                  waits=[(sV, prev_vo_v)] + ([(sG, wvt_g)] if j == 0 else []))
        qload = sK.c
        if DBG_STAGE < 1:
            continue

        evs = {}
        for gi in range(8):
            qt, g = gi // 2, gi % 2
            pb = psT if gi % 2 == 0 else bank(0)
            w = [(sK, qload), (sS, evs.get(gi - 2, sS.c if gi == 0 else 0))]
            for b in range(4):
                dc = 4 * g + b
                P.op("tensor",
                     lambda e, qt=qt, dc=dc, b=b, pb=pb: e.transpose(
                         pb[:, 128 * b:128 * b + 128],
                         qown[qt][:, 128 * dc:128 * dc + 128], iden[:]),
                     waits=w if b == 0 else (),
                     inc=sT if b == 3 else None)
            for b in range(4):
                dc = 4 * g + b
                P.op("scalar",
                     lambda e, qt=qt, dc=dc, b=b, pb=pb: e.copy(
                         xfq[dc][:, 128 * qt:128 * qt + 128],
                         pb[:, 128 * b:128 * b + 128]),
                     waits=[(sT, sT.c)], inc=sS if b == 3 else None)
            evs[gi] = sS.c
        xfq_s = sS.c
        if DBG_STAGE < 2:
            continue

        qp_evac = {}
        for i in range(DCH):
            pq = psQ[i % 2]
            w = [(sS, xfq_s)]
            if i >= 2:
                w = [(sV, qp_evac[i - 2][0]), (sS, qp_evac[i - 2][1])]
            for dc in range(DCH):
                P.op("tensor",
                     lambda e, pq=pq, i=i, dc=dc: e.matmul(
                         pq, msb[dc][:, 128 * i:128 * i + 128], xfq[dc][:],
                         start=(dc == 0), stop=(dc == DCH - 1)),
                     waits=w if dc == 0 else (),
                     inc=sT if dc == DCH - 1 else None)
            mmq = sT.c
            P.op("vector",
                 lambda e, pq=pq, i=i: e.tensor_copy(qp[i][:], pq),
                 waits=[(sT, mmq), (sG, sG.c)], inc=sV)
            cp_v = sV.c
            if DBG_PROJ >= 2:
                P.op("scalar",
                     lambda e, i=i: e.activation(xf[0][:, 0:512], qp[i][:],
                                                 ACT.Square),
                     waits=[(sV, cp_v)], inc=sS)
            sq_s = sS.c
            qp_evac[i] = (cp_v, sq_s)
            if DBG_PROJ >= 3:
                P.op("tensor",
                     lambda e, i=i: e.matmul(psB, ones_c[:], xf[0][:, 0:512],
                                             start=(i == 0),
                                             stop=(i == DCH - 1)),
                     waits=[(sS, sq_s)] + ([(sV, sV.c)] if i == 0 else []),
                     inc=sT if i == DCH - 1 else None)
        if DBG_STAGE < 3:
            continue
        P.op("scalar",
             lambda e: e.activation(nbrow[:], psB, ACT.Sqrt,
                                    scale=kn2[0:1, 0:1]),
             waits=[(sT, sT.c), (sG, kn_g)], inc=sS)
        P.op("vector",
             lambda e: e.tensor_scalar(nbb[:], nbrow[:], -1.0, None, ALU.mult),
             waits=[(sS, sS.c)], inc=sV)
        negb_v = sV.c

        if not DBG_KEYS:
            continue
        zinit = [False] * 4
        av_done_t = {}
        av_h0_t = {}
        ttadd_v = {}
        exp_s = {}
        first_l = True
        kdma_t = {}
        evac_g0_s = {}
        evac_g1_v = {}

        def key_dma(t):
            if t >= ktiles:
                return
            r = t % RING
            w = []
            if t >= RING:
                told = t - RING
                w.append((sT, av_h0_t[told // 8] if told % 8 < 4
                          else av_done_t[told // 8]))
            else:
                if j == 0:
                    w.append((sT, mprime_t))
                else:
                    w.append((sT, prev_av_last_t))
            P.dma("sync", xk[r][:], kj[128 * t:128 * t + 128, :], sK, waits=w)
            kdma_t[t] = sK.c

        def trans_group(t, g):
            r = t % RING
            xft = xf[t % 2]
            if g == 0:
                w = [(sK, kdma_t[t]), (sS, evac_g0_s.get(t - 1, 0)),
                     (sV, evac_g1_v.get(t - 1, 0))]
            else:
                w = [(sS, evac_g0_s[t])]
            for b in range(4):
                dc = 4 * g + b
                P.op("tensor",
                     lambda e, r=r, dc=dc, b=b: e.transpose(
                         psT[:, 128 * b:128 * b + 128],
                         xk[r][:, 128 * dc:128 * dc + 128], iden[:]),
                     waits=w if b == 0 else (),
                     inc=sT if b == 3 else None)
            if g == 0:
                P.op("scalar",
                     lambda e, xft=xft: e.copy(xft[:, 0:512], psT[:, 0:512]),
                     waits=[(sT, sT.c)], inc=sS)
                evac_g0_s[t] = sS.c
            else:
                P.op("vector",
                     lambda e, xft=xft: e.tensor_copy(xft[:, 512:1024],
                                                      psT[:, 0:512]),
                     waits=[(sT, sT.c)], inc=sV)
                evac_g1_v[t] = sV.c

        def scores_half(t, half):
            pss = psS[t % 2]
            xft = xf[t % 2]
            rng = range(0, 4) if half == 0 else range(4, 8)
            if half == 0:
                w = [(sS, evac_g0_s[t]), (sV, max(evac_g1_v[t], negb_v))]
                if t >= 2:
                    w.append((sS, exp_s[t - 2]))
            else:
                w = []
            for dc in rng:
                P.op("tensor",
                     lambda e, pss=pss, xft=xft, dc=dc: e.matmul(
                         pss, xft[:, 128 * dc:128 * dc + 128], qp[dc][:],
                         start=(dc == 0), stop=False),
                     waits=w if dc == rng.start else ())
            if half == 1:
                P.op("tensor",
                     lambda e, pss=pss: e.matmul(pss, ones_b[:], nbb[:],
                                                 start=False, stop=True),
                     inc=sT)

        key_dma(0)
        key_dma(1)
        trans_group(0, 0)
        trans_group(0, 1)
        for t in range(ktiles):
            blk, tb = t // 8, t % 8
            if t + 1 < ktiles:
                trans_group(t + 1, 0)
            scores_half(t, 0)
            if t + 1 < ktiles:
                trans_group(t + 1, 1)
            scores_half(t, 1)
            scores_done = sT.c

            ex_w = [(sT, scores_done)]
            if blk > 0:
                ex_w.append((sT, av_h0_t[blk - 1] if tb < 4
                             else av_done_t[blk - 1]))
            P.op("scalar",
                 lambda e, pss=psS[t % 2], tb=tb: e.activation(
                     pt[tb][:], pss, ACT.Exp),
                 waits=ex_w, inc=sS)
            exp_s[t] = sS.c
            if t >= ktiles - 8:
                P.op("vector",
                     lambda e, tb=tb, mi=t - (ktiles - 8): e.tensor_tensor(
                         pt[tb][:], pt[tb][:], mask[mi][:], ALU.mult),
                     waits=[(sS, exp_s[t]), (sK, mload)], inc=sV)

            def av_half(ablk, half):
                need_s = exp_s[8 * ablk + 4 * half + 3]
                need_v = sV.c
                for qt in range(4):
                    pz = psZ[qt % 2]
                    av_w = [(sS, need_s), (sV, need_v)]
                    prev = (ablk, 0, qt) if half == 1 else \
                        (ablk - 1, 1, qt)
                    if qt >= 2:
                        prev = (ablk, half, qt - 2)
                    if prev in ttadd_v:
                        av_w.append((sV, ttadd_v[prev]))
                    for tb2 in range(4 * half, 4 * half + 4):
                        r2 = (8 * ablk + tb2) % RING
                        for h in range(2):
                            P.op("tensor",
                                 lambda e, pz=pz, tb2=tb2, qt=qt, r2=r2, h=h,
                                 half=half:
                                 e.matmul(
                                     pz[:, 512 * h:512 * h + 512],
                                     pt[tb2][:, 128 * qt:128 * qt + 128],
                                     xk[r2][:, 512 * h:512 * h + 512],
                                     start=(tb2 == 4 * half),
                                     stop=(tb2 == 4 * half + 3)),
                                 waits=av_w if (tb2 == 4 * half and h == 0)
                                 else ())
                        is_last_l = (ablk == nblk - 1 and qt == 3
                                     and tb2 == 7)
                        P.op("tensor",
                             lambda e, tb2=tb2, qt=qt,
                             fl=(ablk == 0 and qt == 0 and tb2 == 0),
                             ll=is_last_l:
                             e.matmul(psL[:, qt:qt + 1],
                                      pt[tb2][:, 128 * qt:128 * qt + 128],
                                      ones_c[:], start=fl, stop=ll,
                                      skip_group_check=True),
                             inc=sT if tb2 == 4 * half + 3 else None)
                    av_mm = sT.c
                    if not zinit[qt]:
                        zw = [(sT, av_mm)]
                        if j == 0:
                            zw.append((sG, wvt_g))
                        P.op("vector",
                             lambda e, qt=qt, pz=pz: e.tensor_copy(
                                 zacc[qt][:], pz),
                             waits=zw, inc=sV)
                        zinit[qt] = True
                    else:
                        P.op("vector",
                             lambda e, qt=qt, pz=pz: e.tensor_tensor(
                                 zacc[qt][:], zacc[qt][:], pz, ALU.add),
                             waits=[(sT, av_mm)], inc=sV)
                    ttadd_v[(ablk, half, qt)] = sV.c
                if half == 0:
                    av_h0_t[ablk] = sT.c
                else:
                    av_done_t[ablk] = sT.c

            if tb == 5:
                av_half(blk, 0)
            elif tb == 1 and blk > 0:
                av_half(blk - 1, 1)
            if t == ktiles - 1:
                av_half(blk, 1)
            key_dma(t + 2)

        av_last_t = av_done_t[nblk - 1]
        av_last_v = sV.c

        P.op("vector", lambda e: e.tensor_copy(l_sb[:], psL),
             waits=[(sT, av_last_t)], inc=sV)
        P.op("vector", lambda e: e.reciprocal(rl_sb[:], l_sb[:]), inc=sV)
        rl_v = sV.c

        evz = {}
        for gi in range(8):
            qt, g = gi // 2, gi % 2
            pb = psT if gi % 2 == 0 else bank(0)
            w = [(sV, sV.c), (sS, evz.get(gi - 2, sS.c if gi == 0 else 0))]
            for b in range(4):
                dc = 4 * g + b
                P.op("tensor",
                     lambda e, qt=qt, dc=dc, b=b, pb=pb: e.transpose(
                         pb[:, 128 * b:128 * b + 128],
                         zacc[qt][:, 128 * dc:128 * dc + 128], iden[:]),
                     waits=w if b == 0 else (),
                     inc=sT if b == 3 else None)
            for b in range(4):
                dc = 4 * g + b
                P.op("scalar",
                     lambda e, qt=qt, dc=dc, b=b, pb=pb: e.copy(
                         xfq[dc][:, 128 * qt:128 * qt + 128],
                         pb[:, 128 * b:128 * b + 128]),
                     waits=[(sT, sT.c)], inc=sS if b == 3 else None)
            evz[gi] = sS.c
        zf_s = sS.c

        for dc in range(DCH):
            P.dma("sync", qp[dc][:], wvt_d[128 * dc:128 * dc + 128, 0:512],
                  sK, waits=[(sT, av_last_t), (sG, wvt_g)])
            P.dma("sync", pt[dc][:], wvt_d[128 * dc:128 * dc + 128, 512:1024],
                  sK, waits=[(sT, av_last_t)])
        wvt_k = sK.c

        vo_prev = {}
        for n in range(8):
            qt, h = n // 2, n % 2
            pv = psV[n % 2]
            w = [(sK, wvt_k), (sS, zf_s)]
            if n >= 2:
                w.append((sV, vo_prev[n - 2]))
            for dc in range(DCH):
                P.op("tensor",
                     lambda e, pv=pv, qt=qt, h=h, dc=dc: e.matmul(
                         pv, xfq[dc][:, 128 * qt:128 * qt + 128],
                         (qp[dc] if h == 0 else pt[dc])[:],
                         start=(dc == 0), stop=(dc == DCH - 1)),
                     waits=w if dc == 0 else (),
                     inc=sT if dc == DCH - 1 else None)
            P.op("vector",
                 lambda e, pv=pv, qt=qt, h=h: e.scalar_tensor_tensor(
                     vout[qt][:, 512 * h:512 * h + 512], pv,
                     rl_sb[:, qt:qt + 1], qown[qt][:, 512 * h:512 * h + 512],
                     ALU.mult, ALU.add),
                 waits=[(sT, sT.c), (sV, rl_v), (sS, sS.c)], inc=sV)
            vo_prev[n] = sV.c
        prev_vo_v = sV.c
        prev_av_last_t = av_last_t
        zwv_t_prev = sT.c

        for qt in range(4):
            P.op("vector",
                 lambda e, qt=qt: e.reduce_sum(srow[:, qt:qt + 1],
                                               vout[qt][:], AX),
                 waits=[(sV, vo_prev[2 * qt + 1])], inc=sV)
            P.op("scalar",
                 lambda e, qt=qt: e.activation(xf[1][:], vout[qt][:],
                                               ACT.Square,
                                               accum_out=ssq[:, qt:qt + 1]),
                 waits=[(sV, vo_prev[2 * qt + 1])], inc=sS)
        P.op("vector",
             lambda e: e.tensor_scalar(negm[:], srow[:], -1.0 / D, None,
                                       ALU.mult),
             waits=[(sV, sV.c)], inc=sV)
        P.op("scalar", lambda e: e.activation(m2t[:], negm[:], ACT.Square),
             waits=[(sV, sV.c)], inc=sS)
        P.op("vector",
             lambda e: e.tensor_scalar(vvar[:], ssq[:], 1.0 / D, None,
                                       ALU.mult),
             waits=[(sS, sS.c)], inc=sV)
        P.op("vector", lambda e: e.tensor_copy(dbg1[:], vvar[:]), inc=sV)
        P.op("vector",
             lambda e: e.tensor_tensor(vvar[:], vvar[:], m2t[:], ALU.subtract),
             inc=sV)
        P.op("vector", lambda e: e.tensor_copy(dbg2[:], vvar[:]), inc=sV)
        P.op("scalar",
             lambda e: e.activation(sdt[:], vvar[:], ACT.Sqrt,
                                    bias=eps_c[:, 0:1]),
             waits=[(sV, sV.c)], inc=sS)
        P.op("vector", lambda e: e.reciprocal(rstd[:], sdt[:]),
             waits=[(sS, sS.c)], inc=sV)
        ln1_v = sV.c
        for qt in range(4):
            for hh in range(2):
                P.op("vector",
                     lambda e, qt=qt, hh=hh: e.tensor_scalar(
                         qp[2 * qt + hh][:],
                         vout[qt][:, 512 * hh:512 * hh + 512],
                         negm[:, qt:qt + 1],
                         rstd[:, qt:qt + 1], ALU.add, ALU.mult),
                     waits=[(sV, ln1_v)], inc=sV)
                P.op("vector",
                     lambda e, qt=qt, hh=hh: e.tensor_reduce(
                         tmax[:], qp[2 * qt + hh][:], AX, ALU.max,
                         apply_absolute_value=True),
                     inc=sV)
                P.op("vector",
                     lambda e: e.tensor_max(gmax[:], gmax[:], tmax[:]),
                     inc=sV)
                P.dma("gpsimd",
                      stash[CH * j + 128 * qt:CH * j + 128 * qt + 128,
                            512 * hh:512 * hh + 512],
                      qp[2 * qt + hh][:], sG, waits=[(sV, sV.c)])

    p2_v = sV.c
    stash_g = sG.c
    if DBG_P2:
        P.dma("gpsimd", gmr[:], gmax[:], sG, waits=[(sV, sV.c)])
        P.op("vector", lambda e: e.tensor_reduce(gms[:], gmr[:], AX, ALU.max),
             waits=[(sG, sG.c)], inc=sV)
        P.dma("gpsimd", cc2_in, gms[:], sG, waits=[(sV, sV.c)])
        P.op("gpsimd",
             lambda e: e.collective_compute("AllReduce", ALU.max,
                                            replica_groups=[list(range(NCORES))],
                                            ins=[cc2_in], outs=[cc2_out]),
             waits=[(sG, sG.c)], inc=sC)
        P.dma("gpsimd", xsr[:], cc2_out.partition_broadcast(128), sG,
              waits=[(sC, sC.c)])
        P.op("vector",
             lambda e: e.tensor_scalar(xs_c[:], xsr[:], 1.0 / 127.0, 1e-5,
                                       ALU.mult, ALU.max),
             waits=[(sG, sG.c)], inc=sV)
        P.op("vector", lambda e: e.reciprocal(rxs_c[:], xs_c[:]), inc=sV)
        P.op("vector",
             lambda e: e.tensor_scalar(sc2_c[:], wsc_c[:], xs_c[:, 0:1], None,
                                       ALU.mult),
             inc=sV)
        p2_v = sV.c
        stash_g = sG.c

    for j in range(DBG_P3):
        for qt in range(4):
            P.dma("sync", qown[qt][:],
                  stash[CH * j + 128 * qt:CH * j + 128 * qt + 128, :], sK,
                  waits=[(sG, stash_g), (sV, sV.c), (sT, sT.c)])
        uload = sK.c

        for qt in range(4):
            P.op("vector",
                 lambda e, qt=qt: e.tensor_scalar(
                     xf[0][:], qown[qt][:], rxs_c[:, 0:1], C2,
                     ALU.mult, ALU.add),
                 waits=[(sK, uload), (sV, p2_v), (sS, sS.c), (sT, sT.c)],
                 inc=sV)
            P.op("vector",
                 lambda e: e.tensor_scalar(xf[1][:], xf[0][:], -C2, None,
                                           ALU.add),
                 inc=sV)
            for hh in range(2):
                P.op("vector",
                     lambda e, qt=qt, hh=hh: e.tensor_scalar(
                         mask[2 * qt + hh][:],
                         xf[1][:, 512 * hh:512 * hh + 512],
                         127.0, -128.0, ALU.min, ALU.max),
                     inc=sV)
        xq_v = sV.c

        b6b = bank(6).bitcast(BF16)
        evq = {}
        for gi in range(8):
            qt, hh = gi // 2, gi % 2
            pbb = psTball if gi % 2 == 0 else b6b
            src_t = mask[2 * qt + hh]
            w = [(sV, sV.c), (sS, sS.c),
                 (sV, evq.get(gi - 2, 0))]
            for b in range(4):
                P.op("tensor",
                     lambda e, src_t=src_t, b=b, pbb=pbb: e.transpose(
                         pbb[:, 128 * b:128 * b + 128],
                         src_t[:, 128 * b:128 * b + 128], idb[:]),
                     waits=w if b == 0 else (),
                     inc=sT if b == 3 else None)
            P.op("vector",
                 lambda e, qt=qt, hh=hh, pbb=pbb: e.tensor_copy(
                     xqt.rearrange("p (dc q) -> p dc q", dc=8)[
                         :, 4 * hh:4 * hh + 4, 128 * qt:128 * qt + 128],
                     pbb[:, 0:512].rearrange("p (dc q) -> p dc q", dc=4)),
                 waits=[(sT, sT.c)], inc=sV)
            evq[gi] = sV.c
        xqt_v = sV.c

        ffn_evac = {}
        for nt in range(DCH):
            pf = psF[nt % 2]
            w = [(sV, xqt_v)]
            if nt >= 2:
                w = [(sV, ffn_evac[nt - 2])]
            for dc in range(DCH):
                P.op("tensor",
                     lambda e, pf=pf, nt=nt, dc=dc: e.matmul(
                         pf, wt[dc][:, 128 * nt:128 * nt + 128],
                         xqt[:, 512 * dc:512 * dc + 512],
                         start=(dc == 0), stop=(dc == DCH - 1)),
                     waits=w if dc == 0 else (),
                     inc=sT if dc == DCH - 1 else None)
            P.op("vector",
                 lambda e, pf=pf, nt=nt: e.tensor_scalar(
                     qp[nt][:], pf, sc2_c[:, nt:nt + 1], wbs_c[:, nt:nt + 1],
                     ALU.mult, ALU.add),
                 waits=[(sT, sT.c)], inc=sV)
            ffn_evac[nt] = sV.c

        evf = {}
        for gi in range(8):
            qt, g = gi // 2, gi % 2
            pb = psT if gi % 2 == 0 else bank(6)
            w = [(sV, sV.c), (sS, sS.c), (sV, evf.get(gi - 2, 0))]
            for b in range(4):
                nt = 4 * g + b
                P.op("tensor",
                     lambda e, qt=qt, nt=nt, b=b, pb=pb: e.transpose(
                         pb[:, 128 * b:128 * b + 128],
                         qp[nt][:, 128 * qt:128 * qt + 128], iden[:]),
                     waits=w if b == 0 else (),
                     inc=sT if b == 3 else None)
            P.op("vector",
                 lambda e, qt=qt, g=g, pb=pb: e.tensor_tensor(
                     vout[qt][:, 512 * g:512 * g + 512], pb[:, 0:512],
                     qown[qt][:, 512 * g:512 * g + 512], ALU.add),
                 waits=[(sT, sT.c), (sG, sG.c)], inc=sV)
            evf[gi] = sV.c
        s2_v = sV.c

        for qt in range(4):
            P.op("vector",
                 lambda e, qt=qt: e.reduce_sum(srow[:, qt:qt + 1],
                                               vout[qt][:], AX),
                 waits=[(sV, s2_v)], inc=sV)
            P.op("scalar",
                 lambda e, qt=qt: e.activation(xf[1][:], vout[qt][:],
                                               ACT.Square,
                                               accum_out=ssq[:, qt:qt + 1]),
                 waits=[(sV, s2_v)], inc=sS)
        P.op("vector",
             lambda e: e.tensor_scalar(negm[:], srow[:], -1.0 / D, None,
                                       ALU.mult),
             waits=[(sV, sV.c)], inc=sV)
        P.op("scalar", lambda e: e.activation(m2t[:], negm[:], ACT.Square),
             waits=[(sV, sV.c)], inc=sS)
        P.op("vector",
             lambda e: e.tensor_scalar(vvar[:], ssq[:], 1.0 / D, None,
                                       ALU.mult),
             waits=[(sS, sS.c)], inc=sV)
        P.op("vector",
             lambda e: e.tensor_tensor(vvar[:], vvar[:], m2t[:], ALU.subtract),
             inc=sV)
        P.op("scalar",
             lambda e: e.activation(sdt[:], vvar[:], ACT.Sqrt,
                                    bias=eps_c[:, 0:1]),
             waits=[(sV, sV.c)], inc=sS)
        P.op("vector", lambda e: e.reciprocal(rstd[:], sdt[:]),
             waits=[(sS, sS.c)], inc=sV)
        ln2_v = sV.c
        for qt in range(4):
            P.op("vector",
                 lambda e, qt=qt: e.tensor_scalar(
                     zacc[qt][:], vout[qt][:], negm[:, qt:qt + 1],
                     rstd[:, qt:qt + 1], ALU.add, ALU.mult),
                 waits=[(sV, ln2_v), (sG, sG.c)], inc=sV)
            P.dma("gpsimd", out_d[CH * j + 128 * qt:CH * j + 128 * qt + 128, :],
                  zacc[qt][:], sG, waits=[(sV, sV.c)])

    if DBG_TAP:
        fw = [(sT, sT.c), (sV, sV.c), (sS, sS.c), (sK, sK.c)]
        def tap(row, ap, cols=D):
            P.dma("gpsimd", out_d[row:row + ap.shape[0], 0:cols], ap, sG,
                  waits=fw, slow=True)
        if DBG_TAP == "msb":
            for i in range(DCH):
                tap(128 * i, msb[i][:])
        elif DBG_TAP == "qp":
            for i in range(DCH):
                tap(128 * i, qp[i][:], 512)
        elif DBG_TAP == "xfq":
            for i in range(DCH):
                tap(128 * i, xfq[i][:], 512)
        elif DBG_TAP == "zacc":
            for i in range(4):
                tap(128 * i, zacc[i][:])
        elif DBG_TAP == "qown":
            for i in range(4):
                tap(128 * i, qown[i][:])
        elif DBG_TAP == "pt":
            for i in range(8):
                tap(128 * i, pt[i][:], 512)
        elif DBG_TAP == "wt":
            for i in range(DCH):
                P.dma("gpsimd", out_d[128 * i:128 * i + 128, :], wt[i][:],
                      sG, waits=fw)
        elif DBG_TAP == "ln1":
            for i in range(4):
                tap(128 * i, zacc[i][:])
            tap(3000, srow[:], 4)
            tap(3200, ssq[:], 4)
            tap(3400, negm[:], 4)
            tap(3600, rstd[:], 4)
            tap(3800, l_sb[:], 4)
            tap(2000, vvar[:], 4)
            tap(2200, sdt[:], 4)
            tap(2400, m2t[:], 4)
            tap(2600, dbg1[:], 4)
            tap(2800, dbg2[:], 4)
        elif DBG_TAP == "scalars":
            tap(0, kn2[:], 1)
            tap(1, nbrow[:], 512)
            tap(128, l_sb[:], 4)
            tap(256, rl_sb[:], 4)
            tap(384, rstd[:], 4)
            tap(512, negm[:], 4)
            tap(640, xs_c[:], 1)
            tap(768, srow[:], 4)
            tap(896, ssq[:], 4)

    P.op("gpsimd", None, waits=[(sG, sG.c)])

    with nc.Block() as block:
        P.realize(block)
    stack.close()
    return nc


_NC_CACHE = None


def _get_nc():
    global _NC_CACHE
    if _NC_CACHE is None:
        _NC_CACHE = build_nc()
    return _NC_CACHE


def unit(c, j):
    p = (c + j) % 2
    return ((c // 2 + j) % 4, 2 * j + p)


def make_in_maps(x, wq, wk, wv, wi8, wsc, wbs):
    wsc_c = np.ascontiguousarray(wsc.reshape(DCH, 128).T)
    wbs_c = np.ascontiguousarray(wbs.reshape(DCH, 128).T)
    iden = np.eye(128, dtype=np.float32)
    idb = np.eye(128, dtype=ml_dtypes.bfloat16)
    in_maps = []
    for c in range(NCORES):
        im = {"wq": wq, "wk": wk, "wv": wv, "wi8": wi8,
              "wsc": wsc_c, "wbs": wbs_c, "iden": iden, "idb": idb}
        for j in range(NSLOT):
            b, ch = unit(c, j)
            im[f"q{j}"] = x[b, CH * ch:CH * ch + CH, :]
            keys = np.arange(1024 * j, 1024 * (j + 1))
            qs = CH * ch + np.arange(CH)
            im[f"m{j}"] = (keys[:, None] <= qs[None, :]).astype(
                ml_dtypes.bfloat16)
            if j >= 4:
                im[f"k{j}"] = x[b, :1024 * (j + 1), :]
        in_maps.append(im)
    return in_maps


def kernel(L0_attn_input_x, L0_attn_Wq, L0_attn_Wk, L0_attn_Wv,
           L0_ffn_weight_int8, L0_ffn_weight_scale, L0_ffn_bias):
    x = np.ascontiguousarray(np.asarray(L0_attn_input_x, dtype=np.float32))
    wq = np.ascontiguousarray(np.asarray(L0_attn_Wq, dtype=np.float32))
    wk = np.ascontiguousarray(np.asarray(L0_attn_Wk, dtype=np.float32))
    wv = np.ascontiguousarray(np.asarray(L0_attn_Wv, dtype=np.float32))
    wi8 = np.ascontiguousarray(np.asarray(L0_ffn_weight_int8, dtype=np.int8))
    wsc = np.asarray(L0_ffn_weight_scale, dtype=np.float32).reshape(D)
    wbs = np.asarray(L0_ffn_bias, dtype=np.float32).reshape(D)

    in_maps = make_in_maps(x, wq, wk, wv, wi8, wsc, wbs)
    nc = _get_nc()
    res = run_bass_kernel_spmd(nc, in_maps, core_ids=list(range(NCORES)))

    full = np.empty((B_, T_, D), dtype=np.float32)
    for c in range(NCORES):
        o = np.asarray(res.results[c]["out"])
        for j in range(NSLOT):
            b, ch = unit(c, j)
            full[b, CH * ch:CH * ch + CH, :] = o[CH * j:CH * j + CH, :]
    return full


if __name__ == "__main__":
    nc = build_nc()
    print("built ok; instructions:", nc.n_instructions())


# version 10 (final)
# speedup vs baseline: 1.0524x; 1.0007x over previous
import numpy as np
import ml_dtypes
import concourse.bass as bass
import concourse.mybir as mybir
from concourse.bass_utils import run_bass_kernel_spmd
from contextlib import ExitStack

F32 = mybir.dt.float32
BF16 = mybir.dt.bfloat16
I8 = mybir.dt.int8
ALU = mybir.AluOpType
ACT = mybir.ActivationFunctionType
AX = mybir.AxisListType.X

NCORES = 8
B_, T_, D = 4, 8192, 1024
CH = 512
NSLOT = 8
DCH = D // 128
RING = 9
C2 = 1.5 * 2.0**23


class Sem:
    def __init__(self, stack, nc, name):
        self.h = stack.enter_context(nc.semaphore(name))
        self.c = 0


class Prog:
    ENGS = ("sync", "tensor", "vector", "scalar", "gpsimd")

    def __init__(self):
        self.items = {e: [] for e in self.ENGS}

    def op(self, eng, fn, waits=(), inc=None, inc_n=1):
        incs = []
        if inc is not None:
            inc.c += inc_n
            incs.append((inc, inc_n))
        self.items[eng].append((tuple(waits), fn, incs))

    def dma(self, eng, out, in_, sem, waits=(), slow=False):
        self.op(eng, lambda e, o=out, i=in_: e.dma_start(
            out=o, in_=i, allow_slow_non_contiguous=slow),
                waits=waits, inc=sem, inc_n=16)

    def realize(self, block):
        def mk(eng_name):
            def body(e):
                waited = {}
                for waits, fn, incs in self.items[eng_name]:
                    for s, v in waits:
                        if v > 0 and waited.get(id(s), -1) < v:
                            e.wait_ge(s.h, v)
                            waited[id(s)] = v
                    if fn is None:
                        continue
                    ins = fn(e)
                    if incs and eng_name in ("vector", "scalar"):
                        ins = e.drain(fusable=True)
                    for s, n in incs:
                        ins.then_inc(s.h, n)
            return body
        block.sync(mk("sync"))
        block.tensor(mk("tensor"))
        block.vector(mk("vector"))
        block.scalar(mk("scalar"))
        block.gpsimd(mk("gpsimd"))


DBG_P0 = True
DBG_SLOTS = NSLOT
DBG_KEYS = True
DBG_P2 = True
DBG_P3 = NSLOT
DBG_STAGE = 9
DBG_PROJ = 3
DBG_TAP = None


def build_nc():
    nc = bass.Bass("TRN2", target_bir_lowering=False)

    k_in = {j: nc.dram_tensor(f"k{j}", [1024 * (j + 1), D], F32,
                              kind="ExternalInput").ap() for j in range(4, 8)}
    q_in = {j: nc.dram_tensor(f"q{j}", [CH, D], F32, kind="ExternalInput").ap()
            for j in range(8)}
    m_in = {j: nc.dram_tensor(f"m{j}", [1024, CH], BF16, kind="ExternalInput").ap()
            for j in range(8)}
    wq_in = nc.dram_tensor("wq", [D, D], F32, kind="ExternalInput").ap()
    wk_in = nc.dram_tensor("wk", [D, D], F32, kind="ExternalInput").ap()
    wv_in = nc.dram_tensor("wv", [D, D], F32, kind="ExternalInput").ap()
    wi8_in = nc.dram_tensor("wi8", [D, D], I8, kind="ExternalInput").ap()
    wsc_in = nc.dram_tensor("wsc", [128, DCH], F32, kind="ExternalInput").ap()
    wbs_in = nc.dram_tensor("wbs", [128, DCH], F32, kind="ExternalInput").ap()
    iden_in = nc.dram_tensor("iden", [128, 128], F32, kind="ExternalInput").ap()
    idb_in = nc.dram_tensor("idb", [128, 128], BF16, kind="ExternalInput").ap()
    out_d = nc.dram_tensor("out", [NSLOT * CH, D], F32, kind="ExternalOutput").ap()

    stash = nc.dram_tensor("stash", [NSLOT * CH, D], F32).ap()
    wvt_d = nc.dram_tensor("wvt", [D, D], F32).ap()
    cc1_in = nc.dram_tensor("cc1_in", [1, 1], F32).ap()
    cc1_out = nc.dram_tensor("cc1_out", [1, 1], F32).ap()
    cc2_in = nc.dram_tensor("cc2_in", [1, 1], F32).ap()
    cc2_out = nc.dram_tensor("cc2_out", [1, 1], F32).ap()

    stack = ExitStack()
    sb = lambda name, shape, dt: stack.enter_context(
        nc.sbuf_tensor(name, shape, dt))

    msb = [sb(f"msb{i}", [128, D], F32) for i in range(DCH)]
    wt = [sb(f"wt{i}", [128, D], BF16) for i in range(DCH)]
    mask = [sb(f"mask{i}", [128, CH], BF16) for i in range(8)]
    qown = [sb(f"qown{i}", [128, D], F32) for i in range(4)]
    xfq1 = sb("xfq1", [128, DCH * CH], F32)
    xfq = [xfq1[:, CH * i:CH * i + CH] for i in range(DCH)]
    qp = [sb(f"qp{i}", [128, CH], F32) for i in range(DCH)]
    xk = [sb(f"xk{i}", [128, D], F32) for i in range(RING)]
    xf = [sb(f"xf{i}", [128, D], F32) for i in range(2)]
    pt = [sb(f"pt{i}", [128, CH], F32) for i in range(8)]
    zacc = [sb(f"zacc{i}", [128, D], F32) for i in range(4)]
    vout = zacc
    xqt = sb("xqt", [128, 4 * D], BF16)

    iden = sb("iden_s", [128, 128], F32)
    idb = sb("idb_s", [128, 128], BF16)
    ones_c = sb("ones_c", [128, 1], F32)
    ones_b = sb("ones_b", [1, 128], BF16)
    eps_c = sb("eps_c", [128, 1], F32)
    wsc_c = sb("wsc_c", [128, DCH], F32)
    wbs_c = sb("wbs_c", [128, DCH], F32)
    sc2_c = sb("sc2_c", [128, DCH], F32)
    kn2 = sb("kn2", [1, 1], F32)
    knl = sb("knl", [128, 1], F32)
    knr = sb("knr", [1, 128], F32)
    kns = sb("kns", [1, 1], F32)
    gmax = sb("gmax", [128, 1], F32)
    gmr = sb("gmr", [1, 128], F32)
    gms = sb("gms", [1, 1], F32)
    xsr = sb("xsr", [128, 1], F32)
    xs_c = sb("xs_c", [128, 1], F32)
    rxs_c = sb("rxs_c", [128, 1], F32)
    l_sb = sb("l_sb", [128, 4], F32)
    rl_sb = sb("rl_sb", [128, 4], F32)
    srow = sb("srow", [128, 4], F32)
    ssq = sb("ssq", [128, 4], F32)
    negm = sb("negm", [128, 4], F32)
    vvar = sb("vvar", [128, 4], F32)
    m2t = sb("m2t", [128, 4], F32)
    sdt = sb("sdt", [128, 4], F32)
    rstd = sb("rstd", [128, 4], F32)
    tmax = sb("tmax", [128, 1], F32)
    dbg1 = sb("dbg1", [128, 4], F32)
    dbg2 = sb("dbg2", [128, 4], F32)
    nbrow = sb("nbrow", [1, CH], F32)
    nbb = sb("nbb", [1, CH], BF16)

    psAll = stack.enter_context(nc.psum_tensor("psAll", [128, 4096], F32))

    def bank(b, w=512):
        return psAll[:, 512 * b:512 * b + w]
    psZ = [psAll[:, 0:1024], psAll[:, 1024:2048]]
    psS = [bank(4), bank(5)]
    psQ = [bank(4), bank(5)]
    psL = bank(6)[:, 0:4]
    psB = bank(6)[0:1, 0:512]
    psT = bank(7)
    psTball = bank(7).bitcast(BF16)
    psF = [bank(0), bank(1)]
    psV = [bank(2), bank(3)]

    sK = Sem(stack, nc, "sK")
    sG = Sem(stack, nc, "sG")
    sT = Sem(stack, nc, "sT")
    sV = Sem(stack, nc, "sV")
    sS = Sem(stack, nc, "sS")
    sC = Sem(stack, nc, "sC")

    P = Prog()

    def tgroup(blocks, bankap, evac_eng, evac_fn, w_extra):
        for i, (src_ap,) in enumerate(blocks):
            P.op("tensor",
                 lambda e, src_ap=src_ap, i=i: e.transpose(
                     bankap[:, 128 * i:128 * i + 128], src_ap, iden[:]),
                 waits=w_extra if i == 0 else (),
                 inc=sT if i == len(blocks) - 1 else None)

    wload = kn_g = wt_s = wvt_g = mprime_t = mprime_v = wi8_k = 0
    P.dma("sync", iden[:], iden_in, sK)
    P.dma("sync", idb[:], idb_in, sK)
    P.dma("sync", wsc_c[:], wsc_in, sK)
    P.dma("sync", wbs_c[:], wbs_in, sK)
    P.op("vector", lambda e: e.memset(ones_c[:], 1.0))
    P.op("vector", lambda e: e.memset(ones_b[:], 1.0))
    P.op("vector", lambda e: e.memset(eps_c[:], 1e-5))
    P.op("vector", lambda e: e.memset(gmax[:], 0.0))
    P.op("vector", lambda e: e.memset(knl[:], 0.0), inc=sV)

    wq_t = xk[:8]
    wk_t = qown + zacc
    for o in range(DCH) if DBG_P0 else []:
        P.dma("sync", wq_t[o][:], wq_in[128 * o:128 * o + 128, :], sK)
        P.dma("sync", wk_t[o][:], wk_in[128 * o:128 * o + 128, :], sK)
    wload = sK.c

    if DBG_P0:
        evac_v = {}
        for n in range(16):
            i, h = n // 2, n % 2
            pq = psQ[n % 2]
            w = [(sK, wload)]
            if n >= 2:
                w.append((sV, evac_v[n - 2]))
            for o in range(DCH):
                P.op("tensor",
                     lambda e, pq=pq, o=o, i=i, h=h: e.matmul(
                         pq, wq_t[o][:, 128 * i:128 * i + 128],
                         wk_t[o][:, 512 * h:512 * h + 512],
                         start=(o == 0), stop=(o == DCH - 1)),
                     waits=w if o == 0 else (),
                     inc=sT if o == DCH - 1 else None)
            P.op("vector",
                 lambda e, pq=pq, i=i, h=h: e.tensor_scalar(
                     msb[i][:, 512 * h:512 * h + 512], pq, 0.03125, None, ALU.mult),
                 waits=[(sT, sT.c)], inc=sV)
            evac_v[n] = sV.c
        mprime_v = sV.c
        mprime_t = sT.c

        qn_sq = {}
        for i in range(32):
            jj, qt = i // 4, i % 4
            buf = xf[i % 2]
            acc = srow[:, i % 2:i % 2 + 1]
            w = [(sS, qn_sq.get(i - 2, 0))]
            P.dma("gpsimd", buf[:], q_in[jj][128 * qt:128 * qt + 128, :],
                  sG, waits=w)
            P.op("scalar",
                 lambda e, buf=buf, acc=acc: e.activation(
                     xk[8][:], buf[:], ACT.Square, accum_out=acc),
                 waits=[(sG, sG.c), (sV, sV.c)], inc=sS)
            qn_sq[i] = sS.c
            P.op("vector",
                 lambda e, acc=acc: e.tensor_max(knl[:], knl[:], acc),
                 waits=[(sS, qn_sq[i])], inc=sV)
        P.dma("gpsimd", knr[:], knl[:], sG, waits=[(sV, sV.c)])
        P.op("vector", lambda e: e.tensor_reduce(kns[:], knr[:], AX, ALU.max),
             waits=[(sG, sG.c)], inc=sV)
        P.dma("gpsimd", cc1_in, kns[:], sG, waits=[(sV, sV.c)])
        P.op("gpsimd",
             lambda e: e.collective_compute("AllReduce", ALU.max,
                                            replica_groups=[list(range(NCORES))],
                                            ins=[cc1_in], outs=[cc1_out]),
             waits=[(sG, sG.c)], inc=sC)
        P.dma("gpsimd", kn2[:], cc1_out, sG, waits=[(sC, sC.c)])
        kn_g = sG.c

        xf0b = xf[0].bitcast(BF16)[:, 0:D]
        for nt in range(DCH):
            P.dma("sync", mask[nt][:].bitcast(I8)[:, 0:D],
                  wi8_in[128 * nt:128 * nt + 128, :], sK)
        wi8_k = sK.c
        for nt in range(DCH):
            P.op("vector",
                 lambda e, nt=nt: e.tensor_copy(xf0b, mask[nt][:].bitcast(I8)[:, 0:D]),
                 waits=[(sK, wi8_k), (sS, sS.c), (sT, sT.c)], inc=sV)
            for dc in range(DCH):
                P.op("tensor",
                     lambda e, dc=dc: e.transpose(
                         psTball[:, 128 * dc:128 * dc + 128],
                         xf0b[:, 128 * dc:128 * dc + 128], idb[:]),
                     waits=[(sV, sV.c), (sS, sS.c)] if dc == 0 else (),
                     inc=sT if dc == DCH - 1 else None)
            for dc in range(DCH):
                P.op("scalar",
                     lambda e, nt=nt, dc=dc: e.copy(
                         wt[dc][:, 128 * nt:128 * nt + 128],
                         psTball[:, 128 * dc:128 * dc + 128]),
                     waits=[(sT, sT.c)], inc=sS if dc == DCH - 1 else None)
        wt_s = sS.c

        wv_asm = qown + zacc
        for nt in range(DCH):
            P.dma("sync", xf[1][:], wv_in[128 * nt:128 * nt + 128, :], sK,
                  waits=[(sS, sS.c)])
            ldk = sK.c
            for g in range(2):
                for b in range(4):
                    dc = 4 * g + b
                    P.op("tensor",
                         lambda e, dc=dc, b=b: e.transpose(
                             psT[:, 128 * b:128 * b + 128],
                             xf[1][:, 128 * dc:128 * dc + 128], iden[:]),
                         waits=[(sK, ldk), (sS, sS.c)] if b == 0 else (),
                         inc=sT if b == 3 else None)
                for b in range(4):
                    dc = 4 * g + b
                    P.op("scalar",
                         lambda e, nt=nt, dc=dc, b=b: e.copy(
                             wv_asm[dc][:, 128 * nt:128 * nt + 128],
                             psT[:, 128 * b:128 * b + 128]),
                         waits=[(sT, sT.c)], inc=sS if b == 3 else None)
        for dc in range(DCH):
            P.dma("gpsimd", wvt_d[128 * dc:128 * dc + 128, :], wv_asm[dc][:], sG,
                  waits=[(sS, sS.c)])
        wvt_g = sG.c

    zwv_t_prev = 0
    prev_av_last_t = 0
    prev_vo_v = 0
    for j in range(DBG_SLOTS):
        ktiles = 8 * (j + 1)
        nblk = ktiles // 8
        kj = k_in[j + 4] if j < 4 else k_in[j]

        for t8 in range(8):
            P.dma("sync", mask[t8][:], m_in[j][128 * t8:128 * t8 + 128, :], sK,
                  waits=[(sS, wt_s), (sV, sV.c)])
        mload = sK.c
        for qt in range(4):
            P.dma("sync", qown[qt][:], q_in[j][128 * qt:128 * qt + 128, :], sK,
                  waits=[(sV, prev_vo_v)] + ([(sG, wvt_g)] if j == 0 else []))
        qload = sK.c
        if DBG_STAGE < 1:
            continue

        evs = {}
        for gi in range(8):
            qt, g = gi // 2, gi % 2
            pb = psT if gi % 2 == 0 else bank(0)
            w = [(sK, qload)]
            pe, pv = evs.get(gi - 2, (None, 0))
            if pe is not None:
                w.append((pe, pv))
            else:
                w += [(sS, sS.c), (sV, sV.c)]
            for b in range(4):
                dc = 4 * g + b
                P.op("tensor",
                     lambda e, qt=qt, dc=dc, b=b, pb=pb: e.transpose(
                         pb[:, 128 * b:128 * b + 128],
                         qown[qt][:, 128 * dc:128 * dc + 128], iden[:]),
                     waits=w if b == 0 else (),
                     inc=sT if b == 3 else None)
            dst = xfq1.rearrange("p (dc q) -> p dc q", dc=DCH)[
                :, 4 * g:4 * g + 4, 128 * qt:128 * qt + 128]
            srcv = pb[:, 0:512].rearrange("p (dc q) -> p dc q", dc=4)
            eng, sem = ("scalar", sS) if gi % 2 == 0 else ("vector", sV)
            if eng == "scalar":
                P.op("scalar", lambda e, dst=dst, srcv=srcv: e.copy(dst, srcv),
                     waits=[(sT, sT.c)], inc=sS)
            else:
                P.op("vector",
                     lambda e, dst=dst, srcv=srcv: e.tensor_copy(dst, srcv),
                     waits=[(sT, sT.c)], inc=sV)
            evs[gi] = (sem, sem.c)
        xfq_s = sS.c
        xfq_v2 = sV.c
        xfq_s = sS.c
        if DBG_STAGE < 2:
            continue

        qp_evac = {}
        for i in range(DCH):
            pq = psQ[i % 2]
            w = [(sS, xfq_s), (sV, xfq_v2)]
            if i >= 2:
                w = [(sV, qp_evac[i - 2][0]), (sS, qp_evac[i - 2][1])]
            for dc in range(DCH):
                P.op("tensor",
                     lambda e, pq=pq, i=i, dc=dc: e.matmul(
                         pq, msb[dc][:, 128 * i:128 * i + 128], xfq[dc][:],
                         start=(dc == 0), stop=(dc == DCH - 1)),
                     waits=w if dc == 0 else (),
                     inc=sT if dc == DCH - 1 else None)
            mmq = sT.c
            P.op("vector",
                 lambda e, pq=pq, i=i: e.tensor_copy(qp[i][:], pq),
                 waits=[(sT, mmq), (sG, sG.c)], inc=sV)
            cp_v = sV.c
            if DBG_PROJ >= 2:
                P.op("scalar",
                     lambda e, i=i: e.activation(xf[0][:, 0:512], qp[i][:],
                                                 ACT.Square),
                     waits=[(sV, cp_v)], inc=sS)
            sq_s = sS.c
            qp_evac[i] = (cp_v, sq_s)
            if DBG_PROJ >= 3:
                P.op("tensor",
                     lambda e, i=i: e.matmul(psB, ones_c[:], xf[0][:, 0:512],
                                             start=(i == 0),
                                             stop=(i == DCH - 1)),
                     waits=[(sS, sq_s)] + ([(sV, sV.c)] if i == 0 else []),
                     inc=sT if i == DCH - 1 else None)
        if DBG_STAGE < 3:
            continue
        P.op("scalar",
             lambda e: e.activation(nbrow[:], psB, ACT.Sqrt,
                                    scale=kn2[0:1, 0:1]),
             waits=[(sT, sT.c), (sG, kn_g)], inc=sS)
        P.op("vector",
             lambda e: e.tensor_scalar(nbb[:], nbrow[:], -1.0, None, ALU.mult),
             waits=[(sS, sS.c)], inc=sV)
        negb_v = sV.c

        if not DBG_KEYS:
            continue
        zinit = [False] * 4
        av_done_t = {}
        av_h0_t = {}
        ttadd_v = {}
        exp_s = {}
        first_l = True
        kdma_t = {}
        evac_g0_s = {}
        evac_g1_v = {}

        def key_dma(t):
            if t >= ktiles:
                return
            r = t % RING
            w = []
            if t >= RING:
                told = t - RING
                w.append((sT, av_h0_t[told // 8] if told % 8 < 4
                          else av_done_t[told // 8]))
            else:
                if j == 0:
                    w.append((sT, mprime_t))
                else:
                    w.append((sT, prev_av_last_t))
            P.dma("sync", xk[r][:], kj[128 * t:128 * t + 128, :], sK, waits=w)
            kdma_t[t] = sK.c

        def trans_group(t, g):
            r = t % RING
            xft = xf[t % 2]
            if g == 0:
                w = [(sK, kdma_t[t]), (sS, evac_g0_s.get(t - 1, 0)),
                     (sV, evac_g1_v.get(t - 1, 0))]
            else:
                w = [(sS, evac_g0_s[t])]
            for b in range(4):
                dc = 4 * g + b
                P.op("tensor",
                     lambda e, r=r, dc=dc, b=b: e.transpose(
                         psT[:, 128 * b:128 * b + 128],
                         xk[r][:, 128 * dc:128 * dc + 128], iden[:]),
                     waits=w if b == 0 else (),
                     inc=sT if b == 3 else None)
            if g == 0:
                P.op("scalar",
                     lambda e, xft=xft: e.copy(xft[:, 0:512], psT[:, 0:512]),
                     waits=[(sT, sT.c)], inc=sS)
                evac_g0_s[t] = sS.c
            else:
                P.op("vector",
                     lambda e, xft=xft: e.tensor_copy(xft[:, 512:1024],
                                                      psT[:, 0:512]),
                     waits=[(sT, sT.c)], inc=sV)
                evac_g1_v[t] = sV.c

        def scores_half(t, half):
            pss = psS[t % 2]
            xft = xf[t % 2]
            rng = range(0, 4) if half == 0 else range(4, 8)
            if half == 0:
                w = [(sS, evac_g0_s[t]), (sV, max(evac_g1_v[t], negb_v))]
                if t >= 2:
                    w.append((sS, exp_s[t - 2]))
            else:
                w = []
            for dc in rng:
                P.op("tensor",
                     lambda e, pss=pss, xft=xft, dc=dc: e.matmul(
                         pss, xft[:, 128 * dc:128 * dc + 128], qp[dc][:],
                         start=(dc == 0), stop=False),
                     waits=w if dc == rng.start else ())
            if half == 1:
                P.op("tensor",
                     lambda e, pss=pss: e.matmul(pss, ones_b[:], nbb[:],
                                                 start=False, stop=True),
                     inc=sT)

        key_dma(0)
        key_dma(1)
        trans_group(0, 0)
        trans_group(0, 1)
        for t in range(ktiles):
            blk, tb = t // 8, t % 8
            if t + 1 < ktiles:
                trans_group(t + 1, 0)
            scores_half(t, 0)
            if t + 1 < ktiles:
                trans_group(t + 1, 1)
            scores_half(t, 1)
            scores_done = sT.c

            ex_w = [(sT, scores_done)]
            if blk > 0:
                ex_w.append((sT, av_h0_t[blk - 1] if tb < 4
                             else av_done_t[blk - 1]))
            P.op("scalar",
                 lambda e, pss=psS[t % 2], tb=tb: e.activation(
                     pt[tb][:], pss, ACT.Exp),
                 waits=ex_w, inc=sS)
            exp_s[t] = sS.c
            if t >= ktiles - 8:
                P.op("vector",
                     lambda e, tb=tb, mi=t - (ktiles - 8): e.tensor_tensor(
                         pt[tb][:], pt[tb][:], mask[mi][:], ALU.mult),
                     waits=[(sS, exp_s[t]), (sK, mload)], inc=sV)

            def av_half(ablk, half):
                need_s = exp_s[8 * ablk + 4 * half + 3]
                need_v = sV.c
                for qt in range(4):
                    pz = psZ[qt % 2]
                    av_w = [(sS, need_s), (sV, need_v)]
                    prev = (ablk, 0, qt) if half == 1 else \
                        (ablk - 1, 1, qt)
                    if qt >= 2:
                        prev = (ablk, half, qt - 2)
                    if prev in ttadd_v:
                        av_w.append((sV, ttadd_v[prev]))
                    for tb2 in range(4 * half, 4 * half + 4):
                        r2 = (8 * ablk + tb2) % RING
                        for h in range(2):
                            P.op("tensor",
                                 lambda e, pz=pz, tb2=tb2, qt=qt, r2=r2, h=h,
                                 half=half:
                                 e.matmul(
                                     pz[:, 512 * h:512 * h + 512],
                                     pt[tb2][:, 128 * qt:128 * qt + 128],
                                     xk[r2][:, 512 * h:512 * h + 512],
                                     start=(tb2 == 4 * half),
                                     stop=(tb2 == 4 * half + 3)),
                                 waits=av_w if (tb2 == 4 * half and h == 0)
                                 else ())
                        is_last_l = (ablk == nblk - 1 and qt == 3
                                     and tb2 == 7)
                        P.op("tensor",
                             lambda e, tb2=tb2, qt=qt,
                             fl=(ablk == 0 and qt == 0 and tb2 == 0),
                             ll=is_last_l:
                             e.matmul(psL[:, qt:qt + 1],
                                      pt[tb2][:, 128 * qt:128 * qt + 128],
                                      ones_c[:], start=fl, stop=ll,
                                      skip_group_check=True),
                             inc=sT if tb2 == 4 * half + 3 else None)
                    av_mm = sT.c
                    if not zinit[qt]:
                        zw = [(sT, av_mm)]
                        if j == 0:
                            zw.append((sG, wvt_g))
                        P.op("vector",
                             lambda e, qt=qt, pz=pz: e.tensor_copy(
                                 zacc[qt][:], pz),
                             waits=zw, inc=sV)
                        zinit[qt] = True
                    else:
                        P.op("vector",
                             lambda e, qt=qt, pz=pz: e.tensor_tensor(
                                 zacc[qt][:], zacc[qt][:], pz, ALU.add),
                             waits=[(sT, av_mm)], inc=sV)
                    ttadd_v[(ablk, half, qt)] = sV.c
                if half == 0:
                    av_h0_t[ablk] = sT.c
                else:
                    av_done_t[ablk] = sT.c

            if tb == 5:
                av_half(blk, 0)
            elif tb == 1 and blk > 0:
                av_half(blk - 1, 1)
            if t == ktiles - 1:
                av_half(blk, 1)
            key_dma(t + 2)

        av_last_t = av_done_t[nblk - 1]
        av_last_v = sV.c

        P.op("vector", lambda e: e.tensor_copy(l_sb[:], psL),
             waits=[(sT, av_last_t)], inc=sV)
        P.op("vector", lambda e: e.reciprocal(rl_sb[:], l_sb[:]), inc=sV)
        rl_v = sV.c

        evz = {}
        for gi in range(8):
            qt, g = gi // 2, gi % 2
            pb = psT if gi % 2 == 0 else bank(0)
            w = [(sV, sV.c)]
            pe, pv = evz.get(gi - 2, (None, 0))
            if pe is not None:
                w.append((pe, pv))
            else:
                w.append((sS, sS.c))
            for b in range(4):
                dc = 4 * g + b
                P.op("tensor",
                     lambda e, qt=qt, dc=dc, b=b, pb=pb: e.transpose(
                         pb[:, 128 * b:128 * b + 128],
                         zacc[qt][:, 128 * dc:128 * dc + 128], iden[:]),
                     waits=w if b == 0 else (),
                     inc=sT if b == 3 else None)
            dst = xfq1.rearrange("p (dc q) -> p dc q", dc=DCH)[
                :, 4 * g:4 * g + 4, 128 * qt:128 * qt + 128]
            srcv = pb[:, 0:512].rearrange("p (dc q) -> p dc q", dc=4)
            eng, sem = ("scalar", sS) if gi % 2 == 0 else ("vector", sV)
            if eng == "scalar":
                P.op("scalar", lambda e, dst=dst, srcv=srcv: e.copy(dst, srcv),
                     waits=[(sT, sT.c)], inc=sS)
            else:
                P.op("vector",
                     lambda e, dst=dst, srcv=srcv: e.tensor_copy(dst, srcv),
                     waits=[(sT, sT.c)], inc=sV)
            evz[gi] = (sem, sem.c)
        zf_s = sS.c
        zf_v2 = sV.c
        zf_s = sS.c

        for dc in range(DCH):
            P.dma("sync", qp[dc][:], wvt_d[128 * dc:128 * dc + 128, 0:512],
                  sK, waits=[(sT, av_last_t), (sG, wvt_g)])
            P.dma("sync", pt[dc][:], wvt_d[128 * dc:128 * dc + 128, 512:1024],
                  sK, waits=[(sT, av_last_t)])
        wvt_k = sK.c

        vo_prev = {}
        for n in range(8):
            qt, h = n // 2, n % 2
            pv = psV[n % 2]
            w = [(sK, wvt_k), (sS, zf_s), (sV, zf_v2)]
            if n >= 2:
                w.append((sV, vo_prev[n - 2]))
            for dc in range(DCH):
                P.op("tensor",
                     lambda e, pv=pv, qt=qt, h=h, dc=dc: e.matmul(
                         pv, xfq[dc][:, 128 * qt:128 * qt + 128],
                         (qp[dc] if h == 0 else pt[dc])[:],
                         start=(dc == 0), stop=(dc == DCH - 1)),
                     waits=w if dc == 0 else (),
                     inc=sT if dc == DCH - 1 else None)
            P.op("vector",
                 lambda e, pv=pv, qt=qt, h=h: e.scalar_tensor_tensor(
                     vout[qt][:, 512 * h:512 * h + 512], pv,
                     rl_sb[:, qt:qt + 1], qown[qt][:, 512 * h:512 * h + 512],
                     ALU.mult, ALU.add),
                 waits=[(sT, sT.c), (sV, rl_v), (sS, sS.c)], inc=sV)
            vo_prev[n] = sV.c
        prev_vo_v = sV.c
        prev_av_last_t = av_last_t
        zwv_t_prev = sT.c

        for qt in range(4):
            P.op("vector",
                 lambda e, qt=qt: e.reduce_sum(srow[:, qt:qt + 1],
                                               vout[qt][:], AX),
                 waits=[(sV, vo_prev[2 * qt + 1])], inc=sV)
            P.op("scalar",
                 lambda e, qt=qt: e.activation(xf[1][:], vout[qt][:],
                                               ACT.Square,
                                               accum_out=ssq[:, qt:qt + 1]),
                 waits=[(sV, vo_prev[2 * qt + 1])], inc=sS)
        P.op("vector",
             lambda e: e.tensor_scalar(negm[:], srow[:], -1.0 / D, None,
                                       ALU.mult),
             waits=[(sV, sV.c)], inc=sV)
        P.op("scalar", lambda e: e.activation(m2t[:], negm[:], ACT.Square),
             waits=[(sV, sV.c)], inc=sS)
        P.op("vector",
             lambda e: e.tensor_scalar(vvar[:], ssq[:], 1.0 / D, None,
                                       ALU.mult),
             waits=[(sS, sS.c)], inc=sV)
        P.op("vector", lambda e: e.tensor_copy(dbg1[:], vvar[:]), inc=sV)
        P.op("vector",
             lambda e: e.tensor_tensor(vvar[:], vvar[:], m2t[:], ALU.subtract),
             inc=sV)
        P.op("vector", lambda e: e.tensor_copy(dbg2[:], vvar[:]), inc=sV)
        P.op("scalar",
             lambda e: e.activation(sdt[:], vvar[:], ACT.Sqrt,
                                    bias=eps_c[:, 0:1]),
             waits=[(sV, sV.c)], inc=sS)
        P.op("vector", lambda e: e.reciprocal(rstd[:], sdt[:]),
             waits=[(sS, sS.c)], inc=sV)
        ln1_v = sV.c
        for qt in range(4):
            for hh in range(2):
                P.op("vector",
                     lambda e, qt=qt, hh=hh: e.tensor_scalar(
                         qp[2 * qt + hh][:],
                         vout[qt][:, 512 * hh:512 * hh + 512],
                         negm[:, qt:qt + 1],
                         rstd[:, qt:qt + 1], ALU.add, ALU.mult),
                     waits=[(sV, ln1_v)], inc=sV)
                P.op("vector",
                     lambda e, qt=qt, hh=hh: e.tensor_reduce(
                         tmax[:], qp[2 * qt + hh][:], AX, ALU.max,
                         apply_absolute_value=True),
                     inc=sV)
                P.op("vector",
                     lambda e: e.tensor_max(gmax[:], gmax[:], tmax[:]),
                     inc=sV)
                P.dma("gpsimd",
                      stash[CH * j + 128 * qt:CH * j + 128 * qt + 128,
                            512 * hh:512 * hh + 512],
                      qp[2 * qt + hh][:], sG, waits=[(sV, sV.c)])

    p2_v = sV.c
    stash_g = sG.c
    if DBG_P2:
        P.dma("gpsimd", gmr[:], gmax[:], sG, waits=[(sV, sV.c)])
        P.op("vector", lambda e: e.tensor_reduce(gms[:], gmr[:], AX, ALU.max),
             waits=[(sG, sG.c)], inc=sV)
        P.dma("gpsimd", cc2_in, gms[:], sG, waits=[(sV, sV.c)])
        P.op("gpsimd",
             lambda e: e.collective_compute("AllReduce", ALU.max,
                                            replica_groups=[list(range(NCORES))],
                                            ins=[cc2_in], outs=[cc2_out]),
             waits=[(sG, sG.c)], inc=sC)
        P.dma("gpsimd", xsr[:], cc2_out.partition_broadcast(128), sG,
              waits=[(sC, sC.c)])
        P.op("vector",
             lambda e: e.tensor_scalar(xs_c[:], xsr[:], 1.0 / 127.0, 1e-5,
                                       ALU.mult, ALU.max),
             waits=[(sG, sG.c)], inc=sV)
        P.op("vector", lambda e: e.reciprocal(rxs_c[:], xs_c[:]), inc=sV)
        P.op("vector",
             lambda e: e.tensor_scalar(sc2_c[:], wsc_c[:], xs_c[:, 0:1], None,
                                       ALU.mult),
             inc=sV)
        p2_v = sV.c
        stash_g = sG.c

    for j in range(DBG_P3):
        for qt in range(4):
            P.dma("sync", qown[qt][:],
                  stash[CH * j + 128 * qt:CH * j + 128 * qt + 128, :], sK,
                  waits=[(sG, stash_g), (sV, sV.c), (sT, sT.c)])
        uload = sK.c

        for qt in range(4):
            P.op("vector",
                 lambda e, qt=qt: e.tensor_scalar(
                     xf[0][:], qown[qt][:], rxs_c[:, 0:1], C2,
                     ALU.mult, ALU.add),
                 waits=[(sK, uload), (sV, p2_v), (sS, sS.c), (sT, sT.c)],
                 inc=sV)
            P.op("vector",
                 lambda e: e.tensor_scalar(xf[1][:], xf[0][:], -C2, None,
                                           ALU.add),
                 inc=sV)
            for hh in range(2):
                P.op("vector",
                     lambda e, qt=qt, hh=hh: e.tensor_scalar(
                         mask[2 * qt + hh][:],
                         xf[1][:, 512 * hh:512 * hh + 512],
                         127.0, -128.0, ALU.min, ALU.max),
                     inc=sV)
        xq_v = sV.c

        b6b = bank(6).bitcast(BF16)
        evq = {}
        for gi in range(8):
            qt, hh = gi // 2, gi % 2
            pbb = psTball if gi % 2 == 0 else b6b
            src_t = mask[2 * qt + hh]
            w = [(sV, sV.c), (sS, sS.c),
                 (sV, evq.get(gi - 2, 0))]
            for b in range(4):
                P.op("tensor",
                     lambda e, src_t=src_t, b=b, pbb=pbb: e.transpose(
                         pbb[:, 128 * b:128 * b + 128],
                         src_t[:, 128 * b:128 * b + 128], idb[:]),
                     waits=w if b == 0 else (),
                     inc=sT if b == 3 else None)
            P.op("vector",
                 lambda e, qt=qt, hh=hh, pbb=pbb: e.tensor_copy(
                     xqt.rearrange("p (dc q) -> p dc q", dc=8)[
                         :, 4 * hh:4 * hh + 4, 128 * qt:128 * qt + 128],
                     pbb[:, 0:512].rearrange("p (dc q) -> p dc q", dc=4)),
                 waits=[(sT, sT.c)], inc=sV)
            evq[gi] = sV.c
        xqt_v = sV.c

        ffn_evac = {}
        for nt in range(DCH):
            pf = psF[nt % 2]
            w = [(sV, xqt_v)]
            if nt >= 2:
                w = [(sV, ffn_evac[nt - 2])]
            for dc in range(DCH):
                P.op("tensor",
                     lambda e, pf=pf, nt=nt, dc=dc: e.matmul(
                         pf, wt[dc][:, 128 * nt:128 * nt + 128],
                         xqt[:, 512 * dc:512 * dc + 512],
                         start=(dc == 0), stop=(dc == DCH - 1)),
                     waits=w if dc == 0 else (),
                     inc=sT if dc == DCH - 1 else None)
            P.op("vector",
                 lambda e, pf=pf, nt=nt: e.tensor_scalar(
                     qp[nt][:], pf, sc2_c[:, nt:nt + 1], wbs_c[:, nt:nt + 1],
                     ALU.mult, ALU.add),
                 waits=[(sT, sT.c)], inc=sV)
            ffn_evac[nt] = sV.c

        evf = {}
        for gi in range(8):
            qt, g = gi // 2, gi % 2
            pb = psT if gi % 2 == 0 else bank(6)
            w = [(sV, sV.c), (sS, sS.c), (sV, evf.get(gi - 2, 0))]
            for b in range(4):
                nt = 4 * g + b
                P.op("tensor",
                     lambda e, qt=qt, nt=nt, b=b, pb=pb: e.transpose(
                         pb[:, 128 * b:128 * b + 128],
                         qp[nt][:, 128 * qt:128 * qt + 128], iden[:]),
                     waits=w if b == 0 else (),
                     inc=sT if b == 3 else None)
            P.op("vector",
                 lambda e, qt=qt, g=g, pb=pb: e.tensor_tensor(
                     vout[qt][:, 512 * g:512 * g + 512], pb[:, 0:512],
                     qown[qt][:, 512 * g:512 * g + 512], ALU.add),
                 waits=[(sT, sT.c), (sG, sG.c)], inc=sV)
            evf[gi] = sV.c
        s2_v = sV.c

        for qt in range(4):
            P.op("vector",
                 lambda e, qt=qt: e.reduce_sum(srow[:, qt:qt + 1],
                                               vout[qt][:], AX),
                 waits=[(sV, s2_v)], inc=sV)
            P.op("scalar",
                 lambda e, qt=qt: e.activation(xf[1][:], vout[qt][:],
                                               ACT.Square,
                                               accum_out=ssq[:, qt:qt + 1]),
                 waits=[(sV, s2_v)], inc=sS)
        P.op("vector",
             lambda e: e.tensor_scalar(negm[:], srow[:], -1.0 / D, None,
                                       ALU.mult),
             waits=[(sV, sV.c)], inc=sV)
        P.op("scalar", lambda e: e.activation(m2t[:], negm[:], ACT.Square),
             waits=[(sV, sV.c)], inc=sS)
        P.op("vector",
             lambda e: e.tensor_scalar(vvar[:], ssq[:], 1.0 / D, None,
                                       ALU.mult),
             waits=[(sS, sS.c)], inc=sV)
        P.op("vector",
             lambda e: e.tensor_tensor(vvar[:], vvar[:], m2t[:], ALU.subtract),
             inc=sV)
        P.op("scalar",
             lambda e: e.activation(sdt[:], vvar[:], ACT.Sqrt,
                                    bias=eps_c[:, 0:1]),
             waits=[(sV, sV.c)], inc=sS)
        P.op("vector", lambda e: e.reciprocal(rstd[:], sdt[:]),
             waits=[(sS, sS.c)], inc=sV)
        ln2_v = sV.c
        for qt in range(4):
            P.op("vector",
                 lambda e, qt=qt: e.tensor_scalar(
                     zacc[qt][:], vout[qt][:], negm[:, qt:qt + 1],
                     rstd[:, qt:qt + 1], ALU.add, ALU.mult),
                 waits=[(sV, ln2_v), (sG, sG.c)], inc=sV)
            P.dma("gpsimd", out_d[CH * j + 128 * qt:CH * j + 128 * qt + 128, :],
                  zacc[qt][:], sG, waits=[(sV, sV.c)])

    if DBG_TAP:
        fw = [(sT, sT.c), (sV, sV.c), (sS, sS.c), (sK, sK.c)]
        def tap(row, ap, cols=D):
            P.dma("gpsimd", out_d[row:row + ap.shape[0], 0:cols], ap, sG,
                  waits=fw, slow=True)
        if DBG_TAP == "msb":
            for i in range(DCH):
                tap(128 * i, msb[i][:])
        elif DBG_TAP == "qp":
            for i in range(DCH):
                tap(128 * i, qp[i][:], 512)
        elif DBG_TAP == "xfq":
            for i in range(DCH):
                tap(128 * i, xfq[i][:], 512)
        elif DBG_TAP == "zacc":
            for i in range(4):
                tap(128 * i, zacc[i][:])
        elif DBG_TAP == "qown":
            for i in range(4):
                tap(128 * i, qown[i][:])
        elif DBG_TAP == "pt":
            for i in range(8):
                tap(128 * i, pt[i][:], 512)
        elif DBG_TAP == "wt":
            for i in range(DCH):
                P.dma("gpsimd", out_d[128 * i:128 * i + 128, :], wt[i][:],
                      sG, waits=fw)
        elif DBG_TAP == "ln1":
            for i in range(4):
                tap(128 * i, zacc[i][:])
            tap(3000, srow[:], 4)
            tap(3200, ssq[:], 4)
            tap(3400, negm[:], 4)
            tap(3600, rstd[:], 4)
            tap(3800, l_sb[:], 4)
            tap(2000, vvar[:], 4)
            tap(2200, sdt[:], 4)
            tap(2400, m2t[:], 4)
            tap(2600, dbg1[:], 4)
            tap(2800, dbg2[:], 4)
        elif DBG_TAP == "scalars":
            tap(0, kn2[:], 1)
            tap(1, nbrow[:], 512)
            tap(128, l_sb[:], 4)
            tap(256, rl_sb[:], 4)
            tap(384, rstd[:], 4)
            tap(512, negm[:], 4)
            tap(640, xs_c[:], 1)
            tap(768, srow[:], 4)
            tap(896, ssq[:], 4)

    P.op("gpsimd", None, waits=[(sG, sG.c)])

    with nc.Block() as block:
        P.realize(block)
    stack.close()
    return nc


_NC_CACHE = None


def _get_nc():
    global _NC_CACHE
    if _NC_CACHE is None:
        _NC_CACHE = build_nc()
    return _NC_CACHE


def unit(c, j):
    p = (c + j) % 2
    return ((c // 2 + j) % 4, 2 * j + p)


def make_in_maps(x, wq, wk, wv, wi8, wsc, wbs):
    wsc_c = np.ascontiguousarray(wsc.reshape(DCH, 128).T)
    wbs_c = np.ascontiguousarray(wbs.reshape(DCH, 128).T)
    iden = np.eye(128, dtype=np.float32)
    idb = np.eye(128, dtype=ml_dtypes.bfloat16)
    in_maps = []
    for c in range(NCORES):
        im = {"wq": wq, "wk": wk, "wv": wv, "wi8": wi8,
              "wsc": wsc_c, "wbs": wbs_c, "iden": iden, "idb": idb}
        for j in range(NSLOT):
            b, ch = unit(c, j)
            im[f"q{j}"] = x[b, CH * ch:CH * ch + CH, :]
            keys = np.arange(1024 * j, 1024 * (j + 1))
            qs = CH * ch + np.arange(CH)
            im[f"m{j}"] = (keys[:, None] <= qs[None, :]).astype(
                ml_dtypes.bfloat16)
            if j >= 4:
                im[f"k{j}"] = x[b, :1024 * (j + 1), :]
        in_maps.append(im)
    return in_maps


def kernel(L0_attn_input_x, L0_attn_Wq, L0_attn_Wk, L0_attn_Wv,
           L0_ffn_weight_int8, L0_ffn_weight_scale, L0_ffn_bias):
    x = np.ascontiguousarray(np.asarray(L0_attn_input_x, dtype=np.float32))
    wq = np.ascontiguousarray(np.asarray(L0_attn_Wq, dtype=np.float32))
    wk = np.ascontiguousarray(np.asarray(L0_attn_Wk, dtype=np.float32))
    wv = np.ascontiguousarray(np.asarray(L0_attn_Wv, dtype=np.float32))
    wi8 = np.ascontiguousarray(np.asarray(L0_ffn_weight_int8, dtype=np.int8))
    wsc = np.asarray(L0_ffn_weight_scale, dtype=np.float32).reshape(D)
    wbs = np.asarray(L0_ffn_bias, dtype=np.float32).reshape(D)

    in_maps = make_in_maps(x, wq, wk, wv, wi8, wsc, wbs)
    nc = _get_nc()
    res = run_bass_kernel_spmd(nc, in_maps, core_ids=list(range(NCORES)))

    full = np.empty((B_, T_, D), dtype=np.float32)
    for c in range(NCORES):
        o = np.asarray(res.results[c]["out"])
        for j in range(NSLOT):
            b, ch = unit(c, j)
            full[b, CH * ch:CH * ch + CH, :] = o[CH * j:CH * j + CH, :]
    return full


if __name__ == "__main__":
    nc = build_nc()
    print("built ok; instructions:", nc.n_instructions())
